# version 04
# baseline (speedup 1.0000x reference)
import numpy as np
import concourse.bass as bass
import concourse.mybir as mybir
from concourse.bass import IndirectOffsetOnAxis
from concourse.bass_utils import run_bass_kernel_spmd

F32 = mybir.dt.float32
BF16 = mybir.dt.bfloat16
F16 = mybir.dt.float16
U32 = mybir.dt.uint32
AF = mybir.ActivationFunctionType
ALU = mybir.AluOpType
AX = mybir.AxisListType

D = 1024
KD = D // 128
HD = 64
NE = 8
EPS = 1e-5
BIGNEG = -1.0e30


def build_program(S=4096, F=4096, C=1152, n_cores=8, debug_taps=False):
    NT = S // 128
    NTN = S // 512
    FC = F // 128
    MC = C // 128
    NXC = [512] * (C // 512) + ([C % 512] if C % 512 else [])
    NNX = len(NXC)
    W2KG = min(4, FC)
    NW2B = (FC + W2KG - 1) // W2KG
    CAPACITY = S * n_cores // NE
    assert S % 512 == 0 and C % 128 == 0 and F % (128 * W2KG) == 0

    def mgroups():
        gs = [list(range(0, min(8, MC)))]
        if MC > 8:
            gs.append(list(range(8, MC)))
        return gs

    nc = bass.Bass("TRN2", target_bir_lowering=False, debug=False,
                   detect_race_conditions=False)

    x_d = nc.dram_tensor("x", [S, D], F32, kind="ExternalInput").ap()
    wq_d = nc.dram_tensor("wqh", [128, KD, HD], F32, kind="ExternalInput").ap()
    wk_d = nc.dram_tensor("wkh", [128, KD, HD], F32, kind="ExternalInput").ap()
    wvr_d = nc.dram_tensor("wvrh", [128, KD, 72], F32, kind="ExternalInput").ap()
    w1_d = nc.dram_tensor("w1h", [NE, FC, 128, KD, 128], F32, kind="ExternalInput").ap()
    w2_d = nc.dram_tensor("w2h", [NE, FC, 2, 128, 512], F32, kind="ExternalInput").ap()
    bias_d = nc.dram_tensor("bias1", [1, D], F32, kind="ExternalInput").ap()
    identf_d = nc.dram_tensor("identf", [128, 128], F32, kind="ExternalInput").ap()
    lincl_d = nc.dram_tensor("lincl", [128, 128], F32, kind="ExternalInput").ap()
    sel127_d = nc.dram_tensor("sel127", [128, 128], F32, kind="ExternalInput").ap()
    onescol_d = nc.dram_tensor("onescol", [128, 1], F32, kind="ExternalInput").ap()
    prefm_d = nc.dram_tensor("prefm", [NE, 1], F32, kind="ExternalInput").ap()
    out_d = nc.dram_tensor("out", [S, D], F32, kind="ExternalOutput").ap()

    xn_d = nc.dram_tensor("xn_dram", [S, D], BF16).ap()
    ZROWS = NE * C + 128
    xe_d = nc.dram_tensor("xe_dram", [ZROWS, D], BF16).ap()
    ye_d = nc.dram_tensor("ye_dram", [NE * C, D], BF16).ap()
    ccin_d = nc.dram_tensor("cc_in", [1, 16], F32).ap()
    ccout_d = nc.dram_tensor("cc_out", [n_cores, 16], F32).ap()
    ccbase_d = nc.dram_tensor("cc_base", [1, 16], F32).ap()
    if debug_taps:
        NTt = S // 128
        dbg = {
            "dbg_s": nc.dram_tensor("dbg_s", [128, NTt * HD], F32, kind="ExternalOutput").ap(),
            "dbg_logit": nc.dram_tensor("dbg_logit", [128, NTt * NE], F32, kind="ExternalOutput").ap(),
            "dbg_wco": nc.dram_tensor("dbg_wco", [128, NTt * 2], F32, kind="ExternalOutput").ap(),
            "dbg_eidx": nc.dram_tensor("dbg_eidx", [128, NTt * 2], F32, kind="ExternalOutput").ap(),
            "dbg_rk": nc.dram_tensor("dbg_rk", [128, NTt * 2], F32, kind="ExternalOutput").ap(),
            "dbg_rsl": nc.dram_tensor("dbg_rsl", [128, NTt * 2], U32, kind="ExternalOutput").ap(),
            "dbg_sct": nc.dram_tensor("dbg_sct", [128, NTt * 2], U32, kind="ExternalOutput").ap(),
            "dbg_base": nc.dram_tensor("dbg_base", [128, 16], F32, kind="ExternalOutput").ap(),
            "dbg_rank": nc.dram_tensor("dbg_rank", [128, NTt * 16], F32, kind="ExternalOutput").ap(),
            "dbg_xnd": nc.dram_tensor("dbg_xnd", [S, D], BF16, kind="ExternalOutput").ap(),
            "dbg_xnfm": nc.dram_tensor("dbg_xnfm", [128, KD * S], BF16, kind="ExternalOutput").ap(),
            "dbg_qT": nc.dram_tensor("dbg_qT", [64, S], BF16, kind="ExternalOutput").ap(),
            "dbg_kT": nc.dram_tensor("dbg_kT", [64, S], BF16, kind="ExternalOutput").ap(),
        }

    def sbuf(name, shape, dtype):
        return nc.alloc_sbuf_tensor(name, shape, dtype)

    def nbytes(shape, dtype):
        return shape[1] * mybir.dt.size(dtype)

    h_sb = sbuf("h_sb", [128, FC * C], BF16)
    h_addr = nc.lookup_mloc(h_sb).addr
    if KD * S <= FC * C:
        xnfm = nc.alloc_sbuf_tensor_at("xn_fm", [128, KD * S], BF16, offset=h_addr)
    else:
        xnfm = sbuf("xn_fm", [128, KD * S], BF16)
    if KD * S * 2 + S * 2 <= FC * C * 2:
        qT = nc.alloc_sbuf_tensor_at("qT", [64, S], BF16,
                                     offset=h_addr + KD * S * 2)
    else:
        qT = sbuf("qT", [64, S], BF16)
    xe_sb = [sbuf(f"xe{i}", [128, KD * C], BF16) for i in range(2)]
    if S * 2 <= KD * C * 2:
        kT = nc.alloc_sbuf_tensor_at("kT", [64, S], BF16,
                                     offset=nc.lookup_mloc(xe_sb[1]).addr)
    else:
        kT = sbuf("kT", [64, S], BF16)

    R1_P1 = 3 * 4096 + 2 * 2048 + 4096
    R1_P4 = 64 * 0 + 4 * S + 2 * S + 2 * 2 * S
    R1_P6 = 4 * 2048 + 2 * 4096 + 2 * 4096 + 4096
    R1 = max(R1_P1, R1_P4, R1_P6)
    r1_sb = sbuf("r1_arena", [128, R1], mybir.dt.uint8)
    r1_addr = nc.lookup_mloc(r1_sb).addr

    def at(name, shape, dtype, off):
        assert off + nbytes(shape, dtype) <= R1, (name, off)
        return nc.alloc_sbuf_tensor_at(name, shape, dtype, offset=r1_addr + off)

    xt_sb = [at(f"xt{i}", [128, D], F32, i * 4096) for i in range(3)]
    xnt_sb = [at(f"xnt{i}", [128, D], BF16, 3 * 4096 + i * 2048) for i in range(2)]
    scr_sb = at("scr_sb", [128, D], F32, 3 * 4096 + 2 * 2048)
    avacc = at("avacc", [64, S], F32, 0)
    att16 = at("att16", [128, S], F16, 4 * S)
    e_sb = [at(f"esb{i}", [128, S], BF16, 4 * S + 2 * S + i * 2 * S)
            for i in range(2)]
    yg_sb = [at(f"yg{i}", [128, D], BF16, i * 2048) for i in range(4)]
    xc_sb = [at(f"xc{i}", [128, D], F32, 4 * 2048 + i * 4096) for i in range(2)]
    o_sb = [at(f"osb{i}", [128, D], F32, 4 * 2048 + 2 * 4096 + i * 4096)
            for i in range(2)]
    g_sb = at("g_sb", [128, D], F32, 4 * 2048 + 4 * 4096)

    v_sb = sbuf("v_sb", [128, NT * HD], BF16)
    s_sb = sbuf("s_sb", [128, NT * HD], F32)
    logit_sb = sbuf("logit_sb", [128, NT * NE], F32)
    rank_sb = sbuf("rank_sb", [128, NT * 16], F32)
    rk_sb = sbuf("rk_sb", [128, NT * 2], F32)
    eidx_sb = sbuf("eidx_sb", [128, NT * 2], F32)
    wco_sb = sbuf("wco_sb", [128, NT * 2], F32)
    sct_sb = sbuf("sct_sb", [128, NT * 2], U32)
    rsl_sb = sbuf("rsl_sb", [128, NT * 2], U32)
    ind_sb = [sbuf(f"ind{i}", [128, 16], F32) for i in range(2)]
    off_sb = sbuf("off_sb", [128, 16], F32)
    cnt_bc = sbuf("cnt_bc", [128, 16], F32)
    base_bc = sbuf("base_bc", [128, 16], F32)
    ccall = sbuf("ccall", [NE, 16], F32)
    prefm = sbuf("prefm_sb", [NE, 1], F32)
    xw_sb = [sbuf(f"xw{i}", [128, D], BF16) for i in range(4)]
    w1b_sb = [sbuf(f"w1b{i}", [128, KD * 128], BF16) for i in range(2)]
    w2b_sb = [sbuf(f"w2b{i}", [128, W2KG * 512], BF16) for i in range(2)]
    wq_sb = sbuf("wq_sb", [128, KD * HD], BF16)
    wk_sb = sbuf("wk_sb", [128, KD * HD], BF16)
    wvr_sb = sbuf("wvr_sb", [128, KD * 72], BF16)
    ye_sb = [sbuf(f"yesb{i}", [128, 512], BF16) for i in range(4)]
    bias_bc = sbuf("bias_bc", [128, D], F32)
    zt_sb = sbuf("zt_sb", [128, D], BF16)
    identf = sbuf("identf_sb", [128, 128], F32)
    lincl = sbuf("lincl_sb", [128, 128], F32)
    sel127 = sbuf("sel127_sb", [128, 128], F32)
    onescol = sbuf("onescol_sb", [128, 1], F32)
    iota8 = sbuf("iota8", [128, NE], F32)
    iotap = sbuf("iotap", [128, 1], F32)
    bmi = sbuf("bmi", [128, NE], F32)
    st1 = sbuf("st1", [128, NE], F32)
    st2 = sbuf("st2", [128, NE], F32)
    st3 = sbuf("st3", [128, NE], F32)
    st4 = sbuf("st4", [128, NE], F32)
    cmax = sbuf("cmax", [128, 16], F32)
    statm = sbuf("statm", [128, 2], F32)
    statv = sbuf("statv", [128, 2], F32)
    statq = sbuf("statq", [128, 2], F32)
    statr = sbuf("statr", [128, 2], F32)
    statnb = sbuf("statnb", [128, 2], F32)
    colA = sbuf("colA", [128, 1], F32)
    colB = sbuf("colB", [128, 1], F32)
    colC = sbuf("colC", [128, 1], F32)
    colD = sbuf("colD", [128, 1], F32)
    colE = sbuf("colE", [128, 1], F32)
    colF = sbuf("colF", [128, 1], F32)
    colG = sbuf("colG", [128, 1], F32)
    colH = sbuf("colH", [128, 1], F32)
    c0col = sbuf("c0col", [128, 1], F32)
    c1col = sbuf("c1col", [128, 1], F32)
    negmax = sbuf("negmax", [128, 1], F32)
    den = sbuf("den", [128, 1], F32)
    vrec = sbuf("vrec", [128, 1], F32)
    vn_sb = sbuf("vn_sb", [128, HD], BF16)
    base_row = sbuf("base_row", [1, 16], F32)

    from contextlib import ExitStack
    es = ExitStack()
    bank = [es.enter_context(nc.psum_tensor(f"bank{i}", [128, 512], F32))
            for i in range(8)]

    def sem(name):
        return es.enter_context(nc.semaphore(name))

    s_xb = [sem(f"s_xb{i}") for i in range(3)]
    s_xnwb = [sem(f"s_xnwb{i}") for i in range(2)]
    s_xwb = [sem(f"s_xwb{i}") for i in range(4)]
    s_scatb = [sem(f"s_scatb{i}") for i in range(4)]
    s_xeb = [sem(f"s_xeb{i}") for i in range(2)]
    s_w1b = [sem(f"s_w1b{i}") for i in range(2)]
    s_w2b = [sem(f"s_w2b{i}") for i in range(2)]
    s_yewb = [sem(f"s_yewb{i}") for i in range(4)]
    s_ygtb = [sem(f"s_ygtb{i}") for i in range(4)]
    s_xcb = [sem(f"s_xcb{i}") for i in range(2)]
    s_owb = [sem(f"s_owb{i}") for i in range(2)]
    s_stat = sem("s_stat"); s_xn = sem("s_xn")
    s_xnfm = sem("s_xnfm"); s_wc = sem("s_wc")
    s_zf = sem("s_zf"); s_zt = sem("s_zt"); s_cl = sem("s_cl")
    s_ind = sem("s_ind"); s_off = sem("s_off"); s_ccw = sem("s_ccw")
    s_cc = sem("s_cc"); s_cntb = sem("s_cntb"); s_slot = sem("s_slot")
    s_sm = sem("s_sm")
    s_exp = sem("s_exp"); s_vn = sem("s_vn"); s_l2g = sem("s_l2g")
    s_w2c = sem("s_w2c")
    s_gsil = sem("s_gsil"); s_out = sem("s_out")
    s_ccb = sem("s_ccb"); s_basb = sem("s_basb"); s_sv = sem("s_sv")
    s_vs = sem("s_vs")
    s_comb = sem("s_comb"); s_sq = sem("s_sq"); s_rst = sem("s_rst")
    s_qkd = sem("s_qkd")
    sgd = [sem(f"sgd{i}") for i in range(8)]
    sbf = [sem(f"sbf{i}") for i in range(8)]

    NZF = (ZROWS + 127) // 128

    rec = {"qk": [], "vr": [], "rank": [], "attqk": [], "attav": [],
           "stp": [], "l1": {}, "l2": [], "base": []}
    ngrp = [0] * 8
    sgdcnt = [0] * 8

    def grp_begin(te, b):
        if ngrp[b] > 0:
            te.wait_ge(sbf[b], ngrp[b])

    def grp_end(stream, b):
        ngrp[b] += 1
        sgdcnt[b] += 1
        rec[stream].append((b, sgdcnt[b]))

    with nc.Block() as block:
        @block.tensor
        def _(te):
            te.wait_ge(s_xnfm, 16 * KD)
            te.wait_ge(s_wc, 16 * 3)
            for wsb in (wq_sb, wk_sb):
                for n in range(NTN):
                    b = n % 2
                    grp_begin(te, b)
                    for k in range(KD):
                        m = te.matmul(bank[b][0:HD, 0:512],
                                      wsb[:, k * HD:(k + 1) * HD],
                                      xnfm[:, k * S + n * 512: k * S + (n + 1) * 512],
                                      start=(k == 0), stop=(k == KD - 1))
                    m.then_inc(sgd[b], 1)
                    grp_end("qk", b)
            for i in range(NT):
                b = 2 + (i % 2)
                grp_begin(te, b)
                for k in range(KD):
                    m = te.matmul(bank[b][0:128, 0:72],
                                  xnfm[:, k * S + i * 128: k * S + (i + 1) * 128],
                                  wvr_sb[:, k * 72:(k + 1) * 72],
                                  start=(k == 0), stop=(k == KD - 1))
                m.then_inc(sgd[b], 1)
                grp_end("vr", b)
            te.wait_ge(s_cl, 16 * 5)
            for i in range(NT):
                b = 4 + (i % 2)
                b2 = 6 + (i % 2)
                grp_begin(te, b)
                te.wait_ge(s_ind, i + 1)
                te.wait_ge(s_off, i)
                te.matmul(bank[b][:, 0:16], lincl[:], ind_sb[i % 2][:],
                          start=True, stop=False)
                m = te.matmul(bank[b][:, 0:16], sel127[:], off_sb[:],
                              start=False, stop=True)
                m.then_inc(sgd[b], 1)
                m2 = te.matmul(bank[b2][0:1, 0:16], onescol[:], ind_sb[i % 2][:],
                               start=True, stop=True)
                m2.then_inc(sgd[b2], 1)
                sgdcnt[b2] += 1
                ngrp[b] += 1
                sgdcnt[b] += 1
                rec["rank"].append(((b, sgdcnt[b]), (b2, sgdcnt[b2])))
            for j in range(NT):
                for ic in range(NTN):
                    b = ic % 2
                    grp_begin(te, b)
                    m = te.matmul(bank[b][:, :],
                                  kT[0:HD, j * 128:(j + 1) * 128],
                                  qT[0:HD, ic * 512:(ic + 1) * 512],
                                  start=True, stop=True)
                    m.then_inc(sgd[b], 1)
                    grp_end("attqk", b)
                te.wait_ge(s_vn, j + 1)
                for ic in range(NTN):
                    b = 2 + ic % 2
                    grp_begin(te, b)
                    m = te.matmul(bank[b][0:HD, :],
                                  vn_sb[:],
                                  e_sb[j % 2][:, ic * 512:(ic + 1) * 512],
                                  start=True, stop=True)
                    m.then_inc(sgd[b], 1)
                    grp_end("attav", b)
            te.wait_ge(sbf[2], ngrp[2])
            te.wait_ge(sbf[3], ngrp[3])
            for i in range(NT):
                b = 4 + (i % 2)
                grp_begin(te, b)
                m = te.transpose(bank[b][0:128, 0:HD],
                                 avacc[0:HD, i * 128:(i + 1) * 128],
                                 identf[0:HD, 0:HD])
                m.then_inc(sgd[b], 1)
                grp_end("stp", b)
            w1_used = 0
            w2_used = 0
            gel_total = 0
            npass = 0
            nw2c = 0
            rec["w2done"] = []
            for e in range(NE):
                te.wait_ge(s_xeb[e % 2], 16 * KD * (e // 2 + 1))
                for m_ in range(FC):
                    w1_used += 1
                    te.wait_ge(s_w1b[(w1_used - 1) % 2],
                               16 * ((w1_used - 1) // 2 + 1))
                    bs = (m_ % 2) * NNX
                    for ni in range(NNX):
                        grp_begin(te, bs + ni)
                    for k in range(KD):
                        off = 0
                        for ni, nsz in enumerate(NXC):
                            b = bs + ni
                            mm = te.matmul(
                                bank[b][:, 0:nsz],
                                w1b_sb[(w1_used - 1) % 2][:, k * 128:(k + 1) * 128],
                                xe_sb[e % 2][:, k * C + off: k * C + off + nsz],
                                start=(k == 0), stop=(k == KD - 1))
                            if k == KD - 1:
                                mm.then_inc(sgd[b], 1)
                            off += nsz
                    for ni in range(NNX):
                        sgdcnt[bs + ni] += 1
                        rec["l1"][(e, m_, ni)] = (bs + ni, sgdcnt[bs + ni],
                                                  ngrp[bs + ni] + 1)
                        ngrp[bs + ni] += 1
                first_pass = True
                for half in range(2):
                    for grp in mgroups():
                        for mi in range(len(grp)):
                            grp_begin(te, mi % 8)
                        for k2 in range(FC):
                            if first_pass:
                                for ni in range(NNX):
                                    hb, _, hn = rec["l1"][(e, k2, ni)]
                                    te.wait_ge(sbf[hb], hn)
                            if k2 % W2KG == 0:
                                w2_used += 1
                                te.wait_ge(s_w2b[(w2_used - 1) % 2],
                                           16 * ((w2_used - 1) // 2 + 1))
                            ws = w2b_sb[(w2_used - 1) % 2][:, (k2 % W2KG) * 512:
                                                           (k2 % W2KG + 1) * 512]
                            blk_end = (k2 % W2KG == W2KG - 1 or k2 == FC - 1)
                            for mi, m_ in enumerate(grp):
                                b = mi % 8
                                mm = te.matmul(
                                    bank[b][:, :],
                                    h_sb[:, k2 * C + m_ * 128: k2 * C + (m_ + 1) * 128],
                                    ws,
                                    start=(k2 == 0), stop=(k2 == FC - 1))
                                last_mi = (mi == len(grp) - 1)
                                if k2 == FC - 1 and last_mi:
                                    mm.then_inc(s_l2g, 1)
                                elif blk_end and last_mi:
                                    mm.then_inc(s_w2c, 1)
                            if blk_end:
                                if k2 == FC - 1:
                                    rec["w2done"].append(("g", npass + 1))
                                else:
                                    nw2c += 1
                                    rec["w2done"].append(("c", nw2c))
                        npass += 1
                        for mi in range(len(grp)):
                            b = mi % 8
                            ngrp[b] += 1
                            rec["l2"].append((b, npass, ngrp[b]))
                        first_pass = False
            te.wait_ge(s_cc, 1)
            te.wait_ge(s_ccb, 16 * 2)
            te.wait_ge(s_off, NT)
            b = 6
            m = te.matmul(bank[b][0:1, 0:16], prefm[:], ccall[:],
                          start=True, stop=True)
            m.then_inc(sgd[b], 1)
            sgdcnt[6] += 1
            rec["base_g"] = sgdcnt[6]

        @block.vector
        def _(ve):
            ve.memset(off_sb[:], 0.0)
            ve.memset(zt_sb[:], 0).then_inc(s_zt, 1)
            for i in range(NT + 1):
                if i < NT:
                    ve.wait_ge(s_xb[i % 3], 16 * (i // 3 + 1))
                    if i >= 2:
                        ve.wait_ge(s_xn, i - 1)
                    xt = xt_sb[i % 3]
                    pc = i % 2
                    ve.reduce_sum(statm[:, pc:pc + 1], xt[:], axis=AX.X)
                    ve.scalar_tensor_tensor(scr_sb[:], xt[:], 1.0, xt[:],
                                            ALU.mult, ALU.mult,
                                            accum_out=statv[:, pc:pc + 1])
                    ve.drain()
                    ve.tensor_scalar_mul(statm[:, pc:pc + 1], statm[:, pc:pc + 1],
                                         1.0 / D)
                    ve.tensor_scalar_mul(statv[:, pc:pc + 1], statv[:, pc:pc + 1],
                                         1.0 / D)
                    ve.drain()
                    ve.tensor_tensor(colH[:], statm[:, pc:pc + 1],
                                     statm[:, pc:pc + 1], ALU.mult)
                    ve.drain()
                    ve.tensor_sub(statv[:, pc:pc + 1], statv[:, pc:pc + 1], colH[:])
                    ve.drain()
                    ve.tensor_scalar_add(statv[:, pc:pc + 1], statv[:, pc:pc + 1],
                                         EPS).then_inc(s_stat, 1)
                if i >= 1:
                    pr = (i - 1) % 2
                    ve.wait_ge(s_sq, i)
                    ve.reciprocal(statr[:, pr:pr + 1], statq[:, pr:pr + 1])
                    ve.drain()
                    ve.tensor_tensor(statnb[:, pr:pr + 1], statm[:, pr:pr + 1],
                                     statr[:, pr:pr + 1], ALU.mult)
                    ve.drain()
                    ve.tensor_scalar_mul(statnb[:, pr:pr + 1],
                                         statnb[:, pr:pr + 1], -1.0) \
                        .then_inc(s_rst, 1)
            qki = 0
            for dst in (qT, kT):
                for n in range(NTN):
                    b, g = rec["qk"][qki]; qki += 1
                    ve.wait_ge(sgd[b], g)
                    ve.tensor_copy(dst[0:HD, n * 512:(n + 1) * 512],
                                   bank[b][0:HD, 0:512]).then_inc(sbf[b], 1)
            ve.sem_inc(s_qkd, 1)
            for i in range(NT):
                b, g = rec["vr"][i]
                ve.wait_ge(sgd[b], g)
                ve.tensor_copy(v_sb[:, i * HD:(i + 1) * HD], bank[b][0:128, 0:HD])
                ve.tensor_copy(logit_sb[:, i * NE:(i + 1) * NE],
                               bank[b][0:128, HD:72]).then_inc(sbf[b], 1)
            for i in range(NT):
                lg = logit_sb[:, i * NE:(i + 1) * NE]
                ve.reduce_max(negmax[:], lg, axis=AX.X, negate=True) \
                    .then_inc(s_vs, 1)
                ve.wait_ge(s_sv, i + 1)
                ve.reciprocal(vrec[:], den[:])
                ve.drain()
                ve.tensor_scalar_mul(st1[:], st2[:], vrec[:])
                ve.drain()
                ve.reduce_max(colA[:], st1[:], axis=AX.X)
                ve.drain()
                ve.tensor_scalar(st3[:], st1[:], colA[:], None, ALU.is_ge)
                ve.drain()
                ve.tensor_tensor(st4[:], st3[:], bmi[:], ALU.mult)
                ve.drain()
                ve.reduce_max(colB[:], st4[:], axis=AX.X)
                ve.drain()
                ve.tensor_scalar(colB[:], colB[:], -1.0, 128.0, ALU.mult, ALU.add)
                ve.drain()
                ve.tensor_scalar(st3[:], iota8[:], colB[:], None, ALU.is_equal)
                ve.drain()
                ve.scalar_tensor_tensor(st4[:], st3[:], BIGNEG, st1[:],
                                        ALU.mult, ALU.add)
                ve.drain()
                ve.reduce_max(colC[:], st4[:], axis=AX.X)
                ve.drain()
                ve.tensor_scalar(st4[:], st4[:], colC[:], None, ALU.is_ge)
                ve.drain()
                ve.tensor_tensor(st4[:], st4[:], bmi[:], ALU.mult)
                ve.drain()
                ve.reduce_max(colD[:], st4[:], axis=AX.X)
                ve.drain()
                ve.tensor_scalar(colD[:], colD[:], -1.0, 128.0, ALU.mult, ALU.add)
                ve.drain()
                ve.tensor_scalar(st4[:], iota8[:], colD[:], None, ALU.is_equal)
                ve.drain()
                ve.tensor_copy(eidx_sb[:, 2 * i:2 * i + 1], colB[:])
                ve.drain()
                ve.tensor_copy(eidx_sb[:, 2 * i + 1:2 * i + 2], colD[:])
                ve.drain()
                ve.tensor_copy(wco_sb[:, 2 * i:2 * i + 1], colA[:])
                ve.drain()
                ve.tensor_copy(wco_sb[:, 2 * i + 1:2 * i + 2], colC[:])
                ve.drain()
                ve.tensor_copy(ind_sb[i % 2][:, 0:NE], st3[:])
                ve.drain()
                ve.tensor_copy(ind_sb[i % 2][:, NE:16], st4[:]).then_inc(s_ind, 1)
                (b, g), (b2, g2) = rec["rank"][i]
                ve.wait_ge(sgd[b], g)
                ve.tensor_sub(rank_sb[:, i * 16:(i + 1) * 16], bank[b][:, 0:16],
                              ind_sb[i % 2][:]).then_inc(sbf[b], 1)
                ve.wait_ge(sgd[b2], g2)
                ve.drain()
                ve.tensor_add(off_sb[0:1, :], off_sb[0:1, :],
                              bank[b2][0:1, 0:16]).then_inc(s_off, 1)
            ve.wait_ge(s_cntb, 16)
            for i in range(NT):
                for kk in range(2):
                    ecol = eidx_sb[:, 2 * i + kk:2 * i + kk + 1]
                    ve.tensor_scalar(st1[:], iota8[:], ecol, None, ALU.is_equal)
                    ve.drain()
                    ve.tensor_tensor(st2[:], st1[:],
                                     rank_sb[:, i * 16 + kk * NE: i * 16 + (kk + 1) * NE],
                                     ALU.mult)
                    ve.drain()
                    ve.reduce_sum(colA[:], st2[:], axis=AX.X)
                    ve.drain()
                    if kk == 1:
                        ve.tensor_tensor(st2[:], st1[:], cnt_bc[:, 0:NE], ALU.mult)
                        ve.drain()
                        ve.reduce_sum(colB[:], st2[:], axis=AX.X)
                        ve.drain()
                        ve.tensor_add(colB[:], colA[:], colB[:])
                        ve.drain()
                    else:
                        ve.tensor_copy(colB[:], colA[:])
                        ve.drain()
                    ve.tensor_scalar(colC[:], colB[:], float(C), None, ALU.is_ge)
                    ve.drain()
                    ve.scalar_tensor_tensor(colD[:], colC[:], 1.0e9, colA[:],
                                            ALU.mult, ALU.add)
                    ve.drain()
                    ve.tensor_copy(rk_sb[:, 2 * i + kk:2 * i + kk + 1], colD[:])
                    ve.scalar_tensor_tensor(colD[:], ecol, float(C), colB[:],
                                            ALU.mult, ALU.add)
                    ve.tensor_scalar_add(colE[:], iotap[:], float(NE * C))
                    ve.drain()
                    ve.tensor_sub(colE[:], colE[:], colD[:])
                    ve.drain()
                    ve.scalar_tensor_tensor(colF[:], colE[:], colC[:], colD[:],
                                            ALU.mult, ALU.add)
                    ve.drain()
                    ve.tensor_copy(sct_sb[:, 2 * i + kk:2 * i + kk + 1], colF[:])
                    ve.tensor_scalar_min(colB[:], colB[:], float(C - 1))
                    ve.drain()
                    ve.scalar_tensor_tensor(colD[:], ecol, float(C), colB[:],
                                            ALU.mult, ALU.add)
                    ve.drain()
                    mcp = ve.tensor_copy(rsl_sb[:, 2 * i + kk:2 * i + kk + 1], colD[:])
                    if kk == 1:
                        mcp.then_inc(s_slot, 1)
            ve.memset(avacc[:], 0.0)
            for j in range(NT):
                if j >= 1:
                    ve.wait_ge(s_exp, j)
                for ic in range(NTN):
                    b, g = rec["attqk"][j * NTN + ic]
                    ve.wait_ge(sgd[b], g)
                    ve.tensor_scalar(att16[:, ic * 512:(ic + 1) * 512],
                                     bank[b][:, :], 0.0, None, ALU.add, ALU.max,
                                     accum_out=cmax[:, ic:ic + 1]).then_inc(sbf[b], 1)
                ve.drain()
                ve.reduce_max(negmax[:], cmax[:, 0:NTN], axis=AX.X, negate=True) \
                    .then_inc(s_sm, 1)
                ve.wait_ge(s_exp, j + 1)
                ve.reciprocal(vrec[:], den[:])
                ve.drain()
                if j >= 1:
                    ab, ag = rec["attav"][(j - 1) * NTN + NTN - 1]
                    ve.wait_ge(sgd[ab], ag)
                ve.tensor_scalar_mul(vn_sb[:], v_sb[:, j * HD:(j + 1) * HD], vrec[:]) \
                    .then_inc(s_vn, 1)
                for ic in range(NTN):
                    b, g = rec["attav"][j * NTN + ic]
                    ve.wait_ge(sgd[b], g)
                    ve.tensor_add(avacc[0:HD, ic * 512:(ic + 1) * 512],
                                  avacc[0:HD, ic * 512:(ic + 1) * 512],
                                  bank[b][0:HD, 0:512]).then_inc(sbf[b], 1)
            for i in range(NT):
                b, g = rec["stp"][i]
                ve.wait_ge(sgd[b], g)
                ve.tensor_copy(s_sb[:, i * HD:(i + 1) * HD],
                               bank[b][0:128, 0:HD]).then_inc(sbf[b], 1)
            yn = 0
            for b, npass_, gcount in rec["l2"]:
                yn += 1
                if yn > 4:
                    ve.wait_ge(s_yewb[(yn - 4 - 1) % 4], 16 * ((yn - 4 - 1) // 4 + 1))
                ve.wait_ge(s_l2g, npass_)
                ve.tensor_copy(ye_sb[yn % 4][:, :], bank[b][:, :]).then_inc(sbf[b], 1)
            ve.wait_ge(sgd[6], rec["base_g"])
            ve.tensor_copy(base_row[:], bank[6][0:1, 0:16]).then_inc(s_basb, 1)
            ve.wait_ge(s_basb, 17)
            for i in range(NT):
                ve.wait_ge(s_ygtb[(2 * i) % 4], 16 * ((2 * i) // 4 + 1))
                ve.wait_ge(s_ygtb[(2 * i + 1) % 4], 16 * ((2 * i + 1) // 4 + 1))
                ve.wait_ge(s_xcb[i % 2], 16 * (i // 2 + 1))
                for kk in range(2):
                    ecol = eidx_sb[:, 2 * i + kk:2 * i + kk + 1]
                    ve.tensor_scalar(st1[:], iota8[:], ecol, None, ALU.is_equal)
                    ve.drain()
                    ve.tensor_tensor(st2[:], st1[:],
                                     base_bc[:, kk * NE:(kk + 1) * NE], ALU.mult)
                    ve.drain()
                    ve.reduce_sum(colA[:], st2[:], axis=AX.X)
                    ve.drain()
                    ve.tensor_add(colA[:], colA[:],
                                  rk_sb[:, 2 * i + kk:2 * i + kk + 1])
                    ve.drain()
                    ve.tensor_scalar(colB[:], colA[:], float(CAPACITY), None, ALU.is_lt)
                    ve.drain()
                    ve.tensor_tensor((c0col if kk == 0 else c1col)[:], colB[:],
                                     wco_sb[:, 2 * i + kk:2 * i + kk + 1], ALU.mult)
                    ve.drain()
                if i >= 1:
                    ve.wait_ge(s_gsil, i)
                ve.scalar_tensor_tensor(g_sb[:], yg_sb[(2 * i) % 4][:], c0col[:],
                                        bias_bc[:], ALU.mult, ALU.add)
                ve.drain()
                ve.scalar_tensor_tensor(g_sb[:], yg_sb[(2 * i + 1) % 4][:], c1col[:],
                                        g_sb[:], ALU.mult, ALU.add).then_inc(s_comb, 1)
                ve.wait_ge(s_gsil, i + 1)
                ob = o_sb[i % 2]
                for c in range(D // HD):
                    ve.tensor_mul(ob[:, c * HD:(c + 1) * HD],
                                  ob[:, c * HD:(c + 1) * HD],
                                  s_sb[:, i * HD:(i + 1) * HD])
                ve.drain()
                ve.tensor_add(ob[:], ob[:], xc_sb[i % 2][:]).then_inc(s_out, 1)

        @block.scalar
        def _(sc):
            sc.wait_ge(s_zt, 1)
            for z in range(NZF):
                r0 = z * 128
                r1 = min(r0 + 128, ZROWS)
                sc.dma_start(out=xe_d[r0:r1, :], in_=zt_sb[0:r1 - r0, :]) \
                    .then_inc(s_zf, 16)
            for i in range(NT):
                sc.wait_ge(s_stat, i + 1)
                if i >= 2:
                    sc.wait_ge(s_xnwb[i % 2], 16 * (i // 2))
                pc = i % 2
                sc.activation(statq[:, pc:pc + 1], statv[:, pc:pc + 1], AF.Sqrt,
                              bias=0.0, scale=1.0).then_inc(s_sq, 1)
                sc.wait_ge(s_rst, i + 1)
                sc.activation(xnt_sb[i % 2][:], xt_sb[i % 3][:], AF.Identity,
                              bias=statnb[:, pc:pc + 1],
                              scale=statr[:, pc:pc + 1]).then_inc(s_xn, 1)
            for i in range(NT):
                sc.wait_ge(s_vs, i + 1)
                sc.activation(st2[:], logit_sb[:, i * NE:(i + 1) * NE],
                              AF.Exp, bias=negmax[:], scale=1.0,
                              accum_out=den[:]).then_inc(s_sv, 1)
            for j in range(NT):
                sc.wait_ge(s_sm, j + 1)
                if j >= 2:
                    ab, ag = rec["attav"][(j - 2) * NTN + NTN - 1]
                    sc.wait_ge(sgd[ab], ag)
                sc.activation(e_sb[j % 2][:], att16[:], AF.Exp,
                              bias=negmax[:], scale=1.0,
                              accum_out=den[:]).then_inc(s_exp, 1)
            for e in range(NE):
                for m_ in range(FC):
                    off = 0
                    for ni, nsz in enumerate(NXC):
                        b, g, _ = rec["l1"][(e, m_, ni)]
                        sc.wait_ge(sgd[b], g)
                        sc.activation(h_sb[:, m_ * C + off: m_ * C + off + nsz],
                                      bank[b][:, 0:nsz], AF.Gelu_apprx_tanh) \
                            .then_inc(sbf[b], 1)
                        off += nsz
            for i in range(NT):
                sc.wait_ge(s_comb, i + 1)
                if i >= 2:
                    sc.wait_ge(s_owb[i % 2], 16 * ((i - 2) // 2 + 1))
                sc.activation(o_sb[i % 2][:], g_sb[:], AF.Silu).then_inc(s_gsil, 1)

        @block.gpsimd
        def _(gp):
            gp.iota(iota8[:], [[1, NE]], base=0, channel_multiplier=0,
                    allow_small_or_imprecise_dtypes=True)
            gp.iota(iotap[:], [[0, 1]], base=0, channel_multiplier=1,
                    allow_small_or_imprecise_dtypes=True)
            gp.iota(bmi[:], [[-1, NE]], base=128, channel_multiplier=0,
                    allow_small_or_imprecise_dtypes=True)
            gp.dma_start(out=wq_sb[:].rearrange("p (k n) -> p k n", k=KD),
                         in_=wq_d).then_inc(s_wc, 16)
            gp.dma_start(out=wk_sb[:].rearrange("p (k n) -> p k n", k=KD),
                         in_=wk_d).then_inc(s_wc, 16)
            gp.dma_start(out=wvr_sb[:].rearrange("p (k n) -> p k n", k=KD),
                         in_=wvr_d).then_inc(s_wc, 16)
            gp.wait_ge(s_off, NT)
            gp.dma_start(out=ccin_d, in_=off_sb[0:1, :]).then_inc(s_ccw, 16)
            gp.wait_ge(s_ccw, 16)
            gp.collective_compute(
                "AllGather", ALU.bypass, replica_groups=[list(range(n_cores))],
                ins=[ccin_d.opt()], outs=[ccout_d.opt()],
            ).then_inc(s_cc, 1)
            gp.wait_ge(s_cc, 1)
            gp.dma_start(out=ccall[:], in_=ccout_d).then_inc(s_ccb, 16)
            gp.dma_start(out=prefm[:], in_=prefm_d).then_inc(s_ccb, 16)
            for i in range(NT):
                gp.wait_ge(s_xwb[i % 4], 16 * (i // 4 + 1))
                gp.wait_ge(s_slot, i + 1)
                if i == 0:
                    gp.wait_ge(s_zf, 16 * NZF)
                for kk in range(2):
                    gp.indirect_dma_start(
                        xe_d,
                        IndirectOffsetOnAxis(ap=sct_sb[:, 2 * i + kk:2 * i + kk + 1],
                                             axis=0),
                        xw_sb[i % 4][:], None,
                    ).then_inc(s_scatb[i % 4], 16)
            w1i = 0
            w2i = 0
            for e in range(NE):
                for m_ in range(FC):
                    w1i += 1
                    if w1i > 2:
                        jb = w1i - 2
                        eb, mb = (jb - 1) // FC, (jb - 1) % FC
                        wb, wg, _ = rec["l1"][(eb, mb, NNX - 1)]
                        gp.wait_ge(sgd[wb], wg)
                    gp.dma_start(out=w1b_sb[(w1i - 1) % 2][:].rearrange(
                        "p (k f) -> p k f", k=KD),
                        in_=w1_d[e, m_]).then_inc(s_w1b[(w1i - 1) % 2], 16)
                for half in range(2):
                    for grp in mgroups():
                        for kb in range(NW2B):
                            w2i += 1
                            if w2i > 2:
                                kind, val = rec["w2done"][w2i - 3]
                                gp.wait_ge(s_w2c if kind == "c" else s_l2g, val)
                            gp.dma_start(
                                out=w2b_sb[(w2i - 1) % 2][:].rearrange(
                                    "p (k j) -> p k j", k=W2KG),
                                in_=w2_d[e, kb * W2KG:(kb + 1) * W2KG, half]
                                .rearrange("k p j -> p k j"),
                            ).then_inc(s_w2b[(w2i - 1) % 2], 16)
            gp.wait_ge(s_basb, 1)
            gp.dma_start(out=ccbase_d, in_=base_row[:]).then_inc(s_ccb, 16)
            nyes = NE * 2 * MC
            for p in range(4):
                tot = nyes // 4 + (1 if p < nyes % 4 else 0)
                gp.wait_ge(s_yewb[p], 16 * tot)
            for i in range(NT):
                for kk in range(2):
                    n = 2 * i + kk
                    if n >= 4:
                        gp.wait_ge(s_comb, (n - 4) // 2 + 1)
                    gp.indirect_dma_start(
                        yg_sb[n % 4][:], None,
                        ye_d,
                        IndirectOffsetOnAxis(ap=rsl_sb[:, n:n + 1], axis=0),
                    ).then_inc(s_ygtb[n % 4], 16)

        @block.sync
        def _(sy):
            sy.dma_start(out=bias_bc[:],
                         in_=bias_d.broadcast_to([128, D])).then_inc(s_cl, 16)
            sy.dma_start(out=identf[:], in_=identf_d).then_inc(s_cl, 16)
            sy.dma_start(out=lincl[:], in_=lincl_d).then_inc(s_cl, 16)
            sy.dma_start(out=sel127[:], in_=sel127_d).then_inc(s_cl, 16)
            sy.dma_start(out=onescol[:], in_=onescol_d).then_inc(s_cl, 16)
            for i in range(NT):
                if i >= 3:
                    sy.wait_ge(s_xn, i - 2)
                sy.dma_start(out=xt_sb[i % 3][:],
                             in_=x_d[i * 128:(i + 1) * 128, :]).then_inc(s_xb[i % 3], 16)
                if i >= 1:
                    sy.wait_ge(s_xn, i)
                    sy.dma_start(out=xn_d[(i - 1) * 128:i * 128, :],
                                 in_=xnt_sb[(i - 1) % 2][:]) \
                        .then_inc(s_xnwb[(i - 1) % 2], 16)
            sy.wait_ge(s_xn, NT)
            sy.dma_start(out=xn_d[(NT - 1) * 128:NT * 128, :],
                         in_=xnt_sb[(NT - 1) % 2][:]) \
                .then_inc(s_xnwb[(NT - 1) % 2], 16)
            sy.wait_ge(s_xnwb[0], 16 * ((NT + 1) // 2))
            sy.wait_ge(s_xnwb[1], 16 * (NT // 2))
            for c in range(KD):
                sy.dma_start(out=xnfm[:, c * S:(c + 1) * S],
                             in_=xn_d[:, c * 128:(c + 1) * 128],
                             transpose=True).then_inc(s_xnfm, 16)
            if debug_taps:
                sy.wait_ge(s_xnfm, 16 * KD)
                sy.dma_start(out=dbg["dbg_xnfm"], in_=xnfm[:]).then_inc(s_cl, 16)
                sy.wait_ge(s_qkd, 1)
                sy.dma_start(out=dbg["dbg_qT"], in_=qT[:]).then_inc(s_cl, 16)
                sy.dma_start(out=dbg["dbg_kT"], in_=kT[:]).then_inc(s_cl, 16)
            sy.wait_ge(s_ccw, 16)
            sy.dma_start(out=cnt_bc[:],
                         in_=ccin_d.broadcast_to([128, 16])).then_inc(s_cntb, 16)
            for i in range(NT):
                if i >= 4:
                    sy.wait_ge(s_scatb[i % 4], 32 * (i // 4))
                sy.dma_start(out=xw_sb[i % 4][:],
                             in_=xn_d[i * 128:(i + 1) * 128, :]) \
                    .then_inc(s_xwb[i % 4], 16)
            def emit_ye_stores(e, yewn):
                for half in range(2):
                    for grp in mgroups():
                        for m_ in grp:
                            yewn += 1
                            yb, ynp, ygc = rec["l2"][yewn - 1]
                            sy.wait_ge(sbf[yb], ygc)
                            sy.dma_start(
                                out=ye_d[e * C + m_ * 128: e * C + (m_ + 1) * 128,
                                         half * 512:(half + 1) * 512],
                                in_=ye_sb[yewn % 4][:, :]) \
                                .then_inc(s_yewb[(yewn - 1) % 4], 16)
                return yewn

            yewn = 0
            for e in range(NE):
                if e == 0:
                    for p in range(4):
                        tot = 2 * (NT // 4 + (1 if p < NT % 4 else 0))
                        sy.wait_ge(s_scatb[p], 16 * tot)
                if e == 1:
                    qb, qg = rec["attqk"][-1]
                    sy.wait_ge(sgd[qb], qg)
                if e >= 2:
                    wb, wg, _ = rec["l1"][(e - 2, FC - 1, NNX - 1)]
                    sy.wait_ge(sgd[wb], wg)
                for k in range(KD):
                    sy.dma_start(out=xe_sb[e % 2][:, k * C:(k + 1) * C],
                                 in_=xe_d[e * C:(e + 1) * C, k * 128:(k + 1) * 128],
                                 transpose=True).then_inc(s_xeb[e % 2], 16)
                if e >= 1:
                    yewn = emit_ye_stores(e - 1, yewn)
            yewn = emit_ye_stores(NE - 1, yewn)
            sy.wait_ge(s_ccb, 16 * 3)
            sy.dma_start(out=base_bc[:],
                         in_=ccbase_d.broadcast_to([128, 16])).then_inc(s_basb, 16)
            for i in range(NT):
                if i >= 2:
                    sy.wait_ge(s_out, i - 1)
                sy.dma_start(out=xc_sb[i % 2][:],
                             in_=x_d[i * 128:(i + 1) * 128, :]) \
                    .then_inc(s_xcb[i % 2], 16)
                if i >= 1:
                    sy.wait_ge(s_out, i)
                    sy.dma_start(out=out_d[(i - 1) * 128:i * 128, :],
                                 in_=o_sb[(i - 1) % 2][:]) \
                        .then_inc(s_owb[(i - 1) % 2], 16)
            sy.wait_ge(s_out, NT)
            sy.dma_start(out=out_d[(NT - 1) * 128:NT * 128, :],
                         in_=o_sb[(NT - 1) % 2][:]).then_inc(s_owb[(NT - 1) % 2], 16)
            sy.wait_ge(s_owb[0], 16 * ((NT + 1) // 2))
            sy.wait_ge(s_owb[1], 16 * (NT // 2))
            if debug_taps:
                ndbg = 0
                sy.dma_start(out=dbg["dbg_xnd"], in_=xn_d).then_inc(s_cl, 16)
                for name, src_sb in (("dbg_s", s_sb), ("dbg_logit", logit_sb),
                                     ("dbg_wco", wco_sb), ("dbg_eidx", eidx_sb),
                                     ("dbg_rk", rk_sb), ("dbg_rsl", rsl_sb),
                                     ("dbg_sct", sct_sb), ("dbg_base", base_bc),
                                     ("dbg_rank", rank_sb)):
                    sy.dma_start(out=dbg[name], in_=src_sb[:]).then_inc(s_cl, 16)
                    ndbg += 1
                sy.wait_ge(s_cl, 16 * 5 + 16 * (ndbg + 4))

    es.close()
    return nc


def _host_layouts(inputs):
    x = np.ascontiguousarray(inputs["L0_attn_x"])
    wq = np.ascontiguousarray(inputs["L0_attn_Wq"])
    wk = np.ascontiguousarray(inputs["L0_attn_Wk"])
    wv = np.ascontiguousarray(inputs["L0_attn_Wv"])
    wr = np.ascontiguousarray(inputs["L0_ffn_router_w"])
    w1 = np.ascontiguousarray(inputs["L0_ffn_w1"])
    w2 = np.ascontiguousarray(inputs["L0_ffn_w2"])
    bias = np.ascontiguousarray(inputs["L0_ffn_bias"]).reshape(1, D)
    S = x.shape[1]
    F = w1.shape[2]
    FC = F // 128

    def chunk_pk(w, ncols):
        return np.ascontiguousarray(w.reshape(KD, 128, ncols).transpose(1, 0, 2))

    wvr = np.concatenate([wv, wr], axis=1)
    w1h = np.ascontiguousarray(
        w1.reshape(NE, KD, 128, FC, 128).transpose(0, 3, 2, 1, 4))
    w2h = np.ascontiguousarray(
        w2.reshape(NE, FC, 128, 2, 512).transpose(0, 1, 3, 2, 4))
    identf = np.eye(128, dtype=np.float32)
    lincl = np.triu(np.ones((128, 128), np.float32))
    sel127 = np.zeros((128, 128), np.float32)
    sel127[0, :] = 1.0
    onescol = np.ones((128, 1), np.float32)
    common = dict(
        wqh=chunk_pk(wq, HD), wkh=chunk_pk(wk, HD), wvrh=chunk_pk(wvr, 72),
        w1h=w1h, w2h=w2h, bias1=bias,
        identf=identf, lincl=lincl, sel127=sel127, onescol=onescol,
    )
    in_maps = []
    for c in range(x.shape[0]):
        prefm = np.zeros((NE, 1), np.float32)
        prefm[:c, 0] = 1.0
        in_maps.append(dict(common, x=x[c], prefm=prefm))
    return in_maps, S, F


def kernel(**inputs):
    in_maps, S, F = _host_layouts(inputs)
    nc = build_program(S=S, F=F, C=1152, n_cores=8)
    res = run_bass_kernel_spmd(nc, in_maps, core_ids=list(range(8)))
    out = np.stack([r["out"] for r in res.results], axis=0)
    return out.astype(np.float32)


# version 06
# speedup vs baseline: 1.0975x; 1.0975x over previous
import numpy as np
import concourse.bass as bass
import concourse.mybir as mybir
from concourse.bass import IndirectOffsetOnAxis
from concourse.bass_utils import run_bass_kernel_spmd

F32 = mybir.dt.float32
BF16 = mybir.dt.bfloat16
F16 = mybir.dt.float16
U32 = mybir.dt.uint32
AF = mybir.ActivationFunctionType
ALU = mybir.AluOpType
AX = mybir.AxisListType

D = 1024
KD = D // 128
HD = 64
NE = 8
EPS = 1e-5
BIGNEG = -1.0e30


def build_program(S=4096, F=4096, C=1152, n_cores=8, debug_taps=False):
    NT = S // 128
    NTN = S // 512
    FC = F // 128
    MC = C // 128
    NXC = [512] * (C // 512) + ([C % 512] if C % 512 else [])
    NNX = len(NXC)
    W2KG = min(4, FC)
    NW2B = (FC + W2KG - 1) // W2KG
    NW2BUF = 3
    NW1BUF = 3
    CAPACITY = S * n_cores // NE
    assert S % 512 == 0 and C % 128 == 0 and F % (128 * W2KG) == 0

    def mgroups():
        if MC <= 8:
            return [list(range(MC))]
        h1 = (MC + 1) // 2
        return [list(range(h1)), list(range(h1, MC))]

    nc = bass.Bass("TRN2", target_bir_lowering=False, debug=False,
                   detect_race_conditions=False)

    x_d = nc.dram_tensor("x", [S, D], F32, kind="ExternalInput").ap()
    wq_d = nc.dram_tensor("wqh", [128, KD, HD], F32, kind="ExternalInput").ap()
    wk_d = nc.dram_tensor("wkh", [128, KD, HD], F32, kind="ExternalInput").ap()
    wvr_d = nc.dram_tensor("wvrh", [128, KD, 72], F32, kind="ExternalInput").ap()
    w1_d = nc.dram_tensor("w1h", [NE, FC, 128, KD, 128], F32, kind="ExternalInput").ap()
    w2_d = nc.dram_tensor("w2h", [NE, FC, 2, 128, 512], F32, kind="ExternalInput").ap()
    bias_d = nc.dram_tensor("bias1", [1, D], F32, kind="ExternalInput").ap()
    identf_d = nc.dram_tensor("identf", [128, 128], F32, kind="ExternalInput").ap()
    lincl_d = nc.dram_tensor("lincl", [128, 128], F32, kind="ExternalInput").ap()
    sel127_d = nc.dram_tensor("sel127", [128, 128], F32, kind="ExternalInput").ap()
    onescol_d = nc.dram_tensor("onescol", [128, 1], F32, kind="ExternalInput").ap()
    prefm_d = nc.dram_tensor("prefm", [NE, 1], F32, kind="ExternalInput").ap()
    out_d = nc.dram_tensor("out", [S, D], F32, kind="ExternalOutput").ap()

    xn_d = nc.dram_tensor("xn_dram", [S, D], BF16).ap()
    ZROWS = NE * C + 128
    xe_d = nc.dram_tensor("xe_dram", [ZROWS, D], BF16).ap()
    ye_d = nc.dram_tensor("ye_dram", [NE * C, D], BF16).ap()
    ccin_d = nc.dram_tensor("cc_in", [1, 16], F32).ap()
    ccout_d = nc.dram_tensor("cc_out", [n_cores, 16], F32).ap()
    ccbase_d = nc.dram_tensor("cc_base", [1, 16], F32).ap()
    if debug_taps:
        NTt = S // 128
        dbg = {
            "dbg_s": nc.dram_tensor("dbg_s", [128, NTt * HD], F32, kind="ExternalOutput").ap(),
            "dbg_logit": nc.dram_tensor("dbg_logit", [128, NTt * NE], F32, kind="ExternalOutput").ap(),
            "dbg_wco": nc.dram_tensor("dbg_wco", [128, NTt * 2], F32, kind="ExternalOutput").ap(),
            "dbg_eidx": nc.dram_tensor("dbg_eidx", [128, NTt * 2], F32, kind="ExternalOutput").ap(),
            "dbg_rk": nc.dram_tensor("dbg_rk", [128, NTt * 2], F32, kind="ExternalOutput").ap(),
            "dbg_rsl": nc.dram_tensor("dbg_rsl", [128, NTt * 2], U32, kind="ExternalOutput").ap(),
            "dbg_sct": nc.dram_tensor("dbg_sct", [128, NTt * 2], U32, kind="ExternalOutput").ap(),
            "dbg_base": nc.dram_tensor("dbg_base", [128, 16], F32, kind="ExternalOutput").ap(),
            "dbg_rank": nc.dram_tensor("dbg_rank", [128, NTt * 16], F32, kind="ExternalOutput").ap(),
            "dbg_xnd": nc.dram_tensor("dbg_xnd", [S, D], BF16, kind="ExternalOutput").ap(),
            "dbg_xnfm": nc.dram_tensor("dbg_xnfm", [128, KD * S], BF16, kind="ExternalOutput").ap(),
            "dbg_qT": nc.dram_tensor("dbg_qT", [64, S], BF16, kind="ExternalOutput").ap(),
            "dbg_kT": nc.dram_tensor("dbg_kT", [64, S], BF16, kind="ExternalOutput").ap(),
        }

    def sbuf(name, shape, dtype):
        return nc.alloc_sbuf_tensor(name, shape, dtype)

    def nbytes(shape, dtype):
        return shape[1] * mybir.dt.size(dtype)

    h_sb = sbuf("h_sb", [128, FC * C], BF16)
    h_addr = nc.lookup_mloc(h_sb).addr
    if KD * S <= FC * C:
        xnfm = nc.alloc_sbuf_tensor_at("xn_fm", [128, KD * S], BF16, offset=h_addr)
    else:
        xnfm = sbuf("xn_fm", [128, KD * S], BF16)
    if KD * S * 2 + S * 2 <= FC * C * 2:
        qT = nc.alloc_sbuf_tensor_at("qT", [64, S], BF16,
                                     offset=h_addr + KD * S * 2)
    else:
        qT = sbuf("qT", [64, S], BF16)
    xe_sb = [sbuf(f"xe{i}", [128, KD * C], BF16) for i in range(2)]
    if S * 2 <= KD * C * 2:
        kT = nc.alloc_sbuf_tensor_at("kT", [64, S], BF16,
                                     offset=nc.lookup_mloc(xe_sb[1]).addr)
    else:
        kT = sbuf("kT", [64, S], BF16)

    R1_P1 = 3 * 4096 + 2 * 2048 + 4096
    R1_P4 = 64 * 0 + 4 * S + 2 * S + 2 * 2 * S
    R1_P6 = 4 * 2048 + 2 * 4096 + 2 * 4096 + 4096
    R1 = max(R1_P1, R1_P4, R1_P6)
    r1_sb = sbuf("r1_arena", [128, R1], mybir.dt.uint8)
    r1_addr = nc.lookup_mloc(r1_sb).addr

    def at(name, shape, dtype, off):
        assert off + nbytes(shape, dtype) <= R1, (name, off)
        return nc.alloc_sbuf_tensor_at(name, shape, dtype, offset=r1_addr + off)

    xt_sb = [at(f"xt{i}", [128, D], F32, i * 4096) for i in range(3)]
    xnt_sb = [at(f"xnt{i}", [128, D], BF16, 3 * 4096 + i * 2048) for i in range(2)]
    scr_sb = at("scr_sb", [128, D], F32, 3 * 4096 + 2 * 2048)
    avacc = at("avacc", [64, S], F32, 0)
    att16 = at("att16", [128, S], F16, 4 * S)
    e_sb = [at(f"esb{i}", [128, S], BF16, 4 * S + 2 * S + i * 2 * S)
            for i in range(2)]
    yg_sb = [at(f"yg{i}", [128, D], BF16, i * 2048) for i in range(4)]
    xc_sb = [at(f"xc{i}", [128, D], F32, 4 * 2048 + i * 4096) for i in range(2)]
    o_sb = [at(f"osb{i}", [128, D], F32, 4 * 2048 + 2 * 4096 + i * 4096)
            for i in range(2)]
    g_sb = at("g_sb", [128, D], F32, 4 * 2048 + 4 * 4096)

    v_sb = sbuf("v_sb", [128, NT * HD], BF16)
    s_sb = sbuf("s_sb", [128, NT * HD], F32)
    logit_sb = sbuf("logit_sb", [128, NT * NE], F32)
    rank_sb = sbuf("rank_sb", [128, NT * 16], F32)
    rk_sb = sbuf("rk_sb", [128, NT * 2], F32)
    eidx_sb = sbuf("eidx_sb", [128, NT * 2], F32)
    wco_sb = sbuf("wco_sb", [128, NT * 2], F32)
    sct_sb = sbuf("sct_sb", [128, NT * 2], U32)
    rsl_sb = sbuf("rsl_sb", [128, NT * 2], U32)
    ind_sb = [sbuf(f"ind{i}", [128, 16], F32) for i in range(2)]
    off_sb = sbuf("off_sb", [128, 16], F32)
    cnt_bc = sbuf("cnt_bc", [128, 16], F32)
    base_bc = sbuf("base_bc", [128, 16], F32)
    ccall = sbuf("ccall", [NE, 16], F32)
    prefm = sbuf("prefm_sb", [NE, 1], F32)
    xw_sb = [sbuf(f"xw{i}", [128, D], BF16) for i in range(4)]
    w1b_sb = [sbuf(f"w1b{i}", [128, KD * 128], BF16) for i in range(NW1BUF)]
    w2b_sb = [sbuf(f"w2b{i}", [128, W2KG * 512], BF16) for i in range(NW2BUF)]
    wq_sb = sbuf("wq_sb", [128, KD * HD], BF16)
    wk_sb = sbuf("wk_sb", [128, KD * HD], BF16)
    wvr_sb = sbuf("wvr_sb", [128, KD * 72], BF16)
    ye_sb = [sbuf(f"yesb{i}", [128, 512], BF16) for i in range(4)]
    bias_bc = sbuf("bias_bc", [128, D], F32)
    zt_sb = sbuf("zt_sb", [128, D], BF16)
    identf = sbuf("identf_sb", [128, 128], F32)
    lincl = sbuf("lincl_sb", [128, 128], F32)
    sel127 = sbuf("sel127_sb", [128, 128], F32)
    onescol = sbuf("onescol_sb", [128, 1], F32)
    iota8 = sbuf("iota8", [128, NE], F32)
    iotap = sbuf("iotap", [128, 1], F32)
    bmi = sbuf("bmi", [128, NE], F32)
    st1 = sbuf("st1", [128, NE], F32)
    st2 = sbuf("st2", [128, NE], F32)
    st3 = sbuf("st3", [128, NE], F32)
    st4 = sbuf("st4", [128, NE], F32)
    cmax = sbuf("cmax", [128, 16], F32)
    statm = sbuf("statm", [128, 2], F32)
    statv = sbuf("statv", [128, 2], F32)
    statq = sbuf("statq", [128, 2], F32)
    statr = sbuf("statr", [128, 2], F32)
    statnb = sbuf("statnb", [128, 2], F32)
    colA = sbuf("colA", [128, 1], F32)
    colB = sbuf("colB", [128, 1], F32)
    colC = sbuf("colC", [128, 1], F32)
    colD = sbuf("colD", [128, 1], F32)
    colE = sbuf("colE", [128, 1], F32)
    colF = sbuf("colF", [128, 1], F32)
    colG = sbuf("colG", [128, 1], F32)
    colH = sbuf("colH", [128, 1], F32)
    c0col = sbuf("c0col", [128, 1], F32)
    c1col = sbuf("c1col", [128, 1], F32)
    negmax = sbuf("negmax", [128, 1], F32)
    den = sbuf("den", [128, 1], F32)
    vrec = sbuf("vrec", [128, 1], F32)
    vn_sb = sbuf("vn_sb", [128, HD], BF16)
    base_row = sbuf("base_row", [1, 16], F32)

    from contextlib import ExitStack
    es = ExitStack()
    bank = [es.enter_context(nc.psum_tensor(f"bank{i}", [128, 512], F32))
            for i in range(8)]

    def sem(name):
        return es.enter_context(nc.semaphore(name))

    s_xb = [sem(f"s_xb{i}") for i in range(3)]
    s_xnwb = [sem(f"s_xnwb{i}") for i in range(2)]
    s_xwb = [sem(f"s_xwb{i}") for i in range(4)]
    s_scatb = [sem(f"s_scatb{i}") for i in range(4)]
    s_xeb = [sem(f"s_xeb{i}") for i in range(2)]
    s_w1b = [sem(f"s_w1b{i}") for i in range(8)]
    s_w2b = [sem(f"s_w2b{i}") for i in range(8)]
    s_yewb = [sem(f"s_yewb{i}") for i in range(4)]
    s_ygtb = [sem(f"s_ygtb{i}") for i in range(4)]
    s_xcb = [sem(f"s_xcb{i}") for i in range(2)]
    s_owb = [sem(f"s_owb{i}") for i in range(2)]
    s_stat = sem("s_stat"); s_xn = sem("s_xn")
    s_xnfm = sem("s_xnfm"); s_wc = sem("s_wc")
    s_zf = sem("s_zf"); s_zt = sem("s_zt"); s_cl = sem("s_cl")
    s_ind = sem("s_ind"); s_off = sem("s_off"); s_ccw = sem("s_ccw")
    s_cc = sem("s_cc"); s_cntb = sem("s_cntb"); s_slot = sem("s_slot")
    s_sm = sem("s_sm")
    s_exp = sem("s_exp"); s_vn = sem("s_vn"); s_l2g = sem("s_l2g")
    s_w2c = sem("s_w2c")
    s_gsil = sem("s_gsil"); s_out = sem("s_out")
    s_ccb = sem("s_ccb"); s_basb = sem("s_basb"); s_sv = sem("s_sv")
    s_vs = sem("s_vs")
    s_comb = sem("s_comb"); s_sq = sem("s_sq"); s_rst = sem("s_rst")
    s_qkd = sem("s_qkd")
    sgd = [sem(f"sgd{i}") for i in range(8)]
    sbf = [sem(f"sbf{i}") for i in range(8)]

    NZF = (ZROWS + 127) // 128

    rec = {"qk": [], "vr": [], "rank": [], "attqk": [], "attav": [],
           "stp": [], "l1": {}, "l2": [], "base": []}
    ngrp = [0] * 8
    sgdcnt = [0] * 8

    def grp_begin(te, b):
        if ngrp[b] > 0:
            te.wait_ge(sbf[b], ngrp[b])

    def grp_end(stream, b):
        ngrp[b] += 1
        sgdcnt[b] += 1
        rec[stream].append((b, sgdcnt[b]))

    with nc.Block() as block:
        @block.tensor
        def _(te):
            te.wait_ge(s_xnfm, 16 * KD)
            te.wait_ge(s_wc, 16 * 3)
            for wsb in (wq_sb, wk_sb):
                for n in range(NTN):
                    b = n % 2
                    grp_begin(te, b)
                    for k in range(KD):
                        m = te.matmul(bank[b][0:HD, 0:512],
                                      wsb[:, k * HD:(k + 1) * HD],
                                      xnfm[:, k * S + n * 512: k * S + (n + 1) * 512],
                                      start=(k == 0), stop=(k == KD - 1))
                    m.then_inc(sgd[b], 1)
                    grp_end("qk", b)
            for i in range(NT):
                b = 2 + (i % 2)
                grp_begin(te, b)
                for k in range(KD):
                    m = te.matmul(bank[b][0:128, 0:72],
                                  xnfm[:, k * S + i * 128: k * S + (i + 1) * 128],
                                  wvr_sb[:, k * 72:(k + 1) * 72],
                                  start=(k == 0), stop=(k == KD - 1))
                m.then_inc(sgd[b], 1)
                grp_end("vr", b)
            te.wait_ge(s_cl, 16 * 5)
            for i in range(NT):
                b = 4 + (i % 2)
                b2 = 6 + (i % 2)
                grp_begin(te, b)
                te.wait_ge(s_ind, i + 1)
                te.wait_ge(s_off, i)
                te.matmul(bank[b][:, 0:16], lincl[:], ind_sb[i % 2][:],
                          start=True, stop=False)
                m = te.matmul(bank[b][:, 0:16], sel127[:], off_sb[:],
                              start=False, stop=True)
                m.then_inc(sgd[b], 1)
                m2 = te.matmul(bank[b2][0:1, 0:16], onescol[:], ind_sb[i % 2][:],
                               start=True, stop=True)
                m2.then_inc(sgd[b2], 1)
                sgdcnt[b2] += 1
                ngrp[b] += 1
                sgdcnt[b] += 1
                rec["rank"].append(((b, sgdcnt[b]), (b2, sgdcnt[b2])))
            for j in range(NT):
                for ic in range(NTN):
                    b = ic % 2
                    grp_begin(te, b)
                    m = te.matmul(bank[b][:, :],
                                  kT[0:HD, j * 128:(j + 1) * 128],
                                  qT[0:HD, ic * 512:(ic + 1) * 512],
                                  start=True, stop=True)
                    m.then_inc(sgd[b], 1)
                    grp_end("attqk", b)
                te.wait_ge(s_vn, j + 1)
                for ic in range(NTN):
                    b = 2 + ic % 2
                    grp_begin(te, b)
                    m = te.matmul(bank[b][0:HD, :],
                                  vn_sb[:],
                                  e_sb[j % 2][:, ic * 512:(ic + 1) * 512],
                                  start=True, stop=True)
                    m.then_inc(sgd[b], 1)
                    grp_end("attav", b)
            te.wait_ge(sbf[2], ngrp[2])
            te.wait_ge(sbf[3], ngrp[3])
            for i in range(NT):
                b = 4 + (i % 2)
                grp_begin(te, b)
                m = te.transpose(bank[b][0:128, 0:HD],
                                 avacc[0:HD, i * 128:(i + 1) * 128],
                                 identf[0:HD, 0:HD])
                m.then_inc(sgd[b], 1)
                grp_end("stp", b)
            w1_used = 0
            w2_used = 0
            gel_total = 0
            npass = 0
            nw2c = 0
            rec["w2done"] = []
            for e in range(NE):
                te.wait_ge(s_xeb[e % 2], 16 * KD * (e // 2 + 1))
                for m_ in range(FC):
                    w1_used += 1
                    te.wait_ge(s_w1b[(w1_used - 1) % NW1BUF],
                               16 * ((w1_used - 1) // NW1BUF + 1))
                    bs = (m_ % 2) * NNX
                    for ni in range(NNX):
                        grp_begin(te, bs + ni)
                    for k in range(KD):
                        off = 0
                        for ni, nsz in enumerate(NXC):
                            b = bs + ni
                            mm = te.matmul(
                                bank[b][:, 0:nsz],
                                w1b_sb[(w1_used - 1) % NW1BUF][:, k * 128:(k + 1) * 128],
                                xe_sb[e % 2][:, k * C + off: k * C + off + nsz],
                                start=(k == 0), stop=(k == KD - 1))
                            if k == KD - 1:
                                mm.then_inc(sgd[b], 1)
                            off += nsz
                    for ni in range(NNX):
                        sgdcnt[bs + ni] += 1
                        rec["l1"][(e, m_, ni)] = (bs + ni, sgdcnt[bs + ni],
                                                  ngrp[bs + ni] + 1)
                        ngrp[bs + ni] += 1
                first_pass = True
                for half in range(2):
                    for grp in mgroups():
                        for mi in range(len(grp)):
                            grp_begin(te, mi % 8)
                        for k2 in range(FC):
                            if first_pass:
                                for ni in range(NNX):
                                    hb, _, hn = rec["l1"][(e, k2, ni)]
                                    te.wait_ge(sbf[hb], hn)
                            if k2 % W2KG == 0:
                                w2_used += 1
                                te.wait_ge(s_w2b[(w2_used - 1) % NW2BUF],
                                           16 * ((w2_used - 1) // NW2BUF + 1))
                            ws = w2b_sb[(w2_used - 1) % NW2BUF][:, (k2 % W2KG) * 512:
                                                                (k2 % W2KG + 1) * 512]
                            blk_end = (k2 % W2KG == W2KG - 1 or k2 == FC - 1)
                            for mi, m_ in enumerate(grp):
                                b = mi % 8
                                mm = te.matmul(
                                    bank[b][:, :],
                                    h_sb[:, k2 * C + m_ * 128: k2 * C + (m_ + 1) * 128],
                                    ws,
                                    start=(k2 == 0), stop=(k2 == FC - 1))
                                last_mi = (mi == len(grp) - 1)
                                if k2 == FC - 1 and last_mi:
                                    mm.then_inc(s_l2g, 1)
                                elif blk_end and last_mi:
                                    mm.then_inc(s_w2c, 1)
                            if blk_end:
                                if k2 == FC - 1:
                                    rec["w2done"].append(("g", npass + 1))
                                else:
                                    nw2c += 1
                                    rec["w2done"].append(("c", nw2c))
                        npass += 1
                        for mi in range(len(grp)):
                            b = mi % 8
                            ngrp[b] += 1
                            rec["l2"].append((b, npass, ngrp[b]))
                        first_pass = False
            te.wait_ge(s_cc, 1)
            te.wait_ge(s_ccb, 16 * 2)
            te.wait_ge(s_off, NT)
            b = 6
            m = te.matmul(bank[b][0:1, 0:16], prefm[:], ccall[:],
                          start=True, stop=True)
            m.then_inc(sgd[b], 1)
            sgdcnt[6] += 1
            rec["base_g"] = sgdcnt[6]

        @block.vector
        def _(ve):
            ve.memset(off_sb[:], 0.0)
            ve.memset(zt_sb[:], 0).then_inc(s_zt, 1)
            for i in range(NT + 1):
                if i < NT:
                    ve.wait_ge(s_xb[i % 3], 16 * (i // 3 + 1))
                    if i >= 2:
                        ve.wait_ge(s_xn, i - 1)
                    xt = xt_sb[i % 3]
                    pc = i % 2
                    ve.reduce_sum(statm[:, pc:pc + 1], xt[:], axis=AX.X)
                    ve.scalar_tensor_tensor(scr_sb[:], xt[:], 1.0, xt[:],
                                            ALU.mult, ALU.mult,
                                            accum_out=statv[:, pc:pc + 1])
                    ve.drain()
                    ve.tensor_scalar_mul(statm[:, pc:pc + 1], statm[:, pc:pc + 1],
                                         1.0 / D)
                    ve.tensor_scalar_mul(statv[:, pc:pc + 1], statv[:, pc:pc + 1],
                                         1.0 / D)
                    ve.drain()
                    ve.tensor_tensor(colH[:], statm[:, pc:pc + 1],
                                     statm[:, pc:pc + 1], ALU.mult)
                    ve.drain()
                    ve.tensor_sub(statv[:, pc:pc + 1], statv[:, pc:pc + 1], colH[:])
                    ve.drain()
                    ve.tensor_scalar_add(statv[:, pc:pc + 1], statv[:, pc:pc + 1],
                                         EPS).then_inc(s_stat, 1)
                if i >= 1:
                    pr = (i - 1) % 2
                    ve.wait_ge(s_sq, i)
                    ve.reciprocal(statr[:, pr:pr + 1], statq[:, pr:pr + 1])
                    ve.drain()
                    ve.tensor_tensor(statnb[:, pr:pr + 1], statm[:, pr:pr + 1],
                                     statr[:, pr:pr + 1], ALU.mult)
                    ve.drain()
                    ve.tensor_scalar_mul(statnb[:, pr:pr + 1],
                                         statnb[:, pr:pr + 1], -1.0) \
                        .then_inc(s_rst, 1)
            qki = 0
            for dst in (qT, kT):
                for n in range(NTN):
                    b, g = rec["qk"][qki]; qki += 1
                    ve.wait_ge(sgd[b], g)
                    ve.tensor_copy(dst[0:HD, n * 512:(n + 1) * 512],
                                   bank[b][0:HD, 0:512]).then_inc(sbf[b], 1)
            ve.sem_inc(s_qkd, 1)
            for i in range(NT):
                b, g = rec["vr"][i]
                ve.wait_ge(sgd[b], g)
                ve.tensor_copy(v_sb[:, i * HD:(i + 1) * HD], bank[b][0:128, 0:HD])
                ve.tensor_copy(logit_sb[:, i * NE:(i + 1) * NE],
                               bank[b][0:128, HD:72]).then_inc(sbf[b], 1)
            for i in range(NT):
                lg = logit_sb[:, i * NE:(i + 1) * NE]
                ve.reduce_max(negmax[:], lg, axis=AX.X, negate=True) \
                    .then_inc(s_vs, 1)
                ve.wait_ge(s_sv, i + 1)
                ve.reciprocal(vrec[:], den[:])
                ve.drain()
                ve.tensor_scalar_mul(st1[:], st2[:], vrec[:])
                ve.drain()
                ve.reduce_max(colA[:], st1[:], axis=AX.X)
                ve.drain()
                ve.tensor_scalar(st3[:], st1[:], colA[:], None, ALU.is_ge)
                ve.drain()
                ve.tensor_tensor(st4[:], st3[:], bmi[:], ALU.mult)
                ve.drain()
                ve.reduce_max(colB[:], st4[:], axis=AX.X)
                ve.drain()
                ve.tensor_scalar(colB[:], colB[:], -1.0, 128.0, ALU.mult, ALU.add)
                ve.drain()
                ve.tensor_scalar(st3[:], iota8[:], colB[:], None, ALU.is_equal)
                ve.drain()
                ve.scalar_tensor_tensor(st4[:], st3[:], BIGNEG, st1[:],
                                        ALU.mult, ALU.add)
                ve.drain()
                ve.reduce_max(colC[:], st4[:], axis=AX.X)
                ve.drain()
                ve.tensor_scalar(st4[:], st4[:], colC[:], None, ALU.is_ge)
                ve.drain()
                ve.tensor_tensor(st4[:], st4[:], bmi[:], ALU.mult)
                ve.drain()
                ve.reduce_max(colD[:], st4[:], axis=AX.X)
                ve.drain()
                ve.tensor_scalar(colD[:], colD[:], -1.0, 128.0, ALU.mult, ALU.add)
                ve.drain()
                ve.tensor_scalar(st4[:], iota8[:], colD[:], None, ALU.is_equal)
                ve.drain()
                ve.tensor_copy(eidx_sb[:, 2 * i:2 * i + 1], colB[:])
                ve.drain()
                ve.tensor_copy(eidx_sb[:, 2 * i + 1:2 * i + 2], colD[:])
                ve.drain()
                ve.tensor_copy(wco_sb[:, 2 * i:2 * i + 1], colA[:])
                ve.drain()
                ve.tensor_copy(wco_sb[:, 2 * i + 1:2 * i + 2], colC[:])
                ve.drain()
                ve.tensor_copy(ind_sb[i % 2][:, 0:NE], st3[:])
                ve.drain()
                ve.tensor_copy(ind_sb[i % 2][:, NE:16], st4[:]).then_inc(s_ind, 1)
                (b, g), (b2, g2) = rec["rank"][i]
                ve.wait_ge(sgd[b], g)
                ve.tensor_sub(rank_sb[:, i * 16:(i + 1) * 16], bank[b][:, 0:16],
                              ind_sb[i % 2][:]).then_inc(sbf[b], 1)
                ve.wait_ge(sgd[b2], g2)
                ve.drain()
                ve.tensor_add(off_sb[0:1, :], off_sb[0:1, :],
                              bank[b2][0:1, 0:16]).then_inc(s_off, 1)
            ve.wait_ge(s_cntb, 16)
            for i in range(NT):
                for kk in range(2):
                    ecol = eidx_sb[:, 2 * i + kk:2 * i + kk + 1]
                    ve.tensor_scalar(st1[:], iota8[:], ecol, None, ALU.is_equal)
                    ve.drain()
                    ve.tensor_tensor(st2[:], st1[:],
                                     rank_sb[:, i * 16 + kk * NE: i * 16 + (kk + 1) * NE],
                                     ALU.mult)
                    ve.drain()
                    ve.reduce_sum(colA[:], st2[:], axis=AX.X)
                    ve.drain()
                    if kk == 1:
                        ve.tensor_tensor(st2[:], st1[:], cnt_bc[:, 0:NE], ALU.mult)
                        ve.drain()
                        ve.reduce_sum(colB[:], st2[:], axis=AX.X)
                        ve.drain()
                        ve.tensor_add(colB[:], colA[:], colB[:])
                        ve.drain()
                    else:
                        ve.tensor_copy(colB[:], colA[:])
                        ve.drain()
                    ve.tensor_scalar(colC[:], colB[:], float(C), None, ALU.is_ge)
                    ve.drain()
                    ve.scalar_tensor_tensor(colD[:], colC[:], 1.0e9, colA[:],
                                            ALU.mult, ALU.add)
                    ve.drain()
                    ve.tensor_copy(rk_sb[:, 2 * i + kk:2 * i + kk + 1], colD[:])
                    ve.scalar_tensor_tensor(colD[:], ecol, float(C), colB[:],
                                            ALU.mult, ALU.add)
                    ve.tensor_scalar_add(colE[:], iotap[:], float(NE * C))
                    ve.drain()
                    ve.tensor_sub(colE[:], colE[:], colD[:])
                    ve.drain()
                    ve.scalar_tensor_tensor(colF[:], colE[:], colC[:], colD[:],
                                            ALU.mult, ALU.add)
                    ve.drain()
                    ve.tensor_copy(sct_sb[:, 2 * i + kk:2 * i + kk + 1], colF[:])
                    ve.tensor_scalar_min(colB[:], colB[:], float(C - 1))
                    ve.drain()
                    ve.scalar_tensor_tensor(colD[:], ecol, float(C), colB[:],
                                            ALU.mult, ALU.add)
                    ve.drain()
                    mcp = ve.tensor_copy(rsl_sb[:, 2 * i + kk:2 * i + kk + 1], colD[:])
                    if kk == 1:
                        mcp.then_inc(s_slot, 1)
            ve.memset(avacc[:], 0.0)
            for j in range(NT):
                if j >= 1:
                    ve.wait_ge(s_exp, j)
                for ic in range(NTN):
                    b, g = rec["attqk"][j * NTN + ic]
                    ve.wait_ge(sgd[b], g)
                    ve.tensor_scalar(att16[:, ic * 512:(ic + 1) * 512],
                                     bank[b][:, :], 0.0, None, ALU.add, ALU.max,
                                     accum_out=cmax[:, ic:ic + 1]).then_inc(sbf[b], 1)
                ve.drain()
                ve.reduce_max(negmax[:], cmax[:, 0:NTN], axis=AX.X, negate=True) \
                    .then_inc(s_sm, 1)
                ve.wait_ge(s_exp, j + 1)
                ve.reciprocal(vrec[:], den[:])
                ve.drain()
                if j >= 1:
                    ab, ag = rec["attav"][(j - 1) * NTN + NTN - 1]
                    ve.wait_ge(sgd[ab], ag)
                ve.tensor_scalar_mul(vn_sb[:], v_sb[:, j * HD:(j + 1) * HD], vrec[:]) \
                    .then_inc(s_vn, 1)
                for ic in range(NTN):
                    b, g = rec["attav"][j * NTN + ic]
                    ve.wait_ge(sgd[b], g)
                    ve.tensor_add(avacc[0:HD, ic * 512:(ic + 1) * 512],
                                  avacc[0:HD, ic * 512:(ic + 1) * 512],
                                  bank[b][0:HD, 0:512]).then_inc(sbf[b], 1)
            for i in range(NT):
                b, g = rec["stp"][i]
                ve.wait_ge(sgd[b], g)
                ve.tensor_copy(s_sb[:, i * HD:(i + 1) * HD],
                               bank[b][0:128, 0:HD]).then_inc(sbf[b], 1)
            yn = 0
            for b, npass_, gcount in rec["l2"]:
                yn += 1
                if yn > 4:
                    ve.wait_ge(s_yewb[(yn - 4 - 1) % 4], 16 * ((yn - 4 - 1) // 4 + 1))
                ve.wait_ge(s_l2g, npass_)
                ve.tensor_copy(ye_sb[yn % 4][:, :], bank[b][:, :]).then_inc(sbf[b], 1)
            ve.wait_ge(sgd[6], rec["base_g"])
            ve.tensor_copy(base_row[:], bank[6][0:1, 0:16]).then_inc(s_basb, 1)
            ve.wait_ge(s_basb, 17)
            for i in range(NT):
                ve.wait_ge(s_ygtb[(2 * i) % 4], 16 * ((2 * i) // 4 + 1))
                ve.wait_ge(s_ygtb[(2 * i + 1) % 4], 16 * ((2 * i + 1) // 4 + 1))
                ve.wait_ge(s_xcb[i % 2], 16 * (i // 2 + 1))
                for kk in range(2):
                    ecol = eidx_sb[:, 2 * i + kk:2 * i + kk + 1]
                    ve.tensor_scalar(st1[:], iota8[:], ecol, None, ALU.is_equal)
                    ve.drain()
                    ve.tensor_tensor(st2[:], st1[:],
                                     base_bc[:, kk * NE:(kk + 1) * NE], ALU.mult)
                    ve.drain()
                    ve.reduce_sum(colA[:], st2[:], axis=AX.X)
                    ve.drain()
                    ve.tensor_add(colA[:], colA[:],
                                  rk_sb[:, 2 * i + kk:2 * i + kk + 1])
                    ve.drain()
                    ve.tensor_scalar(colB[:], colA[:], float(CAPACITY), None, ALU.is_lt)
                    ve.drain()
                    ve.tensor_tensor((c0col if kk == 0 else c1col)[:], colB[:],
                                     wco_sb[:, 2 * i + kk:2 * i + kk + 1], ALU.mult)
                    ve.drain()
                if i >= 1:
                    ve.wait_ge(s_gsil, i)
                ve.scalar_tensor_tensor(g_sb[:], yg_sb[(2 * i) % 4][:], c0col[:],
                                        bias_bc[:], ALU.mult, ALU.add)
                ve.drain()
                ve.scalar_tensor_tensor(g_sb[:], yg_sb[(2 * i + 1) % 4][:], c1col[:],
                                        g_sb[:], ALU.mult, ALU.add).then_inc(s_comb, 1)
                ve.wait_ge(s_gsil, i + 1)
                ob = o_sb[i % 2]
                for c in range(D // HD):
                    ve.tensor_mul(ob[:, c * HD:(c + 1) * HD],
                                  ob[:, c * HD:(c + 1) * HD],
                                  s_sb[:, i * HD:(i + 1) * HD])
                ve.drain()
                ve.tensor_add(ob[:], ob[:], xc_sb[i % 2][:]).then_inc(s_out, 1)

        @block.scalar
        def _(sc):
            sc.wait_ge(s_zt, 1)
            for z in range(NZF):
                r0 = z * 128
                r1 = min(r0 + 128, ZROWS)
                sc.dma_start(out=xe_d[r0:r1, :], in_=zt_sb[0:r1 - r0, :]) \
                    .then_inc(s_zf, 16)
            for i in range(NT):
                sc.wait_ge(s_stat, i + 1)
                if i >= 2:
                    sc.wait_ge(s_xnwb[i % 2], 16 * (i // 2))
                pc = i % 2
                sc.activation(statq[:, pc:pc + 1], statv[:, pc:pc + 1], AF.Sqrt,
                              bias=0.0, scale=1.0).then_inc(s_sq, 1)
                sc.wait_ge(s_rst, i + 1)
                sc.activation(xnt_sb[i % 2][:], xt_sb[i % 3][:], AF.Identity,
                              bias=statnb[:, pc:pc + 1],
                              scale=statr[:, pc:pc + 1]).then_inc(s_xn, 1)
            for i in range(NT):
                sc.wait_ge(s_vs, i + 1)
                sc.activation(st2[:], logit_sb[:, i * NE:(i + 1) * NE],
                              AF.Exp, bias=negmax[:], scale=1.0,
                              accum_out=den[:]).then_inc(s_sv, 1)
            for j in range(NT):
                sc.wait_ge(s_sm, j + 1)
                if j >= 2:
                    ab, ag = rec["attav"][(j - 2) * NTN + NTN - 1]
                    sc.wait_ge(sgd[ab], ag)
                sc.activation(e_sb[j % 2][:], att16[:], AF.Exp,
                              bias=negmax[:], scale=1.0,
                              accum_out=den[:]).then_inc(s_exp, 1)
            for e in range(NE):
                for m_ in range(FC):
                    off = 0
                    for ni, nsz in enumerate(NXC):
                        b, g, _ = rec["l1"][(e, m_, ni)]
                        sc.wait_ge(sgd[b], g)
                        sc.activation(h_sb[:, m_ * C + off: m_ * C + off + nsz],
                                      bank[b][:, 0:nsz], AF.Gelu_apprx_tanh) \
                            .then_inc(sbf[b], 1)
                        off += nsz
            for i in range(NT):
                sc.wait_ge(s_comb, i + 1)
                if i >= 2:
                    sc.wait_ge(s_owb[i % 2], 16 * ((i - 2) // 2 + 1))
                sc.activation(o_sb[i % 2][:], g_sb[:], AF.Silu).then_inc(s_gsil, 1)

        @block.gpsimd
        def _(gp):
            gp.iota(iota8[:], [[1, NE]], base=0, channel_multiplier=0,
                    allow_small_or_imprecise_dtypes=True)
            gp.iota(iotap[:], [[0, 1]], base=0, channel_multiplier=1,
                    allow_small_or_imprecise_dtypes=True)
            gp.iota(bmi[:], [[-1, NE]], base=128, channel_multiplier=0,
                    allow_small_or_imprecise_dtypes=True)
            gp.dma_start(out=wq_sb[:].rearrange("p (k n) -> p k n", k=KD),
                         in_=wq_d).then_inc(s_wc, 16)
            gp.dma_start(out=wk_sb[:].rearrange("p (k n) -> p k n", k=KD),
                         in_=wk_d).then_inc(s_wc, 16)
            gp.dma_start(out=wvr_sb[:].rearrange("p (k n) -> p k n", k=KD),
                         in_=wvr_d).then_inc(s_wc, 16)
            gp.wait_ge(s_off, NT)
            gp.dma_start(out=ccin_d, in_=off_sb[0:1, :]).then_inc(s_ccw, 16)
            gp.wait_ge(s_ccw, 16)
            gp.collective_compute(
                "AllGather", ALU.bypass, replica_groups=[list(range(n_cores))],
                ins=[ccin_d.opt()], outs=[ccout_d.opt()],
            ).then_inc(s_cc, 1)
            gp.wait_ge(s_cc, 1)
            gp.dma_start(out=ccall[:], in_=ccout_d).then_inc(s_ccb, 16)
            gp.dma_start(out=prefm[:], in_=prefm_d).then_inc(s_ccb, 16)
            for i in range(NT):
                gp.wait_ge(s_xwb[i % 4], 16 * (i // 4 + 1))
                gp.wait_ge(s_slot, i + 1)
                if i == 0:
                    gp.wait_ge(s_zf, 16 * NZF)
                for kk in range(2):
                    gp.indirect_dma_start(
                        xe_d,
                        IndirectOffsetOnAxis(ap=sct_sb[:, 2 * i + kk:2 * i + kk + 1],
                                             axis=0),
                        xw_sb[i % 4][:], None,
                    ).then_inc(s_scatb[i % 4], 16)
            w1i = 0
            w2i = 0
            for e in range(NE):
                for m_ in range(FC):
                    w1i += 1
                    if w1i > 2:
                        jb = w1i - 2
                        eb, mb = (jb - 1) // FC, (jb - 1) % FC
                        wb, wg, _ = rec["l1"][(eb, mb, NNX - 1)]
                        gp.wait_ge(sgd[wb], wg)
                    gp.dma_start(out=w1b_sb[(w1i - 1) % NW1BUF][:].rearrange(
                        "p (k f) -> p k f", k=KD),
                        in_=w1_d[e, m_]).then_inc(s_w1b[(w1i - 1) % NW1BUF], 16)
                for half in range(2):
                    for grp in mgroups():
                        for kb in range(NW2B):
                            w2i += 1
                            if w2i > NW2BUF:
                                kind, val = rec["w2done"][w2i - 1 - NW2BUF]
                                gp.wait_ge(s_w2c if kind == "c" else s_l2g, val)
                            gp.dma_start(
                                out=w2b_sb[(w2i - 1) % NW2BUF][:].rearrange(
                                    "p (k j) -> p k j", k=W2KG),
                                in_=w2_d[e, kb * W2KG:(kb + 1) * W2KG, half]
                                .rearrange("k p j -> p k j"),
                            ).then_inc(s_w2b[(w2i - 1) % NW2BUF], 16)
            gp.wait_ge(s_basb, 1)
            gp.dma_start(out=ccbase_d, in_=base_row[:]).then_inc(s_ccb, 16)
            nyes = NE * 2 * MC
            for p in range(4):
                tot = nyes // 4 + (1 if p < nyes % 4 else 0)
                gp.wait_ge(s_yewb[p], 16 * tot)
            for i in range(NT):
                for kk in range(2):
                    n = 2 * i + kk
                    if n >= 4:
                        gp.wait_ge(s_comb, (n - 4) // 2 + 1)
                    gp.indirect_dma_start(
                        yg_sb[n % 4][:], None,
                        ye_d,
                        IndirectOffsetOnAxis(ap=rsl_sb[:, n:n + 1], axis=0),
                    ).then_inc(s_ygtb[n % 4], 16)

        @block.sync
        def _(sy):
            sy.dma_start(out=bias_bc[:],
                         in_=bias_d.broadcast_to([128, D])).then_inc(s_cl, 16)
            sy.dma_start(out=identf[:], in_=identf_d).then_inc(s_cl, 16)
            sy.dma_start(out=lincl[:], in_=lincl_d).then_inc(s_cl, 16)
            sy.dma_start(out=sel127[:], in_=sel127_d).then_inc(s_cl, 16)
            sy.dma_start(out=onescol[:], in_=onescol_d).then_inc(s_cl, 16)
            for i in range(NT):
                if i >= 3:
                    sy.wait_ge(s_xn, i - 2)
                sy.dma_start(out=xt_sb[i % 3][:],
                             in_=x_d[i * 128:(i + 1) * 128, :]).then_inc(s_xb[i % 3], 16)
                if i >= 1:
                    sy.wait_ge(s_xn, i)
                    sy.dma_start(out=xn_d[(i - 1) * 128:i * 128, :],
                                 in_=xnt_sb[(i - 1) % 2][:]) \
                        .then_inc(s_xnwb[(i - 1) % 2], 16)
            sy.wait_ge(s_xn, NT)
            sy.dma_start(out=xn_d[(NT - 1) * 128:NT * 128, :],
                         in_=xnt_sb[(NT - 1) % 2][:]) \
                .then_inc(s_xnwb[(NT - 1) % 2], 16)
            sy.wait_ge(s_xnwb[0], 16 * ((NT + 1) // 2))
            sy.wait_ge(s_xnwb[1], 16 * (NT // 2))
            for c in range(KD):
                sy.dma_start(out=xnfm[:, c * S:(c + 1) * S],
                             in_=xn_d[:, c * 128:(c + 1) * 128],
                             transpose=True).then_inc(s_xnfm, 16)
            if debug_taps:
                sy.wait_ge(s_xnfm, 16 * KD)
                sy.dma_start(out=dbg["dbg_xnfm"], in_=xnfm[:]).then_inc(s_cl, 16)
                sy.wait_ge(s_qkd, 1)
                sy.dma_start(out=dbg["dbg_qT"], in_=qT[:]).then_inc(s_cl, 16)
                sy.dma_start(out=dbg["dbg_kT"], in_=kT[:]).then_inc(s_cl, 16)
            sy.wait_ge(s_ccw, 16)
            sy.dma_start(out=cnt_bc[:],
                         in_=ccin_d.broadcast_to([128, 16])).then_inc(s_cntb, 16)
            for i in range(NT):
                if i >= 4:
                    sy.wait_ge(s_scatb[i % 4], 32 * (i // 4))
                sy.dma_start(out=xw_sb[i % 4][:],
                             in_=xn_d[i * 128:(i + 1) * 128, :]) \
                    .then_inc(s_xwb[i % 4], 16)
            def emit_ye_stores(e, yewn):
                for half in range(2):
                    for grp in mgroups():
                        for m_ in grp:
                            yewn += 1
                            yb, ynp, ygc = rec["l2"][yewn - 1]
                            sy.wait_ge(sbf[yb], ygc)
                            sy.dma_start(
                                out=ye_d[e * C + m_ * 128: e * C + (m_ + 1) * 128,
                                         half * 512:(half + 1) * 512],
                                in_=ye_sb[yewn % 4][:, :]) \
                                .then_inc(s_yewb[(yewn - 1) % 4], 16)
                return yewn

            yewn = 0
            for e in range(NE):
                if e == 0:
                    for p in range(4):
                        tot = 2 * (NT // 4 + (1 if p < NT % 4 else 0))
                        sy.wait_ge(s_scatb[p], 16 * tot)
                if e == 1:
                    qb, qg = rec["attqk"][-1]
                    sy.wait_ge(sgd[qb], qg)
                if e >= 2:
                    wb, wg, _ = rec["l1"][(e - 2, FC - 1, NNX - 1)]
                    sy.wait_ge(sgd[wb], wg)
                for k in range(KD):
                    sy.dma_start(out=xe_sb[e % 2][:, k * C:(k + 1) * C],
                                 in_=xe_d[e * C:(e + 1) * C, k * 128:(k + 1) * 128],
                                 transpose=True).then_inc(s_xeb[e % 2], 16)
                if e >= 1:
                    yewn = emit_ye_stores(e - 1, yewn)
            yewn = emit_ye_stores(NE - 1, yewn)
            sy.wait_ge(s_ccb, 16 * 3)
            sy.dma_start(out=base_bc[:],
                         in_=ccbase_d.broadcast_to([128, 16])).then_inc(s_basb, 16)
            for i in range(NT):
                if i >= 2:
                    sy.wait_ge(s_out, i - 1)
                sy.dma_start(out=xc_sb[i % 2][:],
                             in_=x_d[i * 128:(i + 1) * 128, :]) \
                    .then_inc(s_xcb[i % 2], 16)
                if i >= 1:
                    sy.wait_ge(s_out, i)
                    sy.dma_start(out=out_d[(i - 1) * 128:i * 128, :],
                                 in_=o_sb[(i - 1) % 2][:]) \
                        .then_inc(s_owb[(i - 1) % 2], 16)
            sy.wait_ge(s_out, NT)
            sy.dma_start(out=out_d[(NT - 1) * 128:NT * 128, :],
                         in_=o_sb[(NT - 1) % 2][:]).then_inc(s_owb[(NT - 1) % 2], 16)
            sy.wait_ge(s_owb[0], 16 * ((NT + 1) // 2))
            sy.wait_ge(s_owb[1], 16 * (NT // 2))
            if debug_taps:
                ndbg = 0
                sy.dma_start(out=dbg["dbg_xnd"], in_=xn_d).then_inc(s_cl, 16)
                for name, src_sb in (("dbg_s", s_sb), ("dbg_logit", logit_sb),
                                     ("dbg_wco", wco_sb), ("dbg_eidx", eidx_sb),
                                     ("dbg_rk", rk_sb), ("dbg_rsl", rsl_sb),
                                     ("dbg_sct", sct_sb), ("dbg_base", base_bc),
                                     ("dbg_rank", rank_sb)):
                    sy.dma_start(out=dbg[name], in_=src_sb[:]).then_inc(s_cl, 16)
                    ndbg += 1
                sy.wait_ge(s_cl, 16 * 5 + 16 * (ndbg + 4))

    es.close()
    return nc


def _host_layouts(inputs):
    x = np.ascontiguousarray(inputs["L0_attn_x"])
    wq = np.ascontiguousarray(inputs["L0_attn_Wq"])
    wk = np.ascontiguousarray(inputs["L0_attn_Wk"])
    wv = np.ascontiguousarray(inputs["L0_attn_Wv"])
    wr = np.ascontiguousarray(inputs["L0_ffn_router_w"])
    w1 = np.ascontiguousarray(inputs["L0_ffn_w1"])
    w2 = np.ascontiguousarray(inputs["L0_ffn_w2"])
    bias = np.ascontiguousarray(inputs["L0_ffn_bias"]).reshape(1, D)
    S = x.shape[1]
    F = w1.shape[2]
    FC = F // 128

    def chunk_pk(w, ncols):
        return np.ascontiguousarray(w.reshape(KD, 128, ncols).transpose(1, 0, 2))

    wvr = np.concatenate([wv, wr], axis=1)
    w1h = np.ascontiguousarray(
        w1.reshape(NE, KD, 128, FC, 128).transpose(0, 3, 2, 1, 4))
    w2h = np.ascontiguousarray(
        w2.reshape(NE, FC, 128, 2, 512).transpose(0, 1, 3, 2, 4))
    identf = np.eye(128, dtype=np.float32)
    lincl = np.triu(np.ones((128, 128), np.float32))
    sel127 = np.zeros((128, 128), np.float32)
    sel127[0, :] = 1.0
    onescol = np.ones((128, 1), np.float32)
    common = dict(
        wqh=chunk_pk(wq, HD), wkh=chunk_pk(wk, HD), wvrh=chunk_pk(wvr, 72),
        w1h=w1h, w2h=w2h, bias1=bias,
        identf=identf, lincl=lincl, sel127=sel127, onescol=onescol,
    )
    in_maps = []
    for c in range(x.shape[0]):
        prefm = np.zeros((NE, 1), np.float32)
        prefm[:c, 0] = 1.0
        in_maps.append(dict(common, x=x[c], prefm=prefm))
    return in_maps, S, F


def kernel(**inputs):
    in_maps, S, F = _host_layouts(inputs)
    nc = build_program(S=S, F=F, C=1152, n_cores=8)
    res = run_bass_kernel_spmd(nc, in_maps, core_ids=list(range(8)))
    out = np.stack([r["out"] for r in res.results], axis=0)
    return out.astype(np.float32)


# version 09
# speedup vs baseline: 1.1754x; 1.0710x over previous
import numpy as np
import concourse.bass as bass
import concourse.mybir as mybir
from concourse.bass import IndirectOffsetOnAxis
from concourse.bass_utils import run_bass_kernel_spmd

F32 = mybir.dt.float32
BF16 = mybir.dt.bfloat16
F16 = mybir.dt.float16
U32 = mybir.dt.uint32
AF = mybir.ActivationFunctionType
ALU = mybir.AluOpType
AX = mybir.AxisListType

D = 1024
KD = D // 128
HD = 64
NE = 8
EPS = 1e-5
BIGNEG = -1.0e30


def build_program(S=4096, F=4096, C=1152, n_cores=8, debug_taps=False):
    NT = S // 128
    NTN = S // 512
    FC = F // 128
    MC = C // 128
    NXC = [512] * (C // 512) + ([C % 512] if C % 512 else [])
    NNX = len(NXC)
    W2KG = min(4, FC)
    NW2B = (FC + W2KG - 1) // W2KG
    NW2BUF = 3
    NW1BUF = 3
    CAPACITY = S * n_cores // NE
    assert S % 512 == 0 and C % 128 == 0 and F % (128 * W2KG) == 0

    def mgroups():
        if MC <= 8:
            return [list(range(MC))]
        h1 = (MC + 1) // 2
        return [list(range(h1)), list(range(h1, MC))]

    nc = bass.Bass("TRN2", target_bir_lowering=False, debug=False,
                   detect_race_conditions=False)

    x_d = nc.dram_tensor("x", [S, D], F32, kind="ExternalInput").ap()
    wq_d = nc.dram_tensor("wqh", [128, KD, HD], F32, kind="ExternalInput").ap()
    wk_d = nc.dram_tensor("wkh", [128, KD, HD], F32, kind="ExternalInput").ap()
    wvr_d = nc.dram_tensor("wvrh", [128, KD, 72], F32, kind="ExternalInput").ap()
    w1_d = nc.dram_tensor("w1h", [NE, FC, 128, KD, 128], F32, kind="ExternalInput").ap()
    w2_d = nc.dram_tensor("w2h", [NE, FC, 2, 128, 512], F32, kind="ExternalInput").ap()
    bias_d = nc.dram_tensor("bias1", [1, D], F32, kind="ExternalInput").ap()
    identf_d = nc.dram_tensor("identf", [128, 128], F32, kind="ExternalInput").ap()
    lincl_d = nc.dram_tensor("lincl", [128, 128], F32, kind="ExternalInput").ap()
    sel127_d = nc.dram_tensor("sel127", [128, 128], F32, kind="ExternalInput").ap()
    onescol_d = nc.dram_tensor("onescol", [128, 1], F32, kind="ExternalInput").ap()
    prefm_d = nc.dram_tensor("prefm", [NE, 1], F32, kind="ExternalInput").ap()
    out_d = nc.dram_tensor("out", [S, D], F32, kind="ExternalOutput").ap()

    xn_d = nc.dram_tensor("xn_dram", [S, D], BF16).ap()
    ZROWS = NE * C + 128
    xe_d = nc.dram_tensor("xe_dram", [ZROWS, D], BF16).ap()
    ye_d = nc.dram_tensor("ye_dram", [NE * C, D], BF16).ap()
    ccin_d = nc.dram_tensor("cc_in", [1, 16], F32).ap()
    ccout_d = nc.dram_tensor("cc_out", [n_cores, 16], F32).ap()
    ccbase_d = nc.dram_tensor("cc_base", [1, 16], F32).ap()
    if debug_taps:
        NTt = S // 128
        dbg = {
            "dbg_s": nc.dram_tensor("dbg_s", [128, NTt * HD], F32, kind="ExternalOutput").ap(),
            "dbg_logit": nc.dram_tensor("dbg_logit", [128, NTt * NE], F32, kind="ExternalOutput").ap(),
            "dbg_wco": nc.dram_tensor("dbg_wco", [128, NTt * 2], F32, kind="ExternalOutput").ap(),
            "dbg_eidx": nc.dram_tensor("dbg_eidx", [128, NTt * 2], F32, kind="ExternalOutput").ap(),
            "dbg_rk": nc.dram_tensor("dbg_rk", [128, NTt * 2], F32, kind="ExternalOutput").ap(),
            "dbg_rsl": nc.dram_tensor("dbg_rsl", [128, NTt * 2], U32, kind="ExternalOutput").ap(),
            "dbg_sct": nc.dram_tensor("dbg_sct", [128, NTt * 2], U32, kind="ExternalOutput").ap(),
            "dbg_base": nc.dram_tensor("dbg_base", [128, 16], F32, kind="ExternalOutput").ap(),
            "dbg_rank": nc.dram_tensor("dbg_rank", [128, NTt * 16], F32, kind="ExternalOutput").ap(),
            "dbg_xnd": nc.dram_tensor("dbg_xnd", [S, D], BF16, kind="ExternalOutput").ap(),
            "dbg_xnfm": nc.dram_tensor("dbg_xnfm", [128, KD * S], BF16, kind="ExternalOutput").ap(),
            "dbg_qT": nc.dram_tensor("dbg_qT", [64, S], BF16, kind="ExternalOutput").ap(),
            "dbg_kT": nc.dram_tensor("dbg_kT", [64, S], BF16, kind="ExternalOutput").ap(),
        }

    def sbuf(name, shape, dtype):
        return nc.alloc_sbuf_tensor(name, shape, dtype)

    def nbytes(shape, dtype):
        return shape[1] * mybir.dt.size(dtype)

    h_sb = sbuf("h_sb", [128, FC * C], BF16)
    h_addr = nc.lookup_mloc(h_sb).addr
    if KD * S <= FC * C:
        xnfm = nc.alloc_sbuf_tensor_at("xn_fm", [128, KD * S], BF16, offset=h_addr)
    else:
        xnfm = sbuf("xn_fm", [128, KD * S], BF16)
    if KD * S * 2 + S * 2 <= FC * C * 2:
        qT = nc.alloc_sbuf_tensor_at("qT", [64, S], BF16,
                                     offset=h_addr + KD * S * 2)
    else:
        qT = sbuf("qT", [64, S], BF16)
    xe_sb = [sbuf(f"xe{i}", [128, KD * C], BF16) for i in range(2)]
    if S * 2 <= KD * C * 2:
        kT = nc.alloc_sbuf_tensor_at("kT", [64, S], BF16,
                                     offset=nc.lookup_mloc(xe_sb[1]).addr)
    else:
        kT = sbuf("kT", [64, S], BF16)

    R1_P1 = max(3 * 4096 + 2 * 2048 + 4096,
                10 * (S // 128) * NE * 4 + (S // 128) * 14 * 4)
    R1_P4 = 64 * 0 + 4 * S + 2 * S + 2 * 2 * S
    R1_P6 = 4 * 2048 + 5 * 4096 + 2 * (S // 128) * NE * 4 + (S // 128) * 16
    R1 = max(R1_P1, R1_P4, R1_P6)
    r1_sb = sbuf("r1_arena", [128, R1], mybir.dt.uint8)
    r1_addr = nc.lookup_mloc(r1_sb).addr

    def at(name, shape, dtype, off):
        assert off + nbytes(shape, dtype) <= R1, (name, off)
        return nc.alloc_sbuf_tensor_at(name, shape, dtype, offset=r1_addr + off)

    xt_sb = [at(f"xt{i}", [128, D], F32, i * 4096) for i in range(3)]
    NT_ = S // 128
    expv = at("expv", [128, NT_ * NE], F32, 0)
    scrA = at("scrA", [128, NT_ * NE], F32, NT_ * NE * 4)
    scrB = at("scrB", [128, NT_ * NE], F32, 2 * NT_ * NE * 4)
    scrC = at("scrC", [128, NT_ * NE], F32, 3 * NT_ * NE * 4)
    ind_b = at("ind_b", [128, NT_ * 16], F32, 4 * NT_ * NE * 4)
    iotaB = at("iotaB", [128, NT_ * NE], F32, 6 * NT_ * NE * 4)
    bmi_b = at("bmi_b", [128, NT_ * NE], F32, 7 * NT_ * NE * 4)
    cnt_exp = at("cnt_exp", [128, NT_ * NE], F32, 8 * NT_ * NE * 4)
    ones_b = at("ones_b", [128, NT_ * NE], F32, 9 * NT_ * NE * 4)
    colsB = at("colsB", [128, NT_ * 14], F32, 10 * NT_ * NE * 4)
    p6A = at("p6A", [128, NT_ * NE], F32, 4 * 2048 + 5 * 4096)
    p6B = at("p6B", [128, NT_ * NE], F32, 4 * 2048 + 5 * 4096 + NT_ * NE * 4)
    p6cols = at("p6cols", [128, NT_ * 4], F32, 4 * 2048 + 5 * 4096 + 2 * NT_ * NE * 4)
    coef_b = sbuf("coef_b", [128, NT_ * 2], F32)
    indp = sbuf("indp", [128, NT_ * 16], BF16)
    xnt_sb = [at(f"xnt{i}", [128, D], BF16, 3 * 4096 + i * 2048) for i in range(2)]
    scr_sb = at("scr_sb", [128, D], F32, 3 * 4096 + 2 * 2048)
    avacc = at("avacc", [64, S], F32, 0)
    att16 = at("att16", [128, S], F16, 4 * S)
    e_sb = [at(f"esb{i}", [128, S], BF16, 4 * S + 2 * S + i * 2 * S)
            for i in range(2)]
    yg_sb = [at(f"yg{i}", [128, D], BF16, i * 2048) for i in range(4)]
    xc_sb = [at(f"xc{i}", [128, D], F32, 4 * 2048 + i * 4096) for i in range(2)]
    o_sb = [at(f"osb{i}", [128, D], F32, 4 * 2048 + 2 * 4096 + i * 4096)
            for i in range(2)]
    g_sb = at("g_sb", [128, D], F32, 4 * 2048 + 4 * 4096)

    v_sb = sbuf("v_sb", [128, NT * HD], BF16)
    s_sb = sbuf("s_sb", [128, NT * HD], F32)
    logit_sb = sbuf("logit_sb", [128, NT * NE], F32)
    rank_sb = sbuf("rank_sb", [128, NT * 16], F32)
    rk_sb = sbuf("rk_sb", [128, NT * 2], F32)
    eidx_sb = sbuf("eidx_sb", [128, NT * 2], F32)
    wco_sb = sbuf("wco_sb", [128, NT * 2], F32)
    sct_sb = sbuf("sct_sb", [128, NT * 2], U32)
    rsl_sb = sbuf("rsl_sb", [128, NT * 2], U32)
    ind_sb = [sbuf(f"ind{i}", [128, 16], F32) for i in range(2)]
    off_sb = sbuf("off_sb", [128, 16], F32)
    cnt_bc = sbuf("cnt_bc", [128, 16], F32)
    base_bc = sbuf("base_bc", [128, 16], F32)
    ccall = sbuf("ccall", [NE, 16], F32)
    prefm = sbuf("prefm_sb", [NE, 1], F32)
    xw_sb = [sbuf(f"xw{i}", [128, D], BF16) for i in range(4)]
    w1b_sb = [sbuf(f"w1b{i}", [128, KD * 128], BF16) for i in range(NW1BUF)]
    w2b_sb = [sbuf(f"w2b{i}", [128, W2KG * 512], BF16) for i in range(NW2BUF)]
    wq_sb = sbuf("wq_sb", [128, KD * HD], BF16)
    wk_sb = sbuf("wk_sb", [128, KD * HD], BF16)
    wvr_sb = sbuf("wvr_sb", [128, KD * 72], BF16)
    ye_sb = [sbuf(f"yesb{i}", [128, 512], BF16) for i in range(4)]
    bias_bc = sbuf("bias_bc", [128, D], F32)
    zt_sb = sbuf("zt_sb", [128, D], BF16)
    identf = sbuf("identf_sb", [128, 128], F32)
    lincl = sbuf("lincl_sb", [128, 128], F32)
    sel127 = sbuf("sel127_sb", [128, 128], F32)
    onescol = sbuf("onescol_sb", [128, 1], F32)
    iota8 = sbuf("iota8", [128, NE], F32)
    iotap = sbuf("iotap", [128, 1], F32)
    bmi = sbuf("bmi", [128, NE], F32)
    st1 = sbuf("st1", [128, NE], F32)
    st2 = sbuf("st2", [128, NE], F32)
    st3 = sbuf("st3", [128, NE], F32)
    st4 = sbuf("st4", [128, NE], F32)
    cmax = sbuf("cmax", [128, 16], F32)
    statm = sbuf("statm", [128, 2], F32)
    statv = sbuf("statv", [128, 2], F32)
    statq = sbuf("statq", [128, 2], F32)
    statr = sbuf("statr", [128, 2], F32)
    statnb = sbuf("statnb", [128, 2], F32)
    colA = sbuf("colA", [128, 1], F32)
    colB = sbuf("colB", [128, 1], F32)
    colC = sbuf("colC", [128, 1], F32)
    colD = sbuf("colD", [128, 1], F32)
    colE = sbuf("colE", [128, 1], F32)
    colF = sbuf("colF", [128, 1], F32)
    colG = sbuf("colG", [128, 1], F32)
    colH = sbuf("colH", [128, 1], F32)
    c0col = sbuf("c0col", [128, 1], F32)
    dumpc = sbuf("dumpc", [128, 1], F32)
    c1col = sbuf("c1col", [128, 1], F32)
    negmax = sbuf("negmax", [128, 1], F32)
    den = sbuf("den", [128, 1], F32)
    vrec = sbuf("vrec", [128, 1], F32)
    vn_sb = sbuf("vn_sb", [128, HD], BF16)
    base_row = sbuf("base_row", [1, 16], F32)

    from contextlib import ExitStack
    es = ExitStack()
    bank = [es.enter_context(nc.psum_tensor(f"bank{i}", [128, 512], F32))
            for i in range(8)]

    def sem(name):
        return es.enter_context(nc.semaphore(name))

    s_xb = [sem(f"s_xb{i}") for i in range(3)]
    s_xnwb = [sem(f"s_xnwb{i}") for i in range(2)]
    s_xwb = [sem(f"s_xwb{i}") for i in range(4)]
    s_scatb = [sem(f"s_scatb{i}") for i in range(4)]
    s_xeb = [sem(f"s_xeb{i}") for i in range(2)]
    s_w1b = [sem(f"s_w1b{i}") for i in range(8)]
    s_w2b = [sem(f"s_w2b{i}") for i in range(8)]
    s_yewb = [sem(f"s_yewb{i}") for i in range(4)]
    s_ygtb = [sem(f"s_ygtb{i}") for i in range(4)]
    s_xcb = [sem(f"s_xcb{i}") for i in range(2)]
    s_owb = [sem(f"s_owb{i}") for i in range(2)]
    s_stat = sem("s_stat"); s_xn = sem("s_xn")
    s_xnfm = sem("s_xnfm"); s_wc = sem("s_wc")
    s_zf = sem("s_zf"); s_zt = sem("s_zt"); s_cl = sem("s_cl")
    s_ind = sem("s_ind"); s_off = sem("s_off"); s_ccw = sem("s_ccw")
    s_cc = sem("s_cc"); s_cntb = sem("s_cntb"); s_slot = sem("s_slot")
    s_sm = sem("s_sm")
    s_exp = sem("s_exp"); s_vn = sem("s_vn"); s_l2g = sem("s_l2g")
    s_w2c = sem("s_w2c")
    s_gsil = sem("s_gsil"); s_out = sem("s_out")
    s_ccb = sem("s_ccb"); s_basb = sem("s_basb"); s_sv = sem("s_sv")
    s_vs = sem("s_vs")
    s_comb = sem("s_comb"); s_sq = sem("s_sq"); s_rst = sem("s_rst")
    s_qkd = sem("s_qkd"); s_gpio = sem("s_gpio")
    sgd = [sem(f"sgd{i}") for i in range(8)]
    sbf = [sem(f"sbf{i}") for i in range(8)]

    NZF = (ZROWS + 127) // 128

    rec = {"qk": [], "vr": [], "rank": [], "attqk": [], "attav": [],
           "stp": [], "l1": {}, "l2": [], "base": []}
    ngrp = [0] * 8
    sgdcnt = [0] * 8

    def grp_begin(te, b):
        if ngrp[b] > 0:
            te.wait_ge(sbf[b], ngrp[b])

    def grp_end(stream, b):
        ngrp[b] += 1
        sgdcnt[b] += 1
        rec[stream].append((b, sgdcnt[b]))

    with nc.Block() as block:
        @block.tensor
        def _(te):
            te.wait_ge(s_xnfm, 16 * KD)
            te.wait_ge(s_wc, 16 * 3)
            for wsb in (wq_sb, wk_sb):
                for n in range(NTN):
                    b = n % 2
                    grp_begin(te, b)
                    for k in range(KD):
                        m = te.matmul(bank[b][0:HD, 0:512],
                                      wsb[:, k * HD:(k + 1) * HD],
                                      xnfm[:, k * S + n * 512: k * S + (n + 1) * 512],
                                      start=(k == 0), stop=(k == KD - 1))
                    m.then_inc(sgd[b], 1)
                    grp_end("qk", b)
            for i in range(NT):
                b = 2 + (i % 2)
                grp_begin(te, b)
                for k in range(KD):
                    m = te.matmul(bank[b][0:128, 0:72],
                                  xnfm[:, k * S + i * 128: k * S + (i + 1) * 128],
                                  wvr_sb[:, k * 72:(k + 1) * 72],
                                  start=(k == 0), stop=(k == KD - 1))
                m.then_inc(sgd[b], 1)
                grp_end("vr", b)
            te.wait_ge(s_cl, 16 * 5)
            for i in range(NT):
                b = 4 + (i % 2)
                b2 = 6 + (i % 2)
                grp_begin(te, b)
                te.wait_ge(s_ind, 1)
                te.wait_ge(s_off, i)
                te.matmul(bank[b][:, 0:16], lincl[:], ind_b[:, i * 16:(i + 1) * 16],
                          start=True, stop=False)
                m = te.matmul(bank[b][:, 0:16], sel127[:], off_sb[:],
                              start=False, stop=True)
                m.then_inc(sgd[b], 1)
                m2 = te.matmul(bank[b2][0:1, 0:16], onescol[:],
                               ind_b[:, i * 16:(i + 1) * 16],
                               start=True, stop=True)
                m2.then_inc(sgd[b2], 1)
                sgdcnt[b2] += 1
                ngrp[b] += 1
                sgdcnt[b] += 1
                rec["rank"].append(((b, sgdcnt[b]), (b2, sgdcnt[b2])))
            for j in range(NT):
                for ic in range(NTN):
                    b = ic % 2
                    grp_begin(te, b)
                    m = te.matmul(bank[b][:, :],
                                  kT[0:HD, j * 128:(j + 1) * 128],
                                  qT[0:HD, ic * 512:(ic + 1) * 512],
                                  start=True, stop=True)
                    m.then_inc(sgd[b], 1)
                    grp_end("attqk", b)
                if j == 0:
                    te.wait_ge(s_cc, 1)
                    te.wait_ge(s_ccb, 16 * 2)
                    te.wait_ge(s_off, NT)
                    m = te.matmul(bank[6][0:1, 0:16], prefm[:], ccall[:],
                                  start=True, stop=True)
                    m.then_inc(sgd[6], 1)
                    sgdcnt[6] += 1
                    rec["base_g"] = sgdcnt[6]
                te.wait_ge(s_vn, j + 1)
                for ic in range(NTN):
                    b = 2 + ic % 2
                    grp_begin(te, b)
                    m = te.matmul(bank[b][0:HD, :],
                                  vn_sb[:],
                                  e_sb[j % 2][:, ic * 512:(ic + 1) * 512],
                                  start=True, stop=True)
                    m.then_inc(sgd[b], 1)
                    grp_end("attav", b)
            te.wait_ge(sbf[2], ngrp[2])
            te.wait_ge(sbf[3], ngrp[3])
            for i in range(NT):
                b = 4 + (i % 2)
                grp_begin(te, b)
                m = te.transpose(bank[b][0:128, 0:HD],
                                 avacc[0:HD, i * 128:(i + 1) * 128],
                                 identf[0:HD, 0:HD])
                m.then_inc(sgd[b], 1)
                grp_end("stp", b)
            w1_used = 0
            w2_used = 0
            gel_total = 0
            npass = 0
            nw2c = 0
            rec["w2done"] = []
            for e in range(NE):
                te.wait_ge(s_xeb[e % 2], 16 * KD * (e // 2 + 1))
                for m_ in range(FC):
                    w1_used += 1
                    te.wait_ge(s_w1b[(w1_used - 1) % NW1BUF],
                               16 * ((w1_used - 1) // NW1BUF + 1))
                    bs = (m_ % 2) * NNX
                    for ni in range(NNX):
                        grp_begin(te, bs + ni)
                    for k in range(KD):
                        off = 0
                        for ni, nsz in enumerate(NXC):
                            b = bs + ni
                            mm = te.matmul(
                                bank[b][:, 0:nsz],
                                w1b_sb[(w1_used - 1) % NW1BUF][:, k * 128:(k + 1) * 128],
                                xe_sb[e % 2][:, k * C + off: k * C + off + nsz],
                                start=(k == 0), stop=(k == KD - 1))
                            if k == KD - 1:
                                mm.then_inc(sgd[b], 1)
                            off += nsz
                    for ni in range(NNX):
                        sgdcnt[bs + ni] += 1
                        rec["l1"][(e, m_, ni)] = (bs + ni, sgdcnt[bs + ni],
                                                  ngrp[bs + ni] + 1)
                        ngrp[bs + ni] += 1
                first_pass = True
                for half in range(2):
                    for grp in mgroups():
                        for mi in range(len(grp)):
                            grp_begin(te, mi % 8)
                        for k2 in range(FC):
                            if first_pass:
                                for ni in range(NNX):
                                    hb, _, hn = rec["l1"][(e, k2, ni)]
                                    te.wait_ge(sbf[hb], hn)
                            if k2 % W2KG == 0:
                                w2_used += 1
                                te.wait_ge(s_w2b[(w2_used - 1) % NW2BUF],
                                           16 * ((w2_used - 1) // NW2BUF + 1))
                            ws = w2b_sb[(w2_used - 1) % NW2BUF][:, (k2 % W2KG) * 512:
                                                                (k2 % W2KG + 1) * 512]
                            blk_end = (k2 % W2KG == W2KG - 1 or k2 == FC - 1)
                            for mi, m_ in enumerate(grp):
                                b = mi % 8
                                mm = te.matmul(
                                    bank[b][:, :],
                                    h_sb[:, k2 * C + m_ * 128: k2 * C + (m_ + 1) * 128],
                                    ws,
                                    start=(k2 == 0), stop=(k2 == FC - 1))
                                last_mi = (mi == len(grp) - 1)
                                if k2 == FC - 1 and last_mi:
                                    mm.then_inc(s_l2g, 1)
                                elif blk_end and last_mi:
                                    mm.then_inc(s_w2c, 1)
                            if blk_end:
                                if k2 == FC - 1:
                                    rec["w2done"].append(("g", npass + 1))
                                else:
                                    nw2c += 1
                                    rec["w2done"].append(("c", nw2c))
                        npass += 1
                        for mi in range(len(grp)):
                            b = mi % 8
                            ngrp[b] += 1
                            rec["l2"].append((b, npass, ngrp[b]))
                        first_pass = False

        @block.vector
        def _(ve):
            ve.memset(off_sb[:], 0.0)
            ve.wait_ge(s_wc, 16 * 3)
            ve.drain()
            ve.tensor_scalar_add(dumpc[:], iotap[:], float(NE * C))
            ve.memset(zt_sb[:], 0).then_inc(s_zt, 1)
            for i in range(NT + 1):
                if i < NT:
                    ve.wait_ge(s_xb[i % 3], 16 * (i // 3 + 1))
                    if i >= 2:
                        ve.wait_ge(s_xn, i - 1)
                    xt = xt_sb[i % 3]
                    pc = i % 2
                    ve.reduce_sum(statm[:, pc:pc + 1], xt[:], axis=AX.X)
                    ve.scalar_tensor_tensor(scr_sb[:], xt[:], 1.0, xt[:],
                                            ALU.mult, ALU.mult,
                                            accum_out=statv[:, pc:pc + 1])
                    ve.drain()
                    ve.tensor_scalar_mul(statm[:, pc:pc + 1], statm[:, pc:pc + 1],
                                         1.0 / D)
                    ve.tensor_scalar_mul(statv[:, pc:pc + 1], statv[:, pc:pc + 1],
                                         1.0 / D)
                    ve.drain()
                    ve.tensor_tensor(colH[:], statm[:, pc:pc + 1],
                                     statm[:, pc:pc + 1], ALU.mult)
                    ve.drain()
                    ve.tensor_sub(statv[:, pc:pc + 1], statv[:, pc:pc + 1], colH[:])
                    ve.drain()
                    ve.tensor_scalar_add(statv[:, pc:pc + 1], statv[:, pc:pc + 1],
                                         EPS).then_inc(s_stat, 1)
                if i >= 1:
                    pr = (i - 1) % 2
                    ve.wait_ge(s_sq, i)
                    ve.reciprocal(statr[:, pr:pr + 1], statq[:, pr:pr + 1])
                    ve.drain()
                    ve.tensor_tensor(statnb[:, pr:pr + 1], statm[:, pr:pr + 1],
                                     statr[:, pr:pr + 1], ALU.mult)
                    ve.drain()
                    ve.tensor_scalar_mul(statnb[:, pr:pr + 1],
                                         statnb[:, pr:pr + 1], -1.0) \
                        .then_inc(s_rst, 1)
            qki = 0
            for dst in (qT, kT):
                for n in range(NTN):
                    b, g = rec["qk"][qki]; qki += 1
                    ve.wait_ge(sgd[b], g)
                    ve.tensor_copy(dst[0:HD, n * 512:(n + 1) * 512],
                                   bank[b][0:HD, 0:512]).then_inc(sbf[b], 1)
            ve.sem_inc(s_qkd, 1)
            for i in range(NT):
                b, g = rec["vr"][i]
                ve.wait_ge(sgd[b], g)
                ve.tensor_copy(v_sb[:, i * HD:(i + 1) * HD], bank[b][0:128, 0:HD])
                mlast = ve.tensor_copy(logit_sb[:, i * NE:(i + 1) * NE],
                                       bank[b][0:128, HD:72])
                mlast.then_inc(sbf[b], 1)
                if i == NT - 1:
                    ve.memset(colA[:], 0.0).then_inc(s_vs, 1)
            ve.memset(ones_b[:], 1.0)
            ve.wait_ge(s_gpio, 1)
            ve.wait_ge(s_sv, 1)
            ve.reduce_sum(colsB[:, 0:NT], expv[:].rearrange("p (t e) -> p t e", e=NE),
                          axis=AX.X)
            ve.reduce_max(colsB[:, NT:2 * NT],
                          expv[:].rearrange("p (t e) -> p t e", e=NE), axis=AX.X)
            ve.drain()
            ve.reciprocal(colsB[:, 2 * NT:3 * NT], colsB[:, 0:NT])
            for i in range(NT):
                ve.tensor_scalar_mul(scrA[:, i * NE:(i + 1) * NE],
                                     ones_b[:, 0:NE], colsB[:, NT + i:NT + i + 1])
            ve.drain()
            ve.tensor_tensor(scrB[:], expv[:], scrA[:], ALU.is_ge)
            ve.drain()
            ve.tensor_tensor(scrC[:], scrB[:], bmi_b[:], ALU.mult)
            ve.drain()
            ve.reduce_max(colsB[:, 3 * NT:4 * NT],
                          scrC[:].rearrange("p (t e) -> p t e", e=NE), axis=AX.X)
            ve.drain()
            ve.tensor_scalar(colsB[:, 3 * NT:4 * NT], colsB[:, 3 * NT:4 * NT],
                             -1.0, 128.0, ALU.mult, ALU.add)
            ve.drain()
            for i in range(NT):
                ve.tensor_scalar_mul(scrA[:, i * NE:(i + 1) * NE],
                                     ones_b[:, 0:NE],
                                     colsB[:, 3 * NT + i:3 * NT + i + 1])
            ve.drain()
            ve.tensor_tensor(ind_b[:].rearrange("p (t k e) -> p t k e", k=2, e=NE)
                             [:, :, 0, :],
                             iotaB[:].rearrange("p (t e) -> p t e", e=NE),
                             scrA[:].rearrange("p (t e) -> p t e", e=NE),
                             ALU.is_equal)
            ve.drain()
            ve.scalar_tensor_tensor(scrB[:],
                                    ind_b[:].rearrange("p (t k e) -> p (t k e)"),
                                    1.0, expv[:], ALU.mult, ALU.mult) \
                if False else None
            ve.scalar_tensor_tensor(
                scrB[:],
                ind_b[:].rearrange("p (t k e) -> p t k e", k=2, e=NE)[:, :, 0, :],
                BIGNEG, expv[:], ALU.mult, ALU.add)
            ve.drain()
            ve.reduce_max(colsB[:, 4 * NT:5 * NT],
                          scrB[:].rearrange("p (t e) -> p t e", e=NE), axis=AX.X)
            ve.drain()
            for i in range(NT):
                ve.tensor_scalar_mul(scrA[:, i * NE:(i + 1) * NE],
                                     ones_b[:, 0:NE],
                                     colsB[:, 4 * NT + i:4 * NT + i + 1])
            ve.drain()
            ve.tensor_tensor(scrC[:], scrB[:], scrA[:], ALU.is_ge)
            ve.drain()
            ve.tensor_tensor(scrC[:], scrC[:], bmi_b[:], ALU.mult)
            ve.drain()
            ve.reduce_max(colsB[:, 5 * NT:6 * NT],
                          scrC[:].rearrange("p (t e) -> p t e", e=NE), axis=AX.X)
            ve.drain()
            ve.tensor_scalar(colsB[:, 5 * NT:6 * NT], colsB[:, 5 * NT:6 * NT],
                             -1.0, 128.0, ALU.mult, ALU.add)
            ve.drain()
            for i in range(NT):
                ve.tensor_scalar_mul(scrA[:, i * NE:(i + 1) * NE],
                                     ones_b[:, 0:NE],
                                     colsB[:, 5 * NT + i:5 * NT + i + 1])
            ve.drain()
            ve.tensor_tensor(ind_b[:].rearrange("p (t k e) -> p t k e", k=2, e=NE)
                             [:, :, 1, :],
                             iotaB[:].rearrange("p (t e) -> p t e", e=NE),
                             scrA[:].rearrange("p (t e) -> p t e", e=NE),
                             ALU.is_equal)
            ve.tensor_tensor(colsB[:, 6 * NT:7 * NT], colsB[:, NT:2 * NT],
                             colsB[:, 2 * NT:3 * NT], ALU.mult)
            ve.tensor_tensor(colsB[:, 7 * NT:8 * NT], colsB[:, 4 * NT:5 * NT],
                             colsB[:, 2 * NT:3 * NT], ALU.mult)
            ve.drain()
            eview = eidx_sb[:].rearrange("p (t k) -> p t k", k=2)
            wview = wco_sb[:].rearrange("p (t k) -> p t k", k=2)
            ve.tensor_copy(eview[:, :, 0], colsB[:, 3 * NT:4 * NT])
            ve.tensor_copy(eview[:, :, 1], colsB[:, 5 * NT:6 * NT])
            ve.tensor_copy(wview[:, :, 0], colsB[:, 6 * NT:7 * NT])
            ve.tensor_copy(wview[:, :, 1], colsB[:, 7 * NT:8 * NT])
            ve.tensor_copy(indp[:], ind_b[:]).then_inc(s_ind, 1)
            for i in range(NT):
                (b, g), (b2, g2) = rec["rank"][i]
                ve.wait_ge(sgd[b], g)
                ve.tensor_sub(rank_sb[:, i * 16:(i + 1) * 16], bank[b][:, 0:16],
                              ind_b[:, i * 16:(i + 1) * 16]).then_inc(sbf[b], 1)
                ve.wait_ge(sgd[b2], g2)
                ve.drain()
                ve.tensor_add(off_sb[0:1, :], off_sb[0:1, :],
                              bank[b2][0:1, 0:16]).then_inc(s_off, 1)
            ve.wait_ge(s_cntb, 16)
            for i in range(NT):
                ve.tensor_copy(cnt_exp[:, i * NE:(i + 1) * NE], cnt_bc[:, 0:NE])
            ve.drain()
            indv = ind_b[:].rearrange("p (t k e) -> p t k e", k=2, e=NE)
            rkv = rank_sb[:].rearrange("p (t k e) -> p t k e", k=2, e=NE)
            cB = colsB
            ve.tensor_tensor(scrA[:].rearrange("p (t e) -> p t e", e=NE),
                             indv[:, :, 0, :], rkv[:, :, 0, :], ALU.mult)
            ve.tensor_tensor(scrB[:].rearrange("p (t e) -> p t e", e=NE),
                             indv[:, :, 1, :], rkv[:, :, 1, :], ALU.mult)
            ve.drain()
            ve.reduce_sum(cB[:, 0:NT], scrA[:].rearrange("p (t e) -> p t e", e=NE),
                          axis=AX.X)
            ve.reduce_sum(cB[:, NT:2 * NT],
                          scrB[:].rearrange("p (t e) -> p t e", e=NE), axis=AX.X)
            ve.drain()
            ve.tensor_tensor(scrA[:].rearrange("p (t e) -> p t e", e=NE),
                             indv[:, :, 1, :],
                             cnt_exp[:].rearrange("p (t e) -> p t e", e=NE), ALU.mult)
            ve.drain()
            ve.reduce_sum(cB[:, 2 * NT:3 * NT],
                          scrA[:].rearrange("p (t e) -> p t e", e=NE), axis=AX.X)
            ve.drain()
            ve.tensor_add(cB[:, 2 * NT:3 * NT], cB[:, 2 * NT:3 * NT],
                          cB[:, NT:2 * NT])
            ve.drain()
            rkst = rk_sb[:].rearrange("p (t k) -> p t k", k=2)
            sctst = sct_sb[:].rearrange("p (t k) -> p t k", k=2)
            rslst = rsl_sb[:].rearrange("p (t k) -> p t k", k=2)
            for kk in range(2):
                slot = cB[:, (0 if kk == 0 else 2) * NT:(1 if kk == 0 else 3) * NT]
                rkl = cB[:, (0 if kk == 0 else 1) * NT:(1 if kk == 0 else 2) * NT]
                ecol = cB[:, (3 if kk == 0 else 5) * NT:(4 if kk == 0 else 6) * NT]
                ovf = cB[:, 8 * NT:9 * NT]
                t1 = cB[:, 9 * NT:10 * NT]
                t2 = cB[:, 10 * NT:11 * NT]
                ve.tensor_scalar(ovf, slot, float(C), None, ALU.is_ge)
                ve.drain()
                ve.scalar_tensor_tensor(t1, ovf, 1.0e9, rkl, ALU.mult, ALU.add)
                ve.drain()
                ve.tensor_copy(rkst[:, :, kk], t1)
                ve.scalar_tensor_tensor(t1, ecol, float(C), slot, ALU.mult, ALU.add)
                ve.tensor_scalar_mul(t2, ones_b[:, 0:NT], dumpc[:])
                ve.drain()
                ve.tensor_sub(t2, t2, t1)
                ve.drain()
                ve.tensor_tensor(t2, t2, ovf, ALU.mult)
                ve.drain()
                ve.tensor_add(t2, t2, t1)
                ve.drain()
                ve.tensor_copy(sctst[:, :, kk], t2)
                ve.tensor_scalar_min(t1, slot, float(C - 1))
                ve.drain()
                ve.scalar_tensor_tensor(t1, ecol, float(C), t1, ALU.mult, ALU.add)
                ve.drain()
                mcp = ve.tensor_copy(rslst[:, :, kk], t1)
            mcp.then_inc(s_slot, 1)
            ve.memset(avacc[:], 0.0)
            for j in range(NT):
                if j >= 1:
                    ve.wait_ge(s_exp, j)
                for ic in range(NTN):
                    b, g = rec["attqk"][j * NTN + ic]
                    ve.wait_ge(sgd[b], g)
                    ve.tensor_scalar(att16[:, ic * 512:(ic + 1) * 512],
                                     bank[b][:, :], 0.0, None, ALU.add, ALU.max,
                                     accum_out=cmax[:, ic:ic + 1]).then_inc(sbf[b], 1)
                ve.drain()
                ve.reduce_max(negmax[:], cmax[:, 0:NTN], axis=AX.X, negate=True) \
                    .then_inc(s_sm, 1)
                ve.wait_ge(s_exp, j + 1)
                ve.reciprocal(vrec[:], den[:])
                ve.drain()
                if j >= 1:
                    ab, ag = rec["attav"][(j - 1) * NTN + NTN - 1]
                    ve.wait_ge(sgd[ab], ag)
                ve.tensor_scalar_mul(vn_sb[:], v_sb[:, j * HD:(j + 1) * HD], vrec[:]) \
                    .then_inc(s_vn, 1)
                for ic in range(NTN):
                    b, g = rec["attav"][j * NTN + ic]
                    ve.wait_ge(sgd[b], g)
                    ve.tensor_add(avacc[0:HD, ic * 512:(ic + 1) * 512],
                                  avacc[0:HD, ic * 512:(ic + 1) * 512],
                                  bank[b][0:HD, 0:512]).then_inc(sbf[b], 1)
            for i in range(NT):
                b, g = rec["stp"][i]
                ve.wait_ge(sgd[b], g)
                ve.tensor_copy(s_sb[:, i * HD:(i + 1) * HD],
                               bank[b][0:128, 0:HD]).then_inc(sbf[b], 1)
            ve.wait_ge(sgd[6], rec["base_g"])
            ve.tensor_copy(base_row[:], bank[6][0:1, 0:16]).then_inc(s_basb, 1)
            ve.wait_ge(s_basb, 17)
            indpv = indp[:].rearrange("p (t k e) -> p t k e", k=2, e=NE)
            for kk in range(2):
                gsel = p6cols[:, 0:NT]
                for i in range(NT):
                    ve.tensor_copy(p6A[:, i * NE:(i + 1) * NE],
                                   base_bc[:, kk * NE:(kk + 1) * NE])
                ve.drain()
                ve.tensor_tensor(p6B[:].rearrange("p (t e) -> p t e", e=NE),
                                 indpv[:, :, kk, :],
                                 p6A[:].rearrange("p (t e) -> p t e", e=NE), ALU.mult)
                ve.drain()
                ve.reduce_sum(gsel, p6B[:].rearrange("p (t e) -> p t e", e=NE),
                              axis=AX.X)
                ve.drain()
                ve.tensor_add(gsel, gsel,
                              rk_sb[:].rearrange("p (t k) -> p t k", k=2)[:, :, kk])
                ve.drain()
                ve.tensor_scalar(gsel, gsel, float(CAPACITY), None, ALU.is_lt)
                ve.drain()
                ve.tensor_tensor(gsel, gsel,
                                 wco_sb[:].rearrange("p (t k) -> p t k", k=2)[:, :, kk],
                                 ALU.mult)
                ve.drain()
                ve.tensor_copy(coef_b[:].rearrange("p (t k) -> p t k", k=2)[:, :, kk],
                               gsel)
            ve.drain()
            yn = 0
            for b, npass_, gcount in rec["l2"]:
                yn += 1
                if yn > 4:
                    ve.wait_ge(s_yewb[(yn - 4 - 1) % 4], 16 * ((yn - 4 - 1) // 4 + 1))
                ve.wait_ge(s_l2g, npass_)
                ve.tensor_copy(ye_sb[yn % 4][:, :], bank[b][:, :]).then_inc(sbf[b], 1)
            cview = coef_b[:].rearrange("p (t k) -> p t k", k=2)
            for i in range(NT):
                ve.wait_ge(s_ygtb[(2 * i) % 4], 16 * ((2 * i) // 4 + 1))
                ve.wait_ge(s_ygtb[(2 * i + 1) % 4], 16 * ((2 * i + 1) // 4 + 1))
                ve.wait_ge(s_xcb[i % 2], 16 * (i // 2 + 1))
                if i >= 1:
                    ve.wait_ge(s_gsil, i)
                ve.scalar_tensor_tensor(g_sb[:], yg_sb[(2 * i) % 4][:],
                                        coef_b[:, 2 * i:2 * i + 1],
                                        bias_bc[:], ALU.mult, ALU.add)
                ve.drain()
                ve.scalar_tensor_tensor(g_sb[:], yg_sb[(2 * i + 1) % 4][:],
                                        coef_b[:, 2 * i + 1:2 * i + 2],
                                        g_sb[:], ALU.mult, ALU.add).then_inc(s_comb, 1)
                ve.wait_ge(s_gsil, i + 1)
                ob = o_sb[i % 2]
                for c in range(D // HD):
                    ve.tensor_mul(ob[:, c * HD:(c + 1) * HD],
                                  ob[:, c * HD:(c + 1) * HD],
                                  s_sb[:, i * HD:(i + 1) * HD])
                ve.drain()
                ve.tensor_add(ob[:], ob[:], xc_sb[i % 2][:]).then_inc(s_out, 1)

        @block.scalar
        def _(sc):
            sc.wait_ge(s_zt, 1)
            for z in range(NZF):
                r0 = z * 128
                r1 = min(r0 + 128, ZROWS)
                sc.dma_start(out=xe_d[r0:r1, :], in_=zt_sb[0:r1 - r0, :]) \
                    .then_inc(s_zf, 16)
            for i in range(NT):
                sc.wait_ge(s_stat, i + 1)
                if i >= 2:
                    sc.wait_ge(s_xnwb[i % 2], 16 * (i // 2))
                pc = i % 2
                sc.activation(statq[:, pc:pc + 1], statv[:, pc:pc + 1], AF.Sqrt,
                              bias=0.0, scale=1.0).then_inc(s_sq, 1)
                sc.wait_ge(s_rst, i + 1)
                sc.activation(xnt_sb[i % 2][:], xt_sb[i % 3][:], AF.Identity,
                              bias=statnb[:, pc:pc + 1],
                              scale=statr[:, pc:pc + 1]).then_inc(s_xn, 1)
            sc.wait_ge(s_vs, 1)
            sc.activation(expv[:], logit_sb[:], AF.Exp).then_inc(s_sv, 1)
            for j in range(NT):
                sc.wait_ge(s_sm, j + 1)
                if j >= 2:
                    ab, ag = rec["attav"][(j - 2) * NTN + NTN - 1]
                    sc.wait_ge(sgd[ab], ag)
                sc.activation(e_sb[j % 2][:], att16[:], AF.Exp,
                              bias=negmax[:], scale=1.0,
                              accum_out=den[:]).then_inc(s_exp, 1)
            for e in range(NE):
                for m_ in range(FC):
                    off = 0
                    for ni, nsz in enumerate(NXC):
                        b, g, _ = rec["l1"][(e, m_, ni)]
                        sc.wait_ge(sgd[b], g)
                        sc.activation(h_sb[:, m_ * C + off: m_ * C + off + nsz],
                                      bank[b][:, 0:nsz], AF.Gelu_apprx_tanh) \
                            .then_inc(sbf[b], 1)
                        off += nsz
            for i in range(NT):
                sc.wait_ge(s_comb, i + 1)
                if i >= 2:
                    sc.wait_ge(s_owb[i % 2], 16 * ((i - 2) // 2 + 1))
                sc.activation(o_sb[i % 2][:], g_sb[:], AF.Silu).then_inc(s_gsil, 1)

        @block.gpsimd
        def _(gp):
            gp.iota(iota8[:], [[1, NE]], base=0, channel_multiplier=0,
                    allow_small_or_imprecise_dtypes=True)
            gp.iota(iotap[:], [[0, 1]], base=0, channel_multiplier=1,
                    allow_small_or_imprecise_dtypes=True)
            gp.iota(bmi[:], [[-1, NE]], base=128, channel_multiplier=0,
                    allow_small_or_imprecise_dtypes=True)
            gp.dma_start(out=wq_sb[:].rearrange("p (k n) -> p k n", k=KD),
                         in_=wq_d).then_inc(s_wc, 16)
            gp.dma_start(out=wk_sb[:].rearrange("p (k n) -> p k n", k=KD),
                         in_=wk_d).then_inc(s_wc, 16)
            gp.dma_start(out=wvr_sb[:].rearrange("p (k n) -> p k n", k=KD),
                         in_=wvr_d).then_inc(s_wc, 16)
            gp.wait_ge(s_xn, NT)
            gp.iota(iotaB[:], [[0, NT], [1, NE]], base=0, channel_multiplier=0,
                    allow_small_or_imprecise_dtypes=True)
            gp.iota(bmi_b[:], [[0, NT], [-1, NE]], base=128, channel_multiplier=0,
                    allow_small_or_imprecise_dtypes=True) if False else None
            gp.iota(bmi_b[:], [[0, NT], [-1, NE]], base=128,
                    channel_multiplier=0,
                    allow_small_or_imprecise_dtypes=True).then_inc(s_gpio, 1)
            gp.wait_ge(s_off, NT)
            gp.dma_start(out=ccin_d, in_=off_sb[0:1, :]).then_inc(s_ccw, 16)
            gp.wait_ge(s_ccw, 16)
            gp.collective_compute(
                "AllGather", ALU.bypass, replica_groups=[list(range(n_cores))],
                ins=[ccin_d.opt()], outs=[ccout_d.opt()],
            ).then_inc(s_cc, 1)
            gp.wait_ge(s_cc, 1)
            gp.dma_start(out=ccall[:], in_=ccout_d).then_inc(s_ccb, 16)
            gp.dma_start(out=prefm[:], in_=prefm_d).then_inc(s_ccb, 16)
            gp.wait_ge(s_basb, 1)
            gp.dma_start(out=ccbase_d, in_=base_row[:]).then_inc(s_ccb, 16)
            for i in range(NT):
                gp.wait_ge(s_xwb[i % 4], 16 * (i // 4 + 1))
                gp.wait_ge(s_slot, 1)
                if i == 0:
                    gp.wait_ge(s_zf, 16 * NZF)
                for kk in range(2):
                    gp.indirect_dma_start(
                        xe_d,
                        IndirectOffsetOnAxis(ap=sct_sb[:, 2 * i + kk:2 * i + kk + 1],
                                             axis=0),
                        xw_sb[i % 4][:], None,
                    ).then_inc(s_scatb[i % 4], 16)
            w1i = 0
            w2i = 0
            for e in range(NE):
                for m_ in range(FC):
                    w1i += 1
                    if w1i > 2:
                        jb = w1i - 2
                        eb, mb = (jb - 1) // FC, (jb - 1) % FC
                        wb, wg, _ = rec["l1"][(eb, mb, NNX - 1)]
                        gp.wait_ge(sgd[wb], wg)
                    gp.dma_start(out=w1b_sb[(w1i - 1) % NW1BUF][:].rearrange(
                        "p (k f) -> p k f", k=KD),
                        in_=w1_d[e, m_]).then_inc(s_w1b[(w1i - 1) % NW1BUF], 16)
                for half in range(2):
                    for grp in mgroups():
                        for kb in range(NW2B):
                            w2i += 1
                            if w2i > NW2BUF:
                                kind, val = rec["w2done"][w2i - 1 - NW2BUF]
                                gp.wait_ge(s_w2c if kind == "c" else s_l2g, val)
                            gp.dma_start(
                                out=w2b_sb[(w2i - 1) % NW2BUF][:].rearrange(
                                    "p (k j) -> p k j", k=W2KG),
                                in_=w2_d[e, kb * W2KG:(kb + 1) * W2KG, half]
                                .rearrange("k p j -> p k j"),
                            ).then_inc(s_w2b[(w2i - 1) % NW2BUF], 16)
            nyes = NE * 2 * MC
            for p in range(4):
                tot = nyes // 4 + (1 if p < nyes % 4 else 0)
                gp.wait_ge(s_yewb[p], 16 * tot)
            for i in range(NT):
                for kk in range(2):
                    n = 2 * i + kk
                    if n >= 4:
                        gp.wait_ge(s_comb, (n - 4) // 2 + 1)
                    gp.indirect_dma_start(
                        yg_sb[n % 4][:], None,
                        ye_d,
                        IndirectOffsetOnAxis(ap=rsl_sb[:, n:n + 1], axis=0),
                    ).then_inc(s_ygtb[n % 4], 16)

        @block.sync
        def _(sy):
            sy.dma_start(out=bias_bc[:],
                         in_=bias_d.broadcast_to([128, D])).then_inc(s_cl, 16)
            sy.dma_start(out=identf[:], in_=identf_d).then_inc(s_cl, 16)
            sy.dma_start(out=lincl[:], in_=lincl_d).then_inc(s_cl, 16)
            sy.dma_start(out=sel127[:], in_=sel127_d).then_inc(s_cl, 16)
            sy.dma_start(out=onescol[:], in_=onescol_d).then_inc(s_cl, 16)
            for i in range(NT):
                if i >= 3:
                    sy.wait_ge(s_xn, i - 2)
                sy.dma_start(out=xt_sb[i % 3][:],
                             in_=x_d[i * 128:(i + 1) * 128, :]).then_inc(s_xb[i % 3], 16)
                if i >= 1:
                    sy.wait_ge(s_xn, i)
                    sy.dma_start(out=xn_d[(i - 1) * 128:i * 128, :],
                                 in_=xnt_sb[(i - 1) % 2][:]) \
                        .then_inc(s_xnwb[(i - 1) % 2], 16)
            sy.wait_ge(s_xn, NT)
            sy.dma_start(out=xn_d[(NT - 1) * 128:NT * 128, :],
                         in_=xnt_sb[(NT - 1) % 2][:]) \
                .then_inc(s_xnwb[(NT - 1) % 2], 16)
            sy.wait_ge(s_xnwb[0], 16 * ((NT + 1) // 2))
            sy.wait_ge(s_xnwb[1], 16 * (NT // 2))
            for c in range(KD):
                sy.dma_start(out=xnfm[:, c * S:(c + 1) * S],
                             in_=xn_d[:, c * 128:(c + 1) * 128],
                             transpose=True).then_inc(s_xnfm, 16)
            if debug_taps:
                sy.wait_ge(s_xnfm, 16 * KD)
                sy.dma_start(out=dbg["dbg_xnfm"], in_=xnfm[:]).then_inc(s_cl, 16)
                sy.wait_ge(s_qkd, 1)
                sy.dma_start(out=dbg["dbg_qT"], in_=qT[:]).then_inc(s_cl, 16)
                sy.dma_start(out=dbg["dbg_kT"], in_=kT[:]).then_inc(s_cl, 16)
            sy.wait_ge(s_ccw, 16)
            sy.dma_start(out=cnt_bc[:],
                         in_=ccin_d.broadcast_to([128, 16])).then_inc(s_cntb, 16)
            sy.wait_ge(s_ccb, 16 * 3)
            sy.dma_start(out=base_bc[:],
                         in_=ccbase_d.broadcast_to([128, 16])).then_inc(s_basb, 16)
            for i in range(NT):
                if i >= 4:
                    sy.wait_ge(s_scatb[i % 4], 32 * (i // 4))
                sy.dma_start(out=xw_sb[i % 4][:],
                             in_=xn_d[i * 128:(i + 1) * 128, :]) \
                    .then_inc(s_xwb[i % 4], 16)
            def emit_ye_stores(e, yewn):
                for half in range(2):
                    for grp in mgroups():
                        for m_ in grp:
                            yewn += 1
                            yb, ynp, ygc = rec["l2"][yewn - 1]
                            sy.wait_ge(sbf[yb], ygc)
                            sy.dma_start(
                                out=ye_d[e * C + m_ * 128: e * C + (m_ + 1) * 128,
                                         half * 512:(half + 1) * 512],
                                in_=ye_sb[yewn % 4][:, :]) \
                                .then_inc(s_yewb[(yewn - 1) % 4], 16)
                return yewn

            yewn = 0
            for e in range(NE):
                if e == 0:
                    for p in range(4):
                        tot = 2 * (NT // 4 + (1 if p < NT % 4 else 0))
                        sy.wait_ge(s_scatb[p], 16 * tot)
                if e == 1:
                    qb, qg = rec["attqk"][-1]
                    sy.wait_ge(sgd[qb], qg)
                if e >= 2:
                    wb, wg, _ = rec["l1"][(e - 2, FC - 1, NNX - 1)]
                    sy.wait_ge(sgd[wb], wg)
                for k in range(KD):
                    sy.dma_start(out=xe_sb[e % 2][:, k * C:(k + 1) * C],
                                 in_=xe_d[e * C:(e + 1) * C, k * 128:(k + 1) * 128],
                                 transpose=True).then_inc(s_xeb[e % 2], 16)
                if e >= 1:
                    yewn = emit_ye_stores(e - 1, yewn)
            yewn = emit_ye_stores(NE - 1, yewn)
            for i in range(NT):
                if i >= 2:
                    sy.wait_ge(s_out, i - 1)
                sy.dma_start(out=xc_sb[i % 2][:],
                             in_=x_d[i * 128:(i + 1) * 128, :]) \
                    .then_inc(s_xcb[i % 2], 16)
                if i >= 1:
                    sy.wait_ge(s_out, i)
                    sy.dma_start(out=out_d[(i - 1) * 128:i * 128, :],
                                 in_=o_sb[(i - 1) % 2][:]) \
                        .then_inc(s_owb[(i - 1) % 2], 16)
            sy.wait_ge(s_out, NT)
            sy.dma_start(out=out_d[(NT - 1) * 128:NT * 128, :],
                         in_=o_sb[(NT - 1) % 2][:]).then_inc(s_owb[(NT - 1) % 2], 16)
            sy.wait_ge(s_owb[0], 16 * ((NT + 1) // 2))
            sy.wait_ge(s_owb[1], 16 * (NT // 2))
            if debug_taps:
                ndbg = 0
                sy.dma_start(out=dbg["dbg_xnd"], in_=xn_d).then_inc(s_cl, 16)
                for name, src_sb in (("dbg_s", s_sb), ("dbg_logit", logit_sb),
                                     ("dbg_wco", wco_sb), ("dbg_eidx", eidx_sb),
                                     ("dbg_rk", rk_sb), ("dbg_rsl", rsl_sb),
                                     ("dbg_sct", sct_sb), ("dbg_base", base_bc),
                                     ("dbg_rank", rank_sb)):
                    sy.dma_start(out=dbg[name], in_=src_sb[:]).then_inc(s_cl, 16)
                    ndbg += 1
                sy.wait_ge(s_cl, 16 * 5 + 16 * (ndbg + 4))

    es.close()
    return nc


def _host_layouts(inputs):
    x = np.ascontiguousarray(inputs["L0_attn_x"])
    wq = np.ascontiguousarray(inputs["L0_attn_Wq"])
    wk = np.ascontiguousarray(inputs["L0_attn_Wk"])
    wv = np.ascontiguousarray(inputs["L0_attn_Wv"])
    wr = np.ascontiguousarray(inputs["L0_ffn_router_w"])
    w1 = np.ascontiguousarray(inputs["L0_ffn_w1"])
    w2 = np.ascontiguousarray(inputs["L0_ffn_w2"])
    bias = np.ascontiguousarray(inputs["L0_ffn_bias"]).reshape(1, D)
    S = x.shape[1]
    F = w1.shape[2]
    FC = F // 128

    def chunk_pk(w, ncols):
        return np.ascontiguousarray(w.reshape(KD, 128, ncols).transpose(1, 0, 2))

    wvr = np.concatenate([wv, wr], axis=1)
    w1h = np.ascontiguousarray(
        w1.reshape(NE, KD, 128, FC, 128).transpose(0, 3, 2, 1, 4))
    w2h = np.ascontiguousarray(
        w2.reshape(NE, FC, 128, 2, 512).transpose(0, 1, 3, 2, 4))
    identf = np.eye(128, dtype=np.float32)
    lincl = np.triu(np.ones((128, 128), np.float32))
    sel127 = np.zeros((128, 128), np.float32)
    sel127[0, :] = 1.0
    onescol = np.ones((128, 1), np.float32)
    common = dict(
        wqh=chunk_pk(wq, HD), wkh=chunk_pk(wk, HD), wvrh=chunk_pk(wvr, 72),
        w1h=w1h, w2h=w2h, bias1=bias,
        identf=identf, lincl=lincl, sel127=sel127, onescol=onescol,
    )
    in_maps = []
    for c in range(x.shape[0]):
        prefm = np.zeros((NE, 1), np.float32)
        prefm[:c, 0] = 1.0
        in_maps.append(dict(common, x=x[c], prefm=prefm))
    return in_maps, S, F


def kernel(**inputs):
    in_maps, S, F = _host_layouts(inputs)
    nc = build_program(S=S, F=F, C=1152, n_cores=8)
    res = run_bass_kernel_spmd(nc, in_maps, core_ids=list(range(8)))
    out = np.stack([r["out"] for r in res.results], axis=0)
    return out.astype(np.float32)


# version 10
# speedup vs baseline: 1.2586x; 1.0708x over previous
import numpy as np
import concourse.bass as bass
import concourse.mybir as mybir
from concourse.bass import IndirectOffsetOnAxis
from concourse.bass_utils import run_bass_kernel_spmd

F32 = mybir.dt.float32
BF16 = mybir.dt.bfloat16
F16 = mybir.dt.float16
U32 = mybir.dt.uint32
AF = mybir.ActivationFunctionType
ALU = mybir.AluOpType
AX = mybir.AxisListType

D = 1024
KD = D // 128
HD = 64
NE = 8
EPS = 1e-5
BIGNEG = -1.0e30


def build_program(S=4096, F=4096, C=1152, n_cores=8, debug_taps=False):
    NT = S // 128
    NTN = S // 512
    FC = F // 128
    MC = C // 128
    NXC = [512] * (C // 512) + ([C % 512] if C % 512 else [])
    NNX = len(NXC)
    W2KG = min(4, FC)
    NW2B = (FC + W2KG - 1) // W2KG
    NW2BUF = 3
    NW1BUF = 3
    CAPACITY = S * n_cores // NE
    assert S % 512 == 0 and C % 128 == 0 and F % (128 * W2KG) == 0

    def mgroups():
        if MC <= 8:
            return [list(range(MC))]
        h1 = (MC + 1) // 2
        return [list(range(h1)), list(range(h1, MC))]

    nc = bass.Bass("TRN2", target_bir_lowering=False, debug=False,
                   detect_race_conditions=False)

    x_d = nc.dram_tensor("x", [S, D], F32, kind="ExternalInput").ap()
    wq_d = nc.dram_tensor("wqh", [128, KD, HD], F32, kind="ExternalInput").ap()
    wk_d = nc.dram_tensor("wkh", [128, KD, HD], F32, kind="ExternalInput").ap()
    wvr_d = nc.dram_tensor("wvrh", [128, KD, 72], F32, kind="ExternalInput").ap()
    w1_d = nc.dram_tensor("w1h", [NE, FC, 128, KD, 128], F32, kind="ExternalInput").ap()
    w2_d = nc.dram_tensor("w2h", [NE, FC, 2, 128, 512], F32, kind="ExternalInput").ap()
    bias_d = nc.dram_tensor("bias1", [1, D], F32, kind="ExternalInput").ap()
    identf_d = nc.dram_tensor("identf", [128, 128], F32, kind="ExternalInput").ap()
    lincl_d = nc.dram_tensor("lincl", [128, 128], F32, kind="ExternalInput").ap()
    sel127_d = nc.dram_tensor("sel127", [128, 128], F32, kind="ExternalInput").ap()
    onescol_d = nc.dram_tensor("onescol", [128, 1], F32, kind="ExternalInput").ap()
    prefm_d = nc.dram_tensor("prefm", [NE, 1], F32, kind="ExternalInput").ap()
    out_d = nc.dram_tensor("out", [S, D], F32, kind="ExternalOutput").ap()

    xn_d = nc.dram_tensor("xn_dram", [S, D], BF16).ap()
    ZROWS = NE * C + 128
    xe_d = nc.dram_tensor("xe_dram", [ZROWS, D], BF16).ap()
    ye_d = nc.dram_tensor("ye_dram", [NE * C, D], BF16).ap()
    ccin_d = nc.dram_tensor("cc_in", [1, 16], F32).ap()
    ccout_d = nc.dram_tensor("cc_out", [n_cores, 16], F32).ap()
    ccbase_d = nc.dram_tensor("cc_base", [1, 16], F32).ap()
    if debug_taps:
        NTt = S // 128
        dbg = {
            "dbg_s": nc.dram_tensor("dbg_s", [128, NTt * HD], F32, kind="ExternalOutput").ap(),
            "dbg_logit": nc.dram_tensor("dbg_logit", [128, NTt * NE], F32, kind="ExternalOutput").ap(),
            "dbg_wco": nc.dram_tensor("dbg_wco", [128, NTt * 2], F32, kind="ExternalOutput").ap(),
            "dbg_eidx": nc.dram_tensor("dbg_eidx", [128, NTt * 2], F32, kind="ExternalOutput").ap(),
            "dbg_rk": nc.dram_tensor("dbg_rk", [128, NTt * 2], F32, kind="ExternalOutput").ap(),
            "dbg_rsl": nc.dram_tensor("dbg_rsl", [128, NTt * 2], U32, kind="ExternalOutput").ap(),
            "dbg_sct": nc.dram_tensor("dbg_sct", [128, NTt * 2], U32, kind="ExternalOutput").ap(),
            "dbg_base": nc.dram_tensor("dbg_base", [128, 16], F32, kind="ExternalOutput").ap(),
            "dbg_rank": nc.dram_tensor("dbg_rank", [128, NTt * 16], F32, kind="ExternalOutput").ap(),
            "dbg_xnd": nc.dram_tensor("dbg_xnd", [S, D], BF16, kind="ExternalOutput").ap(),
            "dbg_xnfm": nc.dram_tensor("dbg_xnfm", [128, KD * S], BF16, kind="ExternalOutput").ap(),
            "dbg_qT": nc.dram_tensor("dbg_qT", [64, S], BF16, kind="ExternalOutput").ap(),
            "dbg_kT": nc.dram_tensor("dbg_kT", [64, S], BF16, kind="ExternalOutput").ap(),
        }

    def sbuf(name, shape, dtype):
        return nc.alloc_sbuf_tensor(name, shape, dtype)

    def nbytes(shape, dtype):
        return shape[1] * mybir.dt.size(dtype)

    h_sb = sbuf("h_sb", [128, FC * C], BF16)
    h_addr = nc.lookup_mloc(h_sb).addr
    if KD * S <= FC * C:
        xnfm = nc.alloc_sbuf_tensor_at("xn_fm", [128, KD * S], BF16, offset=h_addr)
    else:
        xnfm = sbuf("xn_fm", [128, KD * S], BF16)
    if KD * S * 2 + S * 2 <= FC * C * 2:
        qT = nc.alloc_sbuf_tensor_at("qT", [64, S], BF16,
                                     offset=h_addr + KD * S * 2)
    else:
        qT = sbuf("qT", [64, S], BF16)
    xe_sb = [sbuf(f"xe{i}", [128, KD * C], BF16) for i in range(2)]
    if S * 2 <= KD * C * 2:
        kT = nc.alloc_sbuf_tensor_at("kT", [64, S], BF16,
                                     offset=nc.lookup_mloc(xe_sb[1]).addr)
    else:
        kT = sbuf("kT", [64, S], BF16)

    R1_P1 = max(3 * 4096 + 2 * 2048 + 4096,
                10 * (S // 128) * NE * 4 + (S // 128) * 14 * 4)
    R1_P4 = 64 * 0 + 4 * S + 2 * S + 2 * 2 * S
    R1_P6 = 4 * 2048 + 5 * 4096 + 2 * (S // 128) * NE * 4 + (S // 128) * 16
    R1 = max(R1_P1, R1_P4, R1_P6)
    r1_sb = sbuf("r1_arena", [128, R1], mybir.dt.uint8)
    r1_addr = nc.lookup_mloc(r1_sb).addr

    def at(name, shape, dtype, off):
        assert off + nbytes(shape, dtype) <= R1, (name, off)
        return nc.alloc_sbuf_tensor_at(name, shape, dtype, offset=r1_addr + off)

    xt_sb = [at(f"xt{i}", [128, D], F32, i * 4096) for i in range(3)]
    NT_ = S // 128
    expv = at("expv", [128, NT_ * NE], F32, 0)
    scrA = at("scrA", [128, NT_ * NE], F32, NT_ * NE * 4)
    scrB = at("scrB", [128, NT_ * NE], F32, 2 * NT_ * NE * 4)
    scrC = at("scrC", [128, NT_ * NE], F32, 3 * NT_ * NE * 4)
    ind_b = at("ind_b", [128, NT_ * 16], F32, 4 * NT_ * NE * 4)
    iotaB = at("iotaB", [128, NT_ * NE], F32, 6 * NT_ * NE * 4)
    bmi_b = at("bmi_b", [128, NT_ * NE], F32, 7 * NT_ * NE * 4)
    cnt_exp = at("cnt_exp", [128, NT_ * NE], F32, 8 * NT_ * NE * 4)
    ones_b = at("ones_b", [128, NT_ * NE], F32, 9 * NT_ * NE * 4)
    colsB = at("colsB", [128, NT_ * 14], F32, 10 * NT_ * NE * 4)
    p6A = at("p6A", [128, NT_ * NE], F32, 4 * 2048 + 5 * 4096)
    p6B = at("p6B", [128, NT_ * NE], F32, 4 * 2048 + 5 * 4096 + NT_ * NE * 4)
    p6cols = at("p6cols", [128, NT_ * 4], F32, 4 * 2048 + 5 * 4096 + 2 * NT_ * NE * 4)
    coef_b = sbuf("coef_b", [128, NT_ * 2], F32)
    indp = sbuf("indp", [128, NT_ * 16], BF16)
    xnt_sb = [at(f"xnt{i}", [128, D], BF16, 3 * 4096 + i * 2048) for i in range(2)]
    scr_sb = at("scr_sb", [128, D], F32, 3 * 4096 + 2 * 2048)
    avacc = at("avacc", [64, S], F32, 0)
    att16 = at("att16", [128, S], F16, 4 * S)
    e_sb = [at(f"esb{i}", [128, S], BF16, 4 * S + 2 * S + i * 2 * S)
            for i in range(2)]
    yg_sb = [at(f"yg{i}", [128, D], BF16, i * 2048) for i in range(4)]
    xc_sb = [at(f"xc{i}", [128, D], F32, 4 * 2048 + i * 4096) for i in range(2)]
    o_sb = [at(f"osb{i}", [128, D], F32, 4 * 2048 + 2 * 4096 + i * 4096)
            for i in range(2)]
    g_sb = at("g_sb", [128, D], F32, 4 * 2048 + 4 * 4096)

    v_sb = sbuf("v_sb", [128, NT * HD], BF16)
    s_sb = sbuf("s_sb", [128, NT * HD], F32)
    logit_sb = sbuf("logit_sb", [128, NT * NE], F32)
    rank_sb = sbuf("rank_sb", [128, NT * 16], F32)
    rk_sb = sbuf("rk_sb", [128, NT * 2], F32)
    eidx_sb = sbuf("eidx_sb", [128, NT * 2], F32)
    wco_sb = sbuf("wco_sb", [128, NT * 2], F32)
    sct_sb = sbuf("sct_sb", [128, NT * 2], U32)
    rsl_sb = sbuf("rsl_sb", [128, NT * 2], U32)
    ind_sb = [sbuf(f"ind{i}", [128, 16], F32) for i in range(2)]
    off_sb = sbuf("off_sb", [128, 16], F32)
    cnt_bc = sbuf("cnt_bc", [128, 16], F32)
    base_bc = sbuf("base_bc", [128, 16], F32)
    ccall = sbuf("ccall", [NE, 16], F32)
    prefm = sbuf("prefm_sb", [NE, 1], F32)
    xw_sb = [sbuf(f"xw{i}", [128, D], BF16) for i in range(4)]
    w1b_sb = [sbuf(f"w1b{i}", [128, KD * 128], BF16) for i in range(NW1BUF)]
    w2b_sb = [sbuf(f"w2b{i}", [128, W2KG * 512], BF16) for i in range(NW2BUF)]
    wq_sb = sbuf("wq_sb", [128, KD * HD], BF16)
    wk_sb = sbuf("wk_sb", [128, KD * HD], BF16)
    wvr_sb = sbuf("wvr_sb", [128, KD * 72], BF16)
    ye_sb = [sbuf(f"yesb{i}", [128, 512], BF16) for i in range(4)]
    bias_bc = sbuf("bias_bc", [128, D], F32)
    zt_sb = sbuf("zt_sb", [128, D], BF16)
    identf = sbuf("identf_sb", [128, 128], F32)
    lincl = sbuf("lincl_sb", [128, 128], F32)
    sel127 = sbuf("sel127_sb", [128, 128], F32)
    onescol = sbuf("onescol_sb", [128, 1], F32)
    iota8 = sbuf("iota8", [128, NE], F32)
    iotap = sbuf("iotap", [128, 1], F32)
    bmi = sbuf("bmi", [128, NE], F32)
    st1 = sbuf("st1", [128, NE], F32)
    st2 = sbuf("st2", [128, NE], F32)
    st3 = sbuf("st3", [128, NE], F32)
    st4 = sbuf("st4", [128, NE], F32)
    cmax = sbuf("cmax", [128, 16], F32)
    statm = sbuf("statm", [128, 2], F32)
    statv = sbuf("statv", [128, 2], F32)
    statq = sbuf("statq", [128, 2], F32)
    statr = sbuf("statr", [128, 2], F32)
    statnb = sbuf("statnb", [128, 2], F32)
    colA = sbuf("colA", [128, 1], F32)
    colB = sbuf("colB", [128, 1], F32)
    colC = sbuf("colC", [128, 1], F32)
    colD = sbuf("colD", [128, 1], F32)
    colE = sbuf("colE", [128, 1], F32)
    colF = sbuf("colF", [128, 1], F32)
    colG = sbuf("colG", [128, 1], F32)
    colH = sbuf("colH", [128, 1], F32)
    c0col = sbuf("c0col", [128, 1], F32)
    dumpc = sbuf("dumpc", [128, 1], F32)
    c1col = sbuf("c1col", [128, 1], F32)
    negmax = sbuf("negmax", [128, 1], F32)
    den = sbuf("den", [128, 1], F32)
    vrec = sbuf("vrec", [128, 1], F32)
    vn_sb = sbuf("vn_sb", [128, HD], BF16)
    base_row = sbuf("base_row", [1, 16], F32)

    from contextlib import ExitStack
    es = ExitStack()
    bank = [es.enter_context(nc.psum_tensor(f"bank{i}", [128, 512], F32))
            for i in range(8)]

    def sem(name):
        return es.enter_context(nc.semaphore(name))

    s_xb = [sem(f"s_xb{i}") for i in range(3)]
    s_xnwb = [sem(f"s_xnwb{i}") for i in range(2)]
    s_xwb = [sem(f"s_xwb{i}") for i in range(4)]
    s_scatb = [sem(f"s_scatb{i}") for i in range(4)]
    s_xeb = [sem(f"s_xeb{i}") for i in range(2)]
    s_w1b = [sem(f"s_w1b{i}") for i in range(8)]
    s_w2b = [sem(f"s_w2b{i}") for i in range(8)]
    s_yewb = [sem(f"s_yewb{i}") for i in range(4)]
    s_ygtb = [sem(f"s_ygtb{i}") for i in range(4)]
    s_xcb = [sem(f"s_xcb{i}") for i in range(2)]
    s_owb = [sem(f"s_owb{i}") for i in range(2)]
    s_stat = sem("s_stat"); s_xn = sem("s_xn")
    s_xnfm = sem("s_xnfm"); s_wc = sem("s_wc")
    s_zf = sem("s_zf"); s_zt = sem("s_zt"); s_cl = sem("s_cl")
    s_ind = sem("s_ind"); s_off = sem("s_off"); s_ccw = sem("s_ccw")
    s_cc = sem("s_cc"); s_cntb = sem("s_cntb"); s_slot = sem("s_slot")
    s_sm = sem("s_sm")
    s_exp = sem("s_exp"); s_vn = sem("s_vn"); s_l2g = sem("s_l2g")
    s_w2c = sem("s_w2c")
    s_gsil = sem("s_gsil"); s_out = sem("s_out")
    s_ccb = sem("s_ccb"); s_basb = sem("s_basb"); s_sv = sem("s_sv")
    s_vs = sem("s_vs")
    s_comb = sem("s_comb"); s_sq = sem("s_sq"); s_rst = sem("s_rst")
    s_qkd = sem("s_qkd"); s_gpio = sem("s_gpio"); s_avj = sem("s_avj")
    sgd = [sem(f"sgd{i}") for i in range(8)]
    sbf = [sem(f"sbf{i}") for i in range(8)]

    NZF = (ZROWS + 127) // 128

    rec = {"qk": [], "vr": [], "rank": [], "attqk": [], "attav": [],
           "stp": [], "l1": {}, "l2": [], "base": []}
    ngrp = [0] * 8
    sgdcnt = [0] * 8

    def grp_begin(te, b):
        if ngrp[b] > 0:
            te.wait_ge(sbf[b], ngrp[b])

    def grp_end(stream, b):
        ngrp[b] += 1
        sgdcnt[b] += 1
        rec[stream].append((b, sgdcnt[b]))

    with nc.Block() as block:
        @block.tensor
        def _(te):
            te.wait_ge(s_xnfm, 16 * KD)
            te.wait_ge(s_wc, 16 * 3)
            for wsb in (wq_sb, wk_sb):
                for n in range(NTN):
                    b = n % 2
                    grp_begin(te, b)
                    for k in range(KD):
                        m = te.matmul(bank[b][0:HD, 0:512],
                                      wsb[:, k * HD:(k + 1) * HD],
                                      xnfm[:, k * S + n * 512: k * S + (n + 1) * 512],
                                      start=(k == 0), stop=(k == KD - 1))
                    m.then_inc(sgd[b], 1)
                    grp_end("qk", b)
            for i in range(NT):
                b = 2 + (i % 2)
                grp_begin(te, b)
                for k in range(KD):
                    m = te.matmul(bank[b][0:128, 0:72],
                                  xnfm[:, k * S + i * 128: k * S + (i + 1) * 128],
                                  wvr_sb[:, k * 72:(k + 1) * 72],
                                  start=(k == 0), stop=(k == KD - 1))
                m.then_inc(sgd[b], 1)
                grp_end("vr", b)
            te.wait_ge(s_cl, 16 * 5)
            for i in range(NT):
                b = 4 + (i % 2)
                b2 = 6 + (i % 2)
                grp_begin(te, b)
                te.wait_ge(s_ind, 1)
                te.wait_ge(s_off, i)
                te.matmul(bank[b][:, 0:16], lincl[:], ind_b[:, i * 16:(i + 1) * 16],
                          start=True, stop=False)
                m = te.matmul(bank[b][:, 0:16], sel127[:], off_sb[:],
                              start=False, stop=True)
                m.then_inc(sgd[b], 1)
                m2 = te.matmul(bank[b2][0:1, 0:16], onescol[:],
                               ind_b[:, i * 16:(i + 1) * 16],
                               start=True, stop=True)
                m2.then_inc(sgd[b2], 1)
                sgdcnt[b2] += 1
                ngrp[b] += 1
                sgdcnt[b] += 1
                rec["rank"].append(((b, sgdcnt[b]), (b2, sgdcnt[b2])))
            AVB = max(1, NTN // 2)
            avbanks = sorted({2 + (ic % AVB) for ic in range(NTN)})
            for j in range(NT):
                for ic in range(NTN):
                    b = ic % 2
                    grp_begin(te, b)
                    m = te.matmul(bank[b][:, :],
                                  kT[0:HD, j * 128:(j + 1) * 128],
                                  qT[0:HD, ic * 512:(ic + 1) * 512],
                                  start=True, stop=True)
                    m.then_inc(sgd[b], 1)
                    grp_end("attqk", b)
                if j == 0:
                    te.wait_ge(s_cc, 1)
                    te.wait_ge(s_ccb, 16 * 2)
                    te.wait_ge(s_off, NT)
                    m = te.matmul(bank[6][0:1, 0:16], prefm[:], ccall[:],
                                  start=True, stop=True)
                    m.then_inc(sgd[6], 1)
                    sgdcnt[6] += 1
                    rec["base_g"] = sgdcnt[6]
                    for b in avbanks:
                        grp_begin(te, b)
                te.wait_ge(s_vn, j + 1)
                for ic in range(NTN):
                    b = 2 + (ic % AVB)
                    hi = ic >= AVB and NTN > 1
                    m = te.matmul(bank[b][64:128, :] if hi else bank[b][0:HD, :],
                                  vn_sb[:],
                                  e_sb[j % 2][:, ic * 512:(ic + 1) * 512],
                                  start=(j == 0), stop=(j == NT - 1),
                                  tile_position=(0, 64) if hi else None,
                                  skip_group_check=True)
                    if ic == NTN - 1 and j < NT - 1:
                        m.then_inc(s_avj, 1)
                    if j == NT - 1 and (ic == NTN - 1 or
                                        (NTN > 1 and ic == NTN - 1 - AVB and False)):
                        pass
                if j == NT - 1:
                    m.then_inc(sgd[2], 1)
                    for b in avbanks:
                        ngrp[b] += 2 if (NTN > 1) else 1
                    sgdcnt[2] += 1
                    rec["attav_done"] = sgdcnt[2]
            te.wait_ge(sbf[2], ngrp[2])
            te.wait_ge(sbf[3], ngrp[3])
            for i in range(NT):
                b = 4 + (i % 2)
                grp_begin(te, b)
                m = te.transpose(bank[b][0:128, 0:HD],
                                 avacc[0:HD, i * 128:(i + 1) * 128],
                                 identf[0:HD, 0:HD])
                m.then_inc(sgd[b], 1)
                grp_end("stp", b)
            w1_used = 0
            w2_used = 0
            gel_total = 0
            npass = 0
            nw2c = 0
            rec["w2done"] = []
            for e in range(NE):
                te.wait_ge(s_xeb[e % 2], 16 * KD * (e // 2 + 1))
                for m_ in range(FC):
                    w1_used += 1
                    te.wait_ge(s_w1b[(w1_used - 1) % NW1BUF],
                               16 * ((w1_used - 1) // NW1BUF + 1))
                    bs = (m_ % 2) * NNX
                    for ni in range(NNX):
                        grp_begin(te, bs + ni)
                    for k in range(KD):
                        off = 0
                        for ni, nsz in enumerate(NXC):
                            b = bs + ni
                            mm = te.matmul(
                                bank[b][:, 0:nsz],
                                w1b_sb[(w1_used - 1) % NW1BUF][:, k * 128:(k + 1) * 128],
                                xe_sb[e % 2][:, k * C + off: k * C + off + nsz],
                                start=(k == 0), stop=(k == KD - 1))
                            if k == KD - 1:
                                mm.then_inc(sgd[b], 1)
                            off += nsz
                    for ni in range(NNX):
                        sgdcnt[bs + ni] += 1
                        rec["l1"][(e, m_, ni)] = (bs + ni, sgdcnt[bs + ni],
                                                  ngrp[bs + ni] + 1)
                        ngrp[bs + ni] += 1
                first_pass = True
                for half in range(2):
                    for grp in mgroups():
                        for mi in range(len(grp)):
                            grp_begin(te, mi % 8)
                        for k2 in range(FC):
                            if first_pass:
                                for ni in range(NNX):
                                    hb, _, hn = rec["l1"][(e, k2, ni)]
                                    te.wait_ge(sbf[hb], hn)
                            if k2 % W2KG == 0:
                                w2_used += 1
                                te.wait_ge(s_w2b[(w2_used - 1) % NW2BUF],
                                           16 * ((w2_used - 1) // NW2BUF + 1))
                            ws = w2b_sb[(w2_used - 1) % NW2BUF][:, (k2 % W2KG) * 512:
                                                                (k2 % W2KG + 1) * 512]
                            blk_end = (k2 % W2KG == W2KG - 1 or k2 == FC - 1)
                            for mi, m_ in enumerate(grp):
                                b = mi % 8
                                mm = te.matmul(
                                    bank[b][:, :],
                                    h_sb[:, k2 * C + m_ * 128: k2 * C + (m_ + 1) * 128],
                                    ws,
                                    start=(k2 == 0), stop=(k2 == FC - 1))
                                last_mi = (mi == len(grp) - 1)
                                if k2 == FC - 1 and last_mi:
                                    mm.then_inc(s_l2g, 1)
                                elif blk_end and last_mi:
                                    mm.then_inc(s_w2c, 1)
                            if blk_end:
                                if k2 == FC - 1:
                                    rec["w2done"].append(("g", npass + 1))
                                else:
                                    nw2c += 1
                                    rec["w2done"].append(("c", nw2c))
                        npass += 1
                        for mi in range(len(grp)):
                            b = mi % 8
                            ngrp[b] += 1
                            rec["l2"].append((b, npass, ngrp[b]))
                        first_pass = False

        @block.vector
        def _(ve):
            ve.memset(off_sb[:], 0.0)
            ve.wait_ge(s_wc, 16 * 3)
            ve.drain()
            ve.tensor_scalar_add(dumpc[:], iotap[:], float(NE * C))
            ve.memset(zt_sb[:], 0).then_inc(s_zt, 1)
            for i in range(NT + 1):
                if i < NT:
                    ve.wait_ge(s_xb[i % 3], 16 * (i // 3 + 1))
                    if i >= 2:
                        ve.wait_ge(s_xn, i - 1)
                    xt = xt_sb[i % 3]
                    pc = i % 2
                    ve.reduce_sum(statm[:, pc:pc + 1], xt[:], axis=AX.X)
                    ve.scalar_tensor_tensor(scr_sb[:], xt[:], 1.0, xt[:],
                                            ALU.mult, ALU.mult,
                                            accum_out=statv[:, pc:pc + 1])
                    ve.drain()
                    ve.tensor_scalar_mul(statm[:, pc:pc + 1], statm[:, pc:pc + 1],
                                         1.0 / D)
                    ve.tensor_scalar_mul(statv[:, pc:pc + 1], statv[:, pc:pc + 1],
                                         1.0 / D)
                    ve.drain()
                    ve.tensor_tensor(colH[:], statm[:, pc:pc + 1],
                                     statm[:, pc:pc + 1], ALU.mult)
                    ve.drain()
                    ve.tensor_sub(statv[:, pc:pc + 1], statv[:, pc:pc + 1], colH[:])
                    ve.drain()
                    ve.tensor_scalar_add(statv[:, pc:pc + 1], statv[:, pc:pc + 1],
                                         EPS).then_inc(s_stat, 1)
                if i >= 1:
                    pr = (i - 1) % 2
                    ve.wait_ge(s_sq, i)
                    ve.reciprocal(statr[:, pr:pr + 1], statq[:, pr:pr + 1])
                    ve.drain()
                    ve.tensor_tensor(statnb[:, pr:pr + 1], statm[:, pr:pr + 1],
                                     statr[:, pr:pr + 1], ALU.mult)
                    ve.drain()
                    ve.tensor_scalar_mul(statnb[:, pr:pr + 1],
                                         statnb[:, pr:pr + 1], -1.0) \
                        .then_inc(s_rst, 1)
            qki = 0
            for dst in (qT, kT):
                for n in range(NTN):
                    b, g = rec["qk"][qki]; qki += 1
                    ve.wait_ge(sgd[b], g)
                    ve.tensor_copy(dst[0:HD, n * 512:(n + 1) * 512],
                                   bank[b][0:HD, 0:512]).then_inc(sbf[b], 1)
            ve.sem_inc(s_qkd, 1)
            for i in range(NT):
                b, g = rec["vr"][i]
                ve.wait_ge(sgd[b], g)
                ve.tensor_copy(v_sb[:, i * HD:(i + 1) * HD], bank[b][0:128, 0:HD])
                mlast = ve.tensor_copy(logit_sb[:, i * NE:(i + 1) * NE],
                                       bank[b][0:128, HD:72])
                mlast.then_inc(sbf[b], 1)
                if i == NT - 1:
                    ve.memset(colA[:], 0.0).then_inc(s_vs, 1)
            ve.memset(ones_b[:], 1.0)
            ve.wait_ge(s_gpio, 1)
            ve.wait_ge(s_sv, 1)
            ve.reduce_sum(colsB[:, 0:NT], expv[:].rearrange("p (t e) -> p t e", e=NE),
                          axis=AX.X)
            ve.reduce_max(colsB[:, NT:2 * NT],
                          expv[:].rearrange("p (t e) -> p t e", e=NE), axis=AX.X)
            ve.drain()
            ve.reciprocal(colsB[:, 2 * NT:3 * NT], colsB[:, 0:NT])
            for i in range(NT):
                ve.tensor_scalar_mul(scrA[:, i * NE:(i + 1) * NE],
                                     ones_b[:, 0:NE], colsB[:, NT + i:NT + i + 1])
            ve.drain()
            ve.tensor_tensor(scrB[:], expv[:], scrA[:], ALU.is_ge)
            ve.drain()
            ve.tensor_tensor(scrC[:], scrB[:], bmi_b[:], ALU.mult)
            ve.drain()
            ve.reduce_max(colsB[:, 3 * NT:4 * NT],
                          scrC[:].rearrange("p (t e) -> p t e", e=NE), axis=AX.X)
            ve.drain()
            ve.tensor_scalar(colsB[:, 3 * NT:4 * NT], colsB[:, 3 * NT:4 * NT],
                             -1.0, 128.0, ALU.mult, ALU.add)
            ve.drain()
            for i in range(NT):
                ve.tensor_scalar_mul(scrA[:, i * NE:(i + 1) * NE],
                                     ones_b[:, 0:NE],
                                     colsB[:, 3 * NT + i:3 * NT + i + 1])
            ve.drain()
            ve.tensor_tensor(ind_b[:].rearrange("p (t k e) -> p t k e", k=2, e=NE)
                             [:, :, 0, :],
                             iotaB[:].rearrange("p (t e) -> p t e", e=NE),
                             scrA[:].rearrange("p (t e) -> p t e", e=NE),
                             ALU.is_equal)
            ve.drain()
            ve.scalar_tensor_tensor(scrB[:],
                                    ind_b[:].rearrange("p (t k e) -> p (t k e)"),
                                    1.0, expv[:], ALU.mult, ALU.mult) \
                if False else None
            ve.scalar_tensor_tensor(
                scrB[:],
                ind_b[:].rearrange("p (t k e) -> p t k e", k=2, e=NE)[:, :, 0, :],
                BIGNEG, expv[:], ALU.mult, ALU.add)
            ve.drain()
            ve.reduce_max(colsB[:, 4 * NT:5 * NT],
                          scrB[:].rearrange("p (t e) -> p t e", e=NE), axis=AX.X)
            ve.drain()
            for i in range(NT):
                ve.tensor_scalar_mul(scrA[:, i * NE:(i + 1) * NE],
                                     ones_b[:, 0:NE],
                                     colsB[:, 4 * NT + i:4 * NT + i + 1])
            ve.drain()
            ve.tensor_tensor(scrC[:], scrB[:], scrA[:], ALU.is_ge)
            ve.drain()
            ve.tensor_tensor(scrC[:], scrC[:], bmi_b[:], ALU.mult)
            ve.drain()
            ve.reduce_max(colsB[:, 5 * NT:6 * NT],
                          scrC[:].rearrange("p (t e) -> p t e", e=NE), axis=AX.X)
            ve.drain()
            ve.tensor_scalar(colsB[:, 5 * NT:6 * NT], colsB[:, 5 * NT:6 * NT],
                             -1.0, 128.0, ALU.mult, ALU.add)
            ve.drain()
            for i in range(NT):
                ve.tensor_scalar_mul(scrA[:, i * NE:(i + 1) * NE],
                                     ones_b[:, 0:NE],
                                     colsB[:, 5 * NT + i:5 * NT + i + 1])
            ve.drain()
            ve.tensor_tensor(ind_b[:].rearrange("p (t k e) -> p t k e", k=2, e=NE)
                             [:, :, 1, :],
                             iotaB[:].rearrange("p (t e) -> p t e", e=NE),
                             scrA[:].rearrange("p (t e) -> p t e", e=NE),
                             ALU.is_equal)
            ve.tensor_tensor(colsB[:, 6 * NT:7 * NT], colsB[:, NT:2 * NT],
                             colsB[:, 2 * NT:3 * NT], ALU.mult)
            ve.tensor_tensor(colsB[:, 7 * NT:8 * NT], colsB[:, 4 * NT:5 * NT],
                             colsB[:, 2 * NT:3 * NT], ALU.mult)
            ve.drain()
            eview = eidx_sb[:].rearrange("p (t k) -> p t k", k=2)
            wview = wco_sb[:].rearrange("p (t k) -> p t k", k=2)
            ve.tensor_copy(eview[:, :, 0], colsB[:, 3 * NT:4 * NT])
            ve.tensor_copy(eview[:, :, 1], colsB[:, 5 * NT:6 * NT])
            ve.tensor_copy(wview[:, :, 0], colsB[:, 6 * NT:7 * NT])
            ve.tensor_copy(wview[:, :, 1], colsB[:, 7 * NT:8 * NT])
            ve.tensor_copy(indp[:], ind_b[:]).then_inc(s_ind, 1)
            for i in range(NT):
                (b, g), (b2, g2) = rec["rank"][i]
                ve.wait_ge(sgd[b], g)
                ve.tensor_sub(rank_sb[:, i * 16:(i + 1) * 16], bank[b][:, 0:16],
                              ind_b[:, i * 16:(i + 1) * 16]).then_inc(sbf[b], 1)
                ve.wait_ge(sgd[b2], g2)
                ve.drain()
                ve.tensor_add(off_sb[0:1, :], off_sb[0:1, :],
                              bank[b2][0:1, 0:16]).then_inc(s_off, 1)
            ve.wait_ge(s_cntb, 16)
            for i in range(NT):
                ve.tensor_copy(cnt_exp[:, i * NE:(i + 1) * NE], cnt_bc[:, 0:NE])
            ve.drain()
            indv = ind_b[:].rearrange("p (t k e) -> p t k e", k=2, e=NE)
            rkv = rank_sb[:].rearrange("p (t k e) -> p t k e", k=2, e=NE)
            cB = colsB
            ve.tensor_tensor(scrA[:].rearrange("p (t e) -> p t e", e=NE),
                             indv[:, :, 0, :], rkv[:, :, 0, :], ALU.mult)
            ve.tensor_tensor(scrB[:].rearrange("p (t e) -> p t e", e=NE),
                             indv[:, :, 1, :], rkv[:, :, 1, :], ALU.mult)
            ve.drain()
            ve.reduce_sum(cB[:, 0:NT], scrA[:].rearrange("p (t e) -> p t e", e=NE),
                          axis=AX.X)
            ve.reduce_sum(cB[:, NT:2 * NT],
                          scrB[:].rearrange("p (t e) -> p t e", e=NE), axis=AX.X)
            ve.drain()
            ve.tensor_tensor(scrA[:].rearrange("p (t e) -> p t e", e=NE),
                             indv[:, :, 1, :],
                             cnt_exp[:].rearrange("p (t e) -> p t e", e=NE), ALU.mult)
            ve.drain()
            ve.reduce_sum(cB[:, 2 * NT:3 * NT],
                          scrA[:].rearrange("p (t e) -> p t e", e=NE), axis=AX.X)
            ve.drain()
            ve.tensor_add(cB[:, 2 * NT:3 * NT], cB[:, 2 * NT:3 * NT],
                          cB[:, NT:2 * NT])
            ve.drain()
            rkst = rk_sb[:].rearrange("p (t k) -> p t k", k=2)
            sctst = sct_sb[:].rearrange("p (t k) -> p t k", k=2)
            rslst = rsl_sb[:].rearrange("p (t k) -> p t k", k=2)
            for kk in range(2):
                slot = cB[:, (0 if kk == 0 else 2) * NT:(1 if kk == 0 else 3) * NT]
                rkl = cB[:, (0 if kk == 0 else 1) * NT:(1 if kk == 0 else 2) * NT]
                ecol = cB[:, (3 if kk == 0 else 5) * NT:(4 if kk == 0 else 6) * NT]
                ovf = cB[:, 8 * NT:9 * NT]
                t1 = cB[:, 9 * NT:10 * NT]
                t2 = cB[:, 10 * NT:11 * NT]
                ve.tensor_scalar(ovf, slot, float(C), None, ALU.is_ge)
                ve.drain()
                ve.scalar_tensor_tensor(t1, ovf, 1.0e9, rkl, ALU.mult, ALU.add)
                ve.drain()
                ve.tensor_copy(rkst[:, :, kk], t1)
                ve.scalar_tensor_tensor(t1, ecol, float(C), slot, ALU.mult, ALU.add)
                ve.tensor_scalar_mul(t2, ones_b[:, 0:NT], dumpc[:])
                ve.drain()
                ve.tensor_sub(t2, t2, t1)
                ve.drain()
                ve.tensor_tensor(t2, t2, ovf, ALU.mult)
                ve.drain()
                ve.tensor_add(t2, t2, t1)
                ve.drain()
                ve.tensor_copy(sctst[:, :, kk], t2)
                ve.tensor_scalar_min(t1, slot, float(C - 1))
                ve.drain()
                ve.scalar_tensor_tensor(t1, ecol, float(C), t1, ALU.mult, ALU.add)
                ve.drain()
                mcp = ve.tensor_copy(rslst[:, :, kk], t1)
            mcp.then_inc(s_slot, 1)
            for j in range(NT):
                if j >= 1:
                    ve.wait_ge(s_exp, j)
                for ic in range(NTN):
                    b, g = rec["attqk"][j * NTN + ic]
                    ve.wait_ge(sgd[b], g)
                    ve.tensor_scalar(att16[:, ic * 512:(ic + 1) * 512],
                                     bank[b][:, :], 0.0, None, ALU.add, ALU.max,
                                     accum_out=cmax[:, ic:ic + 1]).then_inc(sbf[b], 1)
                ve.drain()
                ve.reduce_max(negmax[:], cmax[:, 0:NTN], axis=AX.X, negate=True) \
                    .then_inc(s_sm, 1)
                ve.wait_ge(s_exp, j + 1)
                ve.reciprocal(vrec[:], den[:])
                ve.drain()
                if j >= 1:
                    ve.wait_ge(s_avj, j)
                ve.tensor_scalar_mul(vn_sb[:], v_sb[:, j * HD:(j + 1) * HD], vrec[:]) \
                    .then_inc(s_vn, 1)
            ve.wait_ge(sgd[2], rec["attav_done"])
            AVB = max(1, NTN // 2)
            for ic in range(NTN):
                b = 2 + (ic % AVB)
                hi = ic >= AVB and NTN > 1
                ve.tensor_copy(avacc[0:HD, ic * 512:(ic + 1) * 512],
                               bank[b][64:128, 0:512] if hi
                               else bank[b][0:HD, 0:512]).then_inc(sbf[b], 1)
            for i in range(NT):
                b, g = rec["stp"][i]
                ve.wait_ge(sgd[b], g)
                ve.tensor_copy(s_sb[:, i * HD:(i + 1) * HD],
                               bank[b][0:128, 0:HD]).then_inc(sbf[b], 1)
            ve.wait_ge(sgd[6], rec["base_g"])
            ve.tensor_copy(base_row[:], bank[6][0:1, 0:16]).then_inc(s_basb, 1)
            ve.wait_ge(s_basb, 17)
            indpv = indp[:].rearrange("p (t k e) -> p t k e", k=2, e=NE)
            for kk in range(2):
                gsel = p6cols[:, 0:NT]
                for i in range(NT):
                    ve.tensor_copy(p6A[:, i * NE:(i + 1) * NE],
                                   base_bc[:, kk * NE:(kk + 1) * NE])
                ve.drain()
                ve.tensor_tensor(p6B[:].rearrange("p (t e) -> p t e", e=NE),
                                 indpv[:, :, kk, :],
                                 p6A[:].rearrange("p (t e) -> p t e", e=NE), ALU.mult)
                ve.drain()
                ve.reduce_sum(gsel, p6B[:].rearrange("p (t e) -> p t e", e=NE),
                              axis=AX.X)
                ve.drain()
                ve.tensor_add(gsel, gsel,
                              rk_sb[:].rearrange("p (t k) -> p t k", k=2)[:, :, kk])
                ve.drain()
                ve.tensor_scalar(gsel, gsel, float(CAPACITY), None, ALU.is_lt)
                ve.drain()
                ve.tensor_tensor(gsel, gsel,
                                 wco_sb[:].rearrange("p (t k) -> p t k", k=2)[:, :, kk],
                                 ALU.mult)
                ve.drain()
                ve.tensor_copy(coef_b[:].rearrange("p (t k) -> p t k", k=2)[:, :, kk],
                               gsel)
            ve.drain()
            yn = 0
            for b, npass_, gcount in rec["l2"]:
                yn += 1
                if yn > 4:
                    ve.wait_ge(s_yewb[(yn - 4 - 1) % 4], 16 * ((yn - 4 - 1) // 4 + 1))
                ve.wait_ge(s_l2g, npass_)
                ve.tensor_copy(ye_sb[yn % 4][:, :], bank[b][:, :]).then_inc(sbf[b], 1)
            cview = coef_b[:].rearrange("p (t k) -> p t k", k=2)
            for i in range(NT):
                ve.wait_ge(s_ygtb[(2 * i) % 4], 16 * ((2 * i) // 4 + 1))
                ve.wait_ge(s_ygtb[(2 * i + 1) % 4], 16 * ((2 * i + 1) // 4 + 1))
                ve.wait_ge(s_xcb[i % 2], 16 * (i // 2 + 1))
                if i >= 1:
                    ve.wait_ge(s_gsil, i)
                ve.scalar_tensor_tensor(g_sb[:], yg_sb[(2 * i) % 4][:],
                                        coef_b[:, 2 * i:2 * i + 1],
                                        bias_bc[:], ALU.mult, ALU.add)
                ve.drain()
                ve.scalar_tensor_tensor(g_sb[:], yg_sb[(2 * i + 1) % 4][:],
                                        coef_b[:, 2 * i + 1:2 * i + 2],
                                        g_sb[:], ALU.mult, ALU.add).then_inc(s_comb, 1)
                ve.wait_ge(s_gsil, i + 1)
                ob = o_sb[i % 2]
                for c in range(D // HD):
                    ve.tensor_mul(ob[:, c * HD:(c + 1) * HD],
                                  ob[:, c * HD:(c + 1) * HD],
                                  s_sb[:, i * HD:(i + 1) * HD])
                ve.drain()
                ve.tensor_add(ob[:], ob[:], xc_sb[i % 2][:]).then_inc(s_out, 1)

        @block.scalar
        def _(sc):
            sc.wait_ge(s_zt, 1)
            for z in range(NZF):
                r0 = z * 128
                r1 = min(r0 + 128, ZROWS)
                sc.dma_start(out=xe_d[r0:r1, :], in_=zt_sb[0:r1 - r0, :]) \
                    .then_inc(s_zf, 16)
            for i in range(NT):
                sc.wait_ge(s_stat, i + 1)
                if i >= 2:
                    sc.wait_ge(s_xnwb[i % 2], 16 * (i // 2))
                pc = i % 2
                sc.activation(statq[:, pc:pc + 1], statv[:, pc:pc + 1], AF.Sqrt,
                              bias=0.0, scale=1.0).then_inc(s_sq, 1)
                sc.wait_ge(s_rst, i + 1)
                sc.activation(xnt_sb[i % 2][:], xt_sb[i % 3][:], AF.Identity,
                              bias=statnb[:, pc:pc + 1],
                              scale=statr[:, pc:pc + 1]).then_inc(s_xn, 1)
            sc.wait_ge(s_vs, 1)
            sc.activation(expv[:], logit_sb[:], AF.Exp).then_inc(s_sv, 1)
            for j in range(NT):
                sc.wait_ge(s_sm, j + 1)
                if j >= 2:
                    sc.wait_ge(s_avj, j - 1)
                sc.activation(e_sb[j % 2][:], att16[:], AF.Exp,
                              bias=negmax[:], scale=1.0,
                              accum_out=den[:]).then_inc(s_exp, 1)
            for e in range(NE):
                for m_ in range(FC):
                    off = 0
                    for ni, nsz in enumerate(NXC):
                        b, g, _ = rec["l1"][(e, m_, ni)]
                        sc.wait_ge(sgd[b], g)
                        sc.activation(h_sb[:, m_ * C + off: m_ * C + off + nsz],
                                      bank[b][:, 0:nsz], AF.Gelu_apprx_tanh) \
                            .then_inc(sbf[b], 1)
                        off += nsz
            for i in range(NT):
                sc.wait_ge(s_comb, i + 1)
                if i >= 2:
                    sc.wait_ge(s_owb[i % 2], 16 * ((i - 2) // 2 + 1))
                sc.activation(o_sb[i % 2][:], g_sb[:], AF.Silu).then_inc(s_gsil, 1)

        @block.gpsimd
        def _(gp):
            gp.iota(iota8[:], [[1, NE]], base=0, channel_multiplier=0,
                    allow_small_or_imprecise_dtypes=True)
            gp.iota(iotap[:], [[0, 1]], base=0, channel_multiplier=1,
                    allow_small_or_imprecise_dtypes=True)
            gp.iota(bmi[:], [[-1, NE]], base=128, channel_multiplier=0,
                    allow_small_or_imprecise_dtypes=True)
            gp.dma_start(out=wq_sb[:].rearrange("p (k n) -> p k n", k=KD),
                         in_=wq_d).then_inc(s_wc, 16)
            gp.dma_start(out=wk_sb[:].rearrange("p (k n) -> p k n", k=KD),
                         in_=wk_d).then_inc(s_wc, 16)
            gp.dma_start(out=wvr_sb[:].rearrange("p (k n) -> p k n", k=KD),
                         in_=wvr_d).then_inc(s_wc, 16)
            gp.wait_ge(s_xn, NT)
            gp.iota(iotaB[:], [[0, NT], [1, NE]], base=0, channel_multiplier=0,
                    allow_small_or_imprecise_dtypes=True)
            gp.iota(bmi_b[:], [[0, NT], [-1, NE]], base=128, channel_multiplier=0,
                    allow_small_or_imprecise_dtypes=True) if False else None
            gp.iota(bmi_b[:], [[0, NT], [-1, NE]], base=128,
                    channel_multiplier=0,
                    allow_small_or_imprecise_dtypes=True).then_inc(s_gpio, 1)
            gp.wait_ge(s_off, NT)
            gp.dma_start(out=ccin_d, in_=off_sb[0:1, :]).then_inc(s_ccw, 16)
            gp.wait_ge(s_ccw, 16)
            gp.collective_compute(
                "AllGather", ALU.bypass, replica_groups=[list(range(n_cores))],
                ins=[ccin_d.opt()], outs=[ccout_d.opt()],
            ).then_inc(s_cc, 1)
            gp.wait_ge(s_cc, 1)
            gp.dma_start(out=ccall[:], in_=ccout_d).then_inc(s_ccb, 16)
            gp.dma_start(out=prefm[:], in_=prefm_d).then_inc(s_ccb, 16)
            for i in range(NT):
                gp.wait_ge(s_xwb[i % 4], 16 * (i // 4 + 1))
                gp.wait_ge(s_slot, 1)
                if i == 0:
                    gp.wait_ge(s_zf, 16 * NZF)
                for kk in range(2):
                    gp.indirect_dma_start(
                        xe_d,
                        IndirectOffsetOnAxis(ap=sct_sb[:, 2 * i + kk:2 * i + kk + 1],
                                             axis=0),
                        xw_sb[i % 4][:], None,
                    ).then_inc(s_scatb[i % 4], 16)
            gp.wait_ge(s_basb, 1)
            gp.dma_start(out=ccbase_d, in_=base_row[:]).then_inc(s_ccb, 16)
            w1i = 0
            w2i = 0
            for e in range(NE):
                for m_ in range(FC):
                    w1i += 1
                    if w1i > 2:
                        jb = w1i - 2
                        eb, mb = (jb - 1) // FC, (jb - 1) % FC
                        wb, wg, _ = rec["l1"][(eb, mb, NNX - 1)]
                        gp.wait_ge(sgd[wb], wg)
                    gp.dma_start(out=w1b_sb[(w1i - 1) % NW1BUF][:].rearrange(
                        "p (k f) -> p k f", k=KD),
                        in_=w1_d[e, m_]).then_inc(s_w1b[(w1i - 1) % NW1BUF], 16)
                for half in range(2):
                    for grp in mgroups():
                        for kb in range(NW2B):
                            w2i += 1
                            if w2i > NW2BUF:
                                kind, val = rec["w2done"][w2i - 1 - NW2BUF]
                                gp.wait_ge(s_w2c if kind == "c" else s_l2g, val)
                            gp.dma_start(
                                out=w2b_sb[(w2i - 1) % NW2BUF][:].rearrange(
                                    "p (k j) -> p k j", k=W2KG),
                                in_=w2_d[e, kb * W2KG:(kb + 1) * W2KG, half]
                                .rearrange("k p j -> p k j"),
                            ).then_inc(s_w2b[(w2i - 1) % NW2BUF], 16)
            nyes = NE * 2 * MC
            for p in range(4):
                tot = nyes // 4 + (1 if p < nyes % 4 else 0)
                gp.wait_ge(s_yewb[p], 16 * tot)
            for i in range(NT):
                for kk in range(2):
                    n = 2 * i + kk
                    if n >= 4:
                        gp.wait_ge(s_comb, (n - 4) // 2 + 1)
                    gp.indirect_dma_start(
                        yg_sb[n % 4][:], None,
                        ye_d,
                        IndirectOffsetOnAxis(ap=rsl_sb[:, n:n + 1], axis=0),
                    ).then_inc(s_ygtb[n % 4], 16)

        @block.sync
        def _(sy):
            sy.dma_start(out=bias_bc[:],
                         in_=bias_d.broadcast_to([128, D])).then_inc(s_cl, 16)
            sy.dma_start(out=identf[:], in_=identf_d).then_inc(s_cl, 16)
            sy.dma_start(out=lincl[:], in_=lincl_d).then_inc(s_cl, 16)
            sy.dma_start(out=sel127[:], in_=sel127_d).then_inc(s_cl, 16)
            sy.dma_start(out=onescol[:], in_=onescol_d).then_inc(s_cl, 16)
            for i in range(NT):
                if i >= 3:
                    sy.wait_ge(s_xn, i - 2)
                sy.dma_start(out=xt_sb[i % 3][:],
                             in_=x_d[i * 128:(i + 1) * 128, :]).then_inc(s_xb[i % 3], 16)
                if i >= 1:
                    sy.wait_ge(s_xn, i)
                    sy.dma_start(out=xn_d[(i - 1) * 128:i * 128, :],
                                 in_=xnt_sb[(i - 1) % 2][:]) \
                        .then_inc(s_xnwb[(i - 1) % 2], 16)
            sy.wait_ge(s_xn, NT)
            sy.dma_start(out=xn_d[(NT - 1) * 128:NT * 128, :],
                         in_=xnt_sb[(NT - 1) % 2][:]) \
                .then_inc(s_xnwb[(NT - 1) % 2], 16)
            sy.wait_ge(s_xnwb[0], 16 * ((NT + 1) // 2))
            sy.wait_ge(s_xnwb[1], 16 * (NT // 2))
            for c in range(KD):
                sy.dma_start(out=xnfm[:, c * S:(c + 1) * S],
                             in_=xn_d[:, c * 128:(c + 1) * 128],
                             transpose=True).then_inc(s_xnfm, 16)
            if debug_taps:
                sy.wait_ge(s_xnfm, 16 * KD)
                sy.dma_start(out=dbg["dbg_xnfm"], in_=xnfm[:]).then_inc(s_cl, 16)
                sy.wait_ge(s_qkd, 1)
                sy.dma_start(out=dbg["dbg_qT"], in_=qT[:]).then_inc(s_cl, 16)
                sy.dma_start(out=dbg["dbg_kT"], in_=kT[:]).then_inc(s_cl, 16)
            sy.wait_ge(s_ccw, 16)
            sy.dma_start(out=cnt_bc[:],
                         in_=ccin_d.broadcast_to([128, 16])).then_inc(s_cntb, 16)
            for i in range(NT):
                if i >= 4:
                    sy.wait_ge(s_scatb[i % 4], 32 * (i // 4))
                sy.dma_start(out=xw_sb[i % 4][:],
                             in_=xn_d[i * 128:(i + 1) * 128, :]) \
                    .then_inc(s_xwb[i % 4], 16)
            sy.wait_ge(s_ccb, 16 * 3)
            sy.dma_start(out=base_bc[:],
                         in_=ccbase_d.broadcast_to([128, 16])).then_inc(s_basb, 16)
            def emit_ye_stores(e, yewn):
                for half in range(2):
                    for grp in mgroups():
                        for m_ in grp:
                            yewn += 1
                            yb, ynp, ygc = rec["l2"][yewn - 1]
                            sy.wait_ge(sbf[yb], ygc)
                            sy.dma_start(
                                out=ye_d[e * C + m_ * 128: e * C + (m_ + 1) * 128,
                                         half * 512:(half + 1) * 512],
                                in_=ye_sb[yewn % 4][:, :]) \
                                .then_inc(s_yewb[(yewn - 1) % 4], 16)
                return yewn

            yewn = 0
            for e in range(NE):
                if e == 0:
                    for p in range(4):
                        tot = 2 * (NT // 4 + (1 if p < NT % 4 else 0))
                        sy.wait_ge(s_scatb[p], 16 * tot)
                if e == 1:
                    qb, qg = rec["attqk"][-1]
                    sy.wait_ge(sgd[qb], qg)
                if e >= 2:
                    wb, wg, _ = rec["l1"][(e - 2, FC - 1, NNX - 1)]
                    sy.wait_ge(sgd[wb], wg)
                for k in range(KD):
                    sy.dma_start(out=xe_sb[e % 2][:, k * C:(k + 1) * C],
                                 in_=xe_d[e * C:(e + 1) * C, k * 128:(k + 1) * 128],
                                 transpose=True).then_inc(s_xeb[e % 2], 16)
                if e >= 1:
                    yewn = emit_ye_stores(e - 1, yewn)
            yewn = emit_ye_stores(NE - 1, yewn)
            for i in range(NT):
                if i >= 2:
                    sy.wait_ge(s_out, i - 1)
                sy.dma_start(out=xc_sb[i % 2][:],
                             in_=x_d[i * 128:(i + 1) * 128, :]) \
                    .then_inc(s_xcb[i % 2], 16)
                if i >= 1:
                    sy.wait_ge(s_out, i)
                    sy.dma_start(out=out_d[(i - 1) * 128:i * 128, :],
                                 in_=o_sb[(i - 1) % 2][:]) \
                        .then_inc(s_owb[(i - 1) % 2], 16)
            sy.wait_ge(s_out, NT)
            sy.dma_start(out=out_d[(NT - 1) * 128:NT * 128, :],
                         in_=o_sb[(NT - 1) % 2][:]).then_inc(s_owb[(NT - 1) % 2], 16)
            sy.wait_ge(s_owb[0], 16 * ((NT + 1) // 2))
            sy.wait_ge(s_owb[1], 16 * (NT // 2))
            if debug_taps:
                ndbg = 0
                sy.dma_start(out=dbg["dbg_xnd"], in_=xn_d).then_inc(s_cl, 16)
                for name, src_sb in (("dbg_s", s_sb), ("dbg_logit", logit_sb),
                                     ("dbg_wco", wco_sb), ("dbg_eidx", eidx_sb),
                                     ("dbg_rk", rk_sb), ("dbg_rsl", rsl_sb),
                                     ("dbg_sct", sct_sb), ("dbg_base", base_bc),
                                     ("dbg_rank", rank_sb)):
                    sy.dma_start(out=dbg[name], in_=src_sb[:]).then_inc(s_cl, 16)
                    ndbg += 1
                sy.wait_ge(s_cl, 16 * 5 + 16 * (ndbg + 4))

    es.close()
    return nc


def _host_layouts(inputs):
    x = np.ascontiguousarray(inputs["L0_attn_x"])
    wq = np.ascontiguousarray(inputs["L0_attn_Wq"])
    wk = np.ascontiguousarray(inputs["L0_attn_Wk"])
    wv = np.ascontiguousarray(inputs["L0_attn_Wv"])
    wr = np.ascontiguousarray(inputs["L0_ffn_router_w"])
    w1 = np.ascontiguousarray(inputs["L0_ffn_w1"])
    w2 = np.ascontiguousarray(inputs["L0_ffn_w2"])
    bias = np.ascontiguousarray(inputs["L0_ffn_bias"]).reshape(1, D)
    S = x.shape[1]
    F = w1.shape[2]
    FC = F // 128

    def chunk_pk(w, ncols):
        return np.ascontiguousarray(w.reshape(KD, 128, ncols).transpose(1, 0, 2))

    wvr = np.concatenate([wv, wr], axis=1)
    w1h = np.ascontiguousarray(
        w1.reshape(NE, KD, 128, FC, 128).transpose(0, 3, 2, 1, 4))
    w2h = np.ascontiguousarray(
        w2.reshape(NE, FC, 128, 2, 512).transpose(0, 1, 3, 2, 4))
    identf = np.eye(128, dtype=np.float32)
    lincl = np.triu(np.ones((128, 128), np.float32))
    sel127 = np.zeros((128, 128), np.float32)
    sel127[0, :] = 1.0
    onescol = np.ones((128, 1), np.float32)
    common = dict(
        wqh=chunk_pk(wq, HD), wkh=chunk_pk(wk, HD), wvrh=chunk_pk(wvr, 72),
        w1h=w1h, w2h=w2h, bias1=bias,
        identf=identf, lincl=lincl, sel127=sel127, onescol=onescol,
    )
    in_maps = []
    for c in range(x.shape[0]):
        prefm = np.zeros((NE, 1), np.float32)
        prefm[:c, 0] = 1.0
        in_maps.append(dict(common, x=x[c], prefm=prefm))
    return in_maps, S, F


def kernel(**inputs):
    in_maps, S, F = _host_layouts(inputs)
    nc = build_program(S=S, F=F, C=1152, n_cores=8)
    res = run_bass_kernel_spmd(nc, in_maps, core_ids=list(range(8)))
    out = np.stack([r["out"] for r in res.results], axis=0)
    return out.astype(np.float32)


# version 12
# speedup vs baseline: 1.3710x; 1.0893x over previous
import numpy as np
import concourse.bass as bass
import concourse.mybir as mybir
from concourse.bass import IndirectOffsetOnAxis
from concourse.bass_utils import run_bass_kernel_spmd

F32 = mybir.dt.float32
BF16 = mybir.dt.bfloat16
F16 = mybir.dt.float16
U32 = mybir.dt.uint32
AF = mybir.ActivationFunctionType
ALU = mybir.AluOpType
AX = mybir.AxisListType

D = 1024
KD = D // 128
HD = 64
NE = 8
EPS = 1e-5
BIGNEG = -1.0e30


def build_program(S=4096, F=4096, C=1152, n_cores=8, debug_taps=False):
    NT = S // 128
    NTN = S // 512
    FC = F // 128
    MC = C // 128
    NXC = [512] * (C // 512) + ([C % 512] if C % 512 else [])
    NNX = len(NXC)
    W2KG = min(4, FC)
    NW2B = (FC + W2KG - 1) // W2KG
    NW2BUF = 3
    NW1BUF = 3
    CAPACITY = S * n_cores // NE
    assert S % 512 == 0 and C % 128 == 0 and F % (128 * W2KG) == 0

    def mgroups():
        if MC <= 8:
            return [list(range(MC))]
        h1 = (MC + 1) // 2
        return [list(range(h1)), list(range(h1, MC))]

    nc = bass.Bass("TRN2", target_bir_lowering=False, debug=False,
                   detect_race_conditions=False)

    x_d = nc.dram_tensor("x", [S, D], F32, kind="ExternalInput").ap()
    wq_d = nc.dram_tensor("wqh", [128, KD, HD], F32, kind="ExternalInput").ap()
    wk_d = nc.dram_tensor("wkh", [128, KD, HD], F32, kind="ExternalInput").ap()
    wvr_d = nc.dram_tensor("wvrh", [128, KD, 72], F32, kind="ExternalInput").ap()
    w1_d = nc.dram_tensor("w1h", [NE, FC, 128, KD, 128], F32, kind="ExternalInput").ap()
    w2_d = nc.dram_tensor("w2h", [NE, FC, 2, 128, 512], F32, kind="ExternalInput").ap()
    bias_d = nc.dram_tensor("bias1", [1, D], F32, kind="ExternalInput").ap()
    identf_d = nc.dram_tensor("identf", [128, 128], F32, kind="ExternalInput").ap()
    identb_d = nc.dram_tensor("identb", [128, 128], BF16, kind="ExternalInput").ap()
    lincl_d = nc.dram_tensor("lincl", [128, 128], F32, kind="ExternalInput").ap()
    sel127_d = nc.dram_tensor("sel127", [128, 128], F32, kind="ExternalInput").ap()
    onescol_d = nc.dram_tensor("onescol", [128, 1], F32, kind="ExternalInput").ap()
    prefm_d = nc.dram_tensor("prefm", [NE, 1], F32, kind="ExternalInput").ap()
    out_d = nc.dram_tensor("out", [S, D], F32, kind="ExternalOutput").ap()

    xn_d = nc.dram_tensor("xn_dram", [S, D], BF16).ap()
    av_d = nc.dram_tensor("av_dram", [HD, S], BF16).ap()
    ZROWS = NE * C + 128
    xe_d = nc.dram_tensor("xe_dram", [ZROWS, D], BF16).ap()
    ye_d = nc.dram_tensor("ye_dram", [NE * C, D], BF16).ap()
    ccin_d = nc.dram_tensor("cc_in", [1, 16], F32).ap()
    ccout_d = nc.dram_tensor("cc_out", [n_cores, 16], F32).ap()
    ccbase_d = nc.dram_tensor("cc_base", [1, 16], F32).ap()
    if debug_taps:
        NTt = S // 128
        dbg = {
            "dbg_s": nc.dram_tensor("dbg_s", [128, NTt * HD], BF16, kind="ExternalOutput").ap(),
            "dbg_logit": nc.dram_tensor("dbg_logit", [128, NTt * NE], F32, kind="ExternalOutput").ap(),
            "dbg_wco": nc.dram_tensor("dbg_wco", [128, NTt * 2], F32, kind="ExternalOutput").ap(),
            "dbg_eidx": nc.dram_tensor("dbg_eidx", [128, NTt * 2], F32, kind="ExternalOutput").ap(),
            "dbg_rk": nc.dram_tensor("dbg_rk", [128, NTt * 2], F32, kind="ExternalOutput").ap(),
            "dbg_rsl": nc.dram_tensor("dbg_rsl", [128, NTt * 2], U32, kind="ExternalOutput").ap(),
            "dbg_sct": nc.dram_tensor("dbg_sct", [128, NTt * 2], U32, kind="ExternalOutput").ap(),
            "dbg_base": nc.dram_tensor("dbg_base", [128, 16], F32, kind="ExternalOutput").ap(),
            "dbg_rank": nc.dram_tensor("dbg_rank", [128, NTt * 16], F32, kind="ExternalOutput").ap(),
            "dbg_xnd": nc.dram_tensor("dbg_xnd", [S, D], BF16, kind="ExternalOutput").ap(),
            "dbg_av": nc.dram_tensor("dbg_av", [HD, S], BF16, kind="ExternalOutput").ap(),
            "dbg_xnfm": nc.dram_tensor("dbg_xnfm", [128, KD * S], BF16, kind="ExternalOutput").ap(),
            "dbg_qT": nc.dram_tensor("dbg_qT", [64, S], BF16, kind="ExternalOutput").ap(),
            "dbg_kT": nc.dram_tensor("dbg_kT", [64, S], BF16, kind="ExternalOutput").ap(),
        }

    def sbuf(name, shape, dtype):
        return nc.alloc_sbuf_tensor(name, shape, dtype)

    def nbytes(shape, dtype):
        return shape[1] * mybir.dt.size(dtype)

    h_sb = sbuf("h_sb", [128, FC * C], BF16)
    h_addr = nc.lookup_mloc(h_sb).addr
    if KD * S <= FC * C:
        xnfm = nc.alloc_sbuf_tensor_at("xn_fm", [128, KD * S], BF16, offset=h_addr)
    else:
        xnfm = sbuf("xn_fm", [128, KD * S], BF16)
    if KD * S * 2 + S * 2 <= FC * C * 2:
        qT = nc.alloc_sbuf_tensor_at("qT", [64, S], BF16,
                                     offset=h_addr + KD * S * 2)
    else:
        qT = sbuf("qT", [64, S], BF16)
    xe_sb = [sbuf(f"xe{i}", [128, KD * C], BF16) for i in range(2)]
    if S * 2 <= KD * C * 2:
        kT = nc.alloc_sbuf_tensor_at("kT", [64, S], BF16,
                                     offset=nc.lookup_mloc(xe_sb[1]).addr)
    else:
        kT = sbuf("kT", [64, S], BF16)

    R1_P1 = max(3 * 4096 + 2 * 2048 + 4096,
                10 * (S // 128) * NE * 4 + (S // 128) * 14 * 4)
    R1_P4 = 2 * S + 2 * 2 * S + 2 * 2 * S
    R1_P6 = 4 * 2048 + 5 * 4096 + 2 * (S // 128) * NE * 4 + (S // 128) * 16
    R1 = max(R1_P1, R1_P4, R1_P6)
    r1_sb = sbuf("r1_arena", [128, R1], mybir.dt.uint8)
    r1_addr = nc.lookup_mloc(r1_sb).addr

    def at(name, shape, dtype, off):
        assert off + nbytes(shape, dtype) <= R1, (name, off)
        return nc.alloc_sbuf_tensor_at(name, shape, dtype, offset=r1_addr + off)

    xt_sb = [at(f"xt{i}", [128, D], F32, i * 4096) for i in range(3)]
    NT_ = S // 128
    expv = at("expv", [128, NT_ * NE], F32, 0)
    scrA = at("scrA", [128, NT_ * NE], F32, NT_ * NE * 4)
    scrB = at("scrB", [128, NT_ * NE], F32, 2 * NT_ * NE * 4)
    scrC = at("scrC", [128, NT_ * NE], F32, 3 * NT_ * NE * 4)
    ind_b = at("ind_b", [128, NT_ * 16], F32, 4 * NT_ * NE * 4)
    iotaB = at("iotaB", [128, NT_ * NE], F32, 6 * NT_ * NE * 4)
    bmi_b = at("bmi_b", [128, NT_ * NE], F32, 7 * NT_ * NE * 4)
    cnt_exp = at("cnt_exp", [128, NT_ * NE], F32, 8 * NT_ * NE * 4)
    ones_b = at("ones_b", [128, NT_ * NE], F32, 9 * NT_ * NE * 4)
    colsB = at("colsB", [128, NT_ * 14], F32, 10 * NT_ * NE * 4)
    p6A = at("p6A", [128, NT_ * NE], F32, 4 * 2048 + 5 * 4096)
    p6B = at("p6B", [128, NT_ * NE], F32, 4 * 2048 + 5 * 4096 + NT_ * NE * 4)
    p6cols = at("p6cols", [128, NT_ * 4], F32, 4 * 2048 + 5 * 4096 + 2 * NT_ * NE * 4)
    coef_b = sbuf("coef_b", [128, NT_ * 2], F32)
    indp = sbuf("indp", [128, NT_ * 16], BF16)
    xnt_sb = [at(f"xnt{i}", [128, D], BF16, 3 * 4096 + i * 2048) for i in range(2)]
    scr_sb = at("scr_sb", [128, D], F32, 3 * 4096 + 2 * 2048)
    avacc = at("avacc", [64, S], BF16, 0)
    att16 = [at(f"att16_{i}", [128, S], F16, 2 * S + i * 2 * S) for i in range(2)]
    e_sb = [at(f"esb{i}", [128, S], BF16, 2 * S + 4 * S + i * 2 * S)
            for i in range(2)]
    yg_sb = [at(f"yg{i}", [128, D], BF16, i * 2048) for i in range(4)]
    xc_sb = [at(f"xc{i}", [128, D], F32, 4 * 2048 + i * 4096) for i in range(2)]
    o_sb = [at(f"osb{i}", [128, D], F32, 4 * 2048 + 2 * 4096 + i * 4096)
            for i in range(2)]
    g_sb = at("g_sb", [128, D], F32, 4 * 2048 + 4 * 4096)

    v_sb = sbuf("v_sb", [128, NT * HD], BF16)
    s_sb = sbuf("s_sb", [128, NT * HD], BF16)
    logit_sb = sbuf("logit_sb", [128, NT * NE], F32)
    rank_sb = sbuf("rank_sb", [128, NT * 16], F32)
    rk_sb = sbuf("rk_sb", [128, NT * 2], F32)
    eidx_sb = sbuf("eidx_sb", [128, NT * 2], F32)
    wco_sb = sbuf("wco_sb", [128, NT * 2], F32)
    sct_sb = sbuf("sct_sb", [128, NT * 2], U32)
    rsl_sb = sbuf("rsl_sb", [128, NT * 2], U32)
    ind_sb = [sbuf(f"ind{i}", [128, 16], F32) for i in range(2)]
    off_sb = sbuf("off_sb", [128, 16], F32)
    cnt_bc = sbuf("cnt_bc", [128, 16], F32)
    base_bc = sbuf("base_bc", [128, 16], F32)
    ccall = sbuf("ccall", [NE, 16], F32)
    prefm = sbuf("prefm_sb", [NE, 1], F32)
    xw_sb = [sbuf(f"xw{i}", [128, D], BF16) for i in range(4)]
    w1b_sb = [sbuf(f"w1b{i}", [128, KD * 128], BF16) for i in range(NW1BUF)]
    w2b_sb = [sbuf(f"w2b{i}", [128, W2KG * 512], BF16) for i in range(NW2BUF)]
    wq_sb = sbuf("wq_sb", [128, KD * HD], BF16)
    wk_sb = sbuf("wk_sb", [128, KD * HD], BF16)
    wvr_sb = sbuf("wvr_sb", [128, KD * 72], BF16)
    ye_sb = [sbuf(f"yesb{i}", [128, 512], BF16) for i in range(4)]
    bias_bc = sbuf("bias_bc", [128, D], F32)
    zt_sb = sbuf("zt_sb", [128, D], BF16)
    identf = sbuf("identf_sb", [128, 128], F32)
    identb = sbuf("identb_sb", [128, 128], BF16)
    lincl = sbuf("lincl_sb", [128, 128], F32)
    sel127 = sbuf("sel127_sb", [128, 128], F32)
    onescol = sbuf("onescol_sb", [128, 1], F32)
    iota8 = sbuf("iota8", [128, NE], F32)
    iotap = sbuf("iotap", [128, 1], F32)
    bmi = sbuf("bmi", [128, NE], F32)
    st1 = sbuf("st1", [128, NE], F32)
    st2 = sbuf("st2", [128, NE], F32)
    st3 = sbuf("st3", [128, NE], F32)
    st4 = sbuf("st4", [128, NE], F32)
    cmax = sbuf("cmax", [128, 16], F32)
    _sw = max(32, -((S // 128) * -4) // 32 * 32)
    _stat0 = (R1 - 5 * _sw) // 32 * 32
    assert _stat0 >= R1_P1
    statm = at("statm", [128, S // 128], F32, _stat0)
    statv = at("statv", [128, S // 128], F32, _stat0 + _sw)
    statr = at("statr", [128, S // 128], F32, _stat0 + 2 * _sw)
    statnb = at("statnb", [128, S // 128], F32, _stat0 + 3 * _sw)
    statt = at("statt", [128, S // 128], F32, _stat0 + 4 * _sw)
    colA = sbuf("colA", [128, 1], F32)
    colB = sbuf("colB", [128, 1], F32)
    colC = sbuf("colC", [128, 1], F32)
    colD = sbuf("colD", [128, 1], F32)
    colE = sbuf("colE", [128, 1], F32)
    colF = sbuf("colF", [128, 1], F32)
    colG = sbuf("colG", [128, 1], F32)
    colH = sbuf("colH", [128, 1], F32)
    c0col = sbuf("c0col", [128, 1], F32)
    dumpc = sbuf("dumpc", [128, 1], F32)
    c1col = sbuf("c1col", [128, 1], F32)
    negmax = sbuf("negmax", [128, 2], F32)
    den = sbuf("den", [128, 1], F32)
    vrec = sbuf("vrec", [128, 1], F32)
    vn_sb = sbuf("vn_sb", [128, HD], BF16)
    base_row = sbuf("base_row", [1, 16], F32)

    from contextlib import ExitStack
    es = ExitStack()
    bank = [es.enter_context(nc.psum_tensor(f"bank{i}", [128, 512], F32))
            for i in range(8)]

    def sem(name):
        return es.enter_context(nc.semaphore(name))

    s_xb = [sem(f"s_xb{i}") for i in range(3)]
    s_xnwb = [sem(f"s_xnwb{i}") for i in range(2)]
    s_xwb = [sem(f"s_xwb{i}") for i in range(4)]
    s_scatb = [sem(f"s_scatb{i}") for i in range(4)]
    s_xeb = [sem(f"s_xeb{i}") for i in range(2)]
    s_w1b = [sem(f"s_w1b{i}") for i in range(8)]
    s_w2b = [sem(f"s_w2b{i}") for i in range(8)]
    s_yewb = [sem(f"s_yewb{i}") for i in range(4)]
    s_ygtb = [sem(f"s_ygtb{i}") for i in range(4)]
    s_xcb = [sem(f"s_xcb{i}") for i in range(2)]
    s_owb = [sem(f"s_owb{i}") for i in range(2)]
    s_stat = sem("s_stat"); s_xn = sem("s_xn")
    s_xnfm = sem("s_xnfm"); s_wc = sem("s_wc")
    s_zf = sem("s_zf"); s_zt = sem("s_zt"); s_cl = sem("s_cl")
    s_ind = sem("s_ind"); s_off = sem("s_off"); s_ccw = sem("s_ccw")
    s_cc = sem("s_cc"); s_cntb = sem("s_cntb"); s_slot = sem("s_slot")
    s_sm = sem("s_sm")
    s_exp = sem("s_exp"); s_vn = sem("s_vn"); s_l2g = sem("s_l2g")
    s_w2c = sem("s_w2c")
    s_gsil = sem("s_gsil"); s_out = sem("s_out")
    s_ccb = sem("s_ccb"); s_basb = sem("s_basb"); s_sv = sem("s_sv")
    s_vs = sem("s_vs")
    s_comb = sem("s_comb"); s_sq = sem("s_sq"); s_rst = sem("s_rst")
    s_av2d = sem("s_av2d"); s_stw = sem("s_stw")
    s_qkd = sem("s_qkd"); s_gpio = sem("s_gpio"); s_avj = sem("s_avj")
    sgd = [sem(f"sgd{i}") for i in range(8)]
    sbf = [sem(f"sbf{i}") for i in range(8)]

    NZF = (ZROWS + 127) // 128

    rec = {"qk": [], "vr": [], "rank": [], "attqk": [], "attav": [],
           "stp": [], "l1": {}, "l2": [], "base": []}
    ngrp = [0] * 8
    sgdcnt = [0] * 8

    def grp_begin(te, b):
        if ngrp[b] > 0:
            te.wait_ge(sbf[b], ngrp[b])

    def grp_end(stream, b):
        ngrp[b] += 1
        sgdcnt[b] += 1
        rec[stream].append((b, sgdcnt[b]))

    with nc.Block() as block:
        @block.tensor
        def _(te):
            te.wait_ge(s_xnfm, 16 * KD)
            te.wait_ge(s_wc, 16 * 3)
            for wsb in (wq_sb, wk_sb):
                for n in range(NTN):
                    b = n % 2
                    grp_begin(te, b)
                    for k in range(KD):
                        m = te.matmul(bank[b][0:HD, 0:512],
                                      wsb[:, k * HD:(k + 1) * HD],
                                      xnfm[:, k * S + n * 512: k * S + (n + 1) * 512],
                                      start=(k == 0), stop=(k == KD - 1))
                    m.then_inc(sgd[b], 1)
                    grp_end("qk", b)
            for i in range(NT):
                b = 2 + (i % 2)
                grp_begin(te, b)
                for k in range(KD):
                    m = te.matmul(bank[b][0:128, 0:72],
                                  xnfm[:, k * S + i * 128: k * S + (i + 1) * 128],
                                  wvr_sb[:, k * 72:(k + 1) * 72],
                                  start=(k == 0), stop=(k == KD - 1))
                m.then_inc(sgd[b], 1)
                grp_end("vr", b)
            te.wait_ge(s_cl, 16 * 6)
            for i in range(NT):
                b = 4 + (i % 2)
                b2 = 6 + (i % 2)
                grp_begin(te, b)
                te.wait_ge(s_ind, 1)
                te.wait_ge(s_off, i)
                te.matmul(bank[b][:, 0:16], lincl[:], ind_b[:, i * 16:(i + 1) * 16],
                          start=True, stop=False)
                m = te.matmul(bank[b][:, 0:16], sel127[:], off_sb[:],
                              start=False, stop=True)
                m.then_inc(sgd[b], 1)
                m2 = te.matmul(bank[b2][0:1, 0:16], onescol[:],
                               ind_b[:, i * 16:(i + 1) * 16],
                               start=True, stop=True)
                m2.then_inc(sgd[b2], 1)
                sgdcnt[b2] += 1
                ngrp[b] += 1
                sgdcnt[b] += 1
                rec["rank"].append(((b, sgdcnt[b]), (b2, sgdcnt[b2])))
            AVB = max(1, NTN // 2)
            avbanks = sorted({2 + (ic % AVB) for ic in range(NTN)})

            def emit_qk(jq):
                for ic in range(NTN):
                    b = ic % 2
                    grp_begin(te, b)
                    m = te.matmul(bank[b][:, :],
                                  kT[0:HD, jq * 128:(jq + 1) * 128],
                                  qT[0:HD, ic * 512:(ic + 1) * 512],
                                  start=True, stop=True)
                    m.then_inc(sgd[b], 1)
                    grp_end("attqk", b)

            QKLA = 2
            for jq in range(min(QKLA, NT)):
                emit_qk(jq)
            for j in range(NT):
                if j + QKLA < NT:
                    emit_qk(j + QKLA)
                if j == 0:
                    te.wait_ge(s_cc, 1)
                    te.wait_ge(s_ccb, 16 * 2)
                    te.wait_ge(s_off, NT)
                    m = te.matmul(bank[6][0:1, 0:16], prefm[:], ccall[:],
                                  start=True, stop=True)
                    m.then_inc(sgd[6], 1)
                    sgdcnt[6] += 1
                    rec["base_g"] = sgdcnt[6]
                    for b in avbanks:
                        grp_begin(te, b)
                te.wait_ge(s_vn, j + 1)
                for ic in range(NTN):
                    b = 2 + (ic % AVB)
                    hi = ic >= AVB and NTN > 1
                    m = te.matmul(bank[b][64:128, :] if hi else bank[b][0:HD, :],
                                  vn_sb[:],
                                  e_sb[j % 2][:, ic * 512:(ic + 1) * 512],
                                  start=(j == 0), stop=(j == NT - 1),
                                  tile_position=(0, 64) if hi else None,
                                  skip_group_check=True)
                    if ic == NTN - 1 and j < NT - 1:
                        m.then_inc(s_avj, 1)
                    if j == NT - 1 and (ic == NTN - 1 or
                                        (NTN > 1 and ic == NTN - 1 - AVB and False)):
                        pass
                if j == NT - 1:
                    m.then_inc(sgd[2], 1)
                    for b in avbanks:
                        ngrp[b] += 2 if (NTN > 1) else 1
                    sgdcnt[2] += 1
                    rec["attav_done"] = sgdcnt[2]
                    rec["avevac"] = {b: ngrp[b] for b in avbanks}
            w1_used = 0
            w2_used = 0
            gel_total = 0
            npass = 0
            nw2c = 0
            rec["w2done"] = []
            for e in range(NE):
                te.wait_ge(s_xeb[e % 2], 16 * KD * (e // 2 + 1))
                for m_ in range(FC):
                    w1_used += 1
                    te.wait_ge(s_w1b[(w1_used - 1) % NW1BUF],
                               16 * ((w1_used - 1) // NW1BUF + 1))
                    bs = (m_ % 2) * NNX
                    for ni in range(NNX):
                        grp_begin(te, bs + ni)
                    for k in range(KD):
                        off = 0
                        for ni, nsz in enumerate(NXC):
                            b = bs + ni
                            mm = te.matmul(
                                bank[b][:, 0:nsz],
                                w1b_sb[(w1_used - 1) % NW1BUF][:, k * 128:(k + 1) * 128],
                                xe_sb[e % 2][:, k * C + off: k * C + off + nsz],
                                start=(k == 0), stop=(k == KD - 1))
                            if k == KD - 1:
                                mm.then_inc(sgd[b], 1)
                            off += nsz
                    for ni in range(NNX):
                        sgdcnt[bs + ni] += 1
                        rec["l1"][(e, m_, ni)] = (bs + ni, sgdcnt[bs + ni],
                                                  ngrp[bs + ni] + 1)
                        ngrp[bs + ni] += 1
                first_pass = True
                for half in range(2):
                    for grp in mgroups():
                        for mi in range(len(grp)):
                            grp_begin(te, mi % 8)
                        for k2 in range(FC):
                            if first_pass:
                                for ni in range(NNX):
                                    hb, _, hn = rec["l1"][(e, k2, ni)]
                                    te.wait_ge(sbf[hb], hn)
                            if k2 % W2KG == 0:
                                w2_used += 1
                                te.wait_ge(s_w2b[(w2_used - 1) % NW2BUF],
                                           16 * ((w2_used - 1) // NW2BUF + 1))
                            ws = w2b_sb[(w2_used - 1) % NW2BUF][:, (k2 % W2KG) * 512:
                                                                (k2 % W2KG + 1) * 512]
                            blk_end = (k2 % W2KG == W2KG - 1 or k2 == FC - 1)
                            for mi, m_ in enumerate(grp):
                                b = mi % 8
                                mm = te.matmul(
                                    bank[b][:, :],
                                    h_sb[:, k2 * C + m_ * 128: k2 * C + (m_ + 1) * 128],
                                    ws,
                                    start=(k2 == 0), stop=(k2 == FC - 1))
                                last_mi = (mi == len(grp) - 1)
                                if k2 == FC - 1 and last_mi:
                                    mm.then_inc(s_l2g, 1)
                                elif blk_end and last_mi:
                                    mm.then_inc(s_w2c, 1)
                            if blk_end:
                                if k2 == FC - 1:
                                    rec["w2done"].append(("g", npass + 1))
                                else:
                                    nw2c += 1
                                    rec["w2done"].append(("c", nw2c))
                        npass += 1
                        for mi in range(len(grp)):
                            b = mi % 8
                            ngrp[b] += 1
                            rec["l2"].append((b, npass, ngrp[b]))
                        first_pass = False

        @block.vector
        def _(ve):
            ve.memset(off_sb[:], 0.0)
            ve.wait_ge(s_wc, 16 * 3)
            ve.drain()
            ve.tensor_scalar_add(dumpc[:], iotap[:], float(NE * C))
            ve.memset(zt_sb[:], 0).then_inc(s_zt, 1)
            for i in range(NT):
                ve.wait_ge(s_xb[i % 3], 16 * (i // 3 + 1))
                xt = xt_sb[i % 3]
                ve.reduce_sum(statm[:, i:i + 1], xt[:], axis=AX.X)
                ve.scalar_tensor_tensor(scr_sb[:], xt[:], 1.0, xt[:],
                                        ALU.mult, ALU.mult,
                                        accum_out=statv[:, i:i + 1]) \
                    .then_inc(s_stat, 1)
            ve.drain()
            ve.tensor_scalar_mul(statm[:], statm[:], 1.0 / D)
            ve.tensor_scalar_mul(statv[:], statv[:], 1.0 / D)
            ve.drain()
            ve.tensor_tensor(statt[:], statm[:], statm[:], ALU.mult)
            ve.drain()
            ve.tensor_sub(statv[:], statv[:], statt[:])
            ve.drain()
            ve.tensor_scalar_add(statv[:], statv[:], EPS).then_inc(s_stat, 1)
            ve.wait_ge(s_sq, 1)
            ve.reciprocal(statr[:], statt[:])
            ve.drain()
            ve.tensor_tensor(statnb[:], statm[:], statr[:], ALU.mult)
            ve.drain()
            ve.tensor_scalar_mul(statnb[:], statnb[:], -1.0).then_inc(s_rst, 1)
            qki = 0
            for dst in (qT, kT):
                for n in range(NTN):
                    b, g = rec["qk"][qki]; qki += 1
                    ve.wait_ge(sgd[b], g)
                    ve.tensor_copy(dst[0:HD, n * 512:(n + 1) * 512],
                                   bank[b][0:HD, 0:512]).then_inc(sbf[b], 1)
            ve.sem_inc(s_qkd, 1)
            for i in range(NT):
                b, g = rec["vr"][i]
                ve.wait_ge(sgd[b], g)
                ve.tensor_copy(v_sb[:, i * HD:(i + 1) * HD], bank[b][0:128, 0:HD])
                mlast = ve.tensor_copy(logit_sb[:, i * NE:(i + 1) * NE],
                                       bank[b][0:128, HD:72])
                mlast.then_inc(sbf[b], 1)
                if i == NT - 1:
                    ve.memset(colA[:], 0.0).then_inc(s_vs, 1)
            ve.memset(ones_b[:], 1.0)
            ve.wait_ge(s_gpio, 1)
            ve.wait_ge(s_sv, 1)
            ve.reduce_sum(colsB[:, 0:NT], expv[:].rearrange("p (t e) -> p t e", e=NE),
                          axis=AX.X)
            ve.reduce_max(colsB[:, NT:2 * NT],
                          expv[:].rearrange("p (t e) -> p t e", e=NE), axis=AX.X)
            ve.drain()
            ve.reciprocal(colsB[:, 2 * NT:3 * NT], colsB[:, 0:NT])
            for i in range(NT):
                ve.tensor_scalar_mul(scrA[:, i * NE:(i + 1) * NE],
                                     ones_b[:, 0:NE], colsB[:, NT + i:NT + i + 1])
            ve.drain()
            ve.tensor_tensor(scrB[:], expv[:], scrA[:], ALU.is_ge)
            ve.drain()
            ve.tensor_tensor(scrC[:], scrB[:], bmi_b[:], ALU.mult)
            ve.drain()
            ve.reduce_max(colsB[:, 3 * NT:4 * NT],
                          scrC[:].rearrange("p (t e) -> p t e", e=NE), axis=AX.X)
            ve.drain()
            ve.tensor_scalar(colsB[:, 3 * NT:4 * NT], colsB[:, 3 * NT:4 * NT],
                             -1.0, 128.0, ALU.mult, ALU.add)
            ve.drain()
            for i in range(NT):
                ve.tensor_scalar_mul(scrA[:, i * NE:(i + 1) * NE],
                                     ones_b[:, 0:NE],
                                     colsB[:, 3 * NT + i:3 * NT + i + 1])
            ve.drain()
            ve.tensor_tensor(ind_b[:].rearrange("p (t k e) -> p t k e", k=2, e=NE)
                             [:, :, 0, :],
                             iotaB[:].rearrange("p (t e) -> p t e", e=NE),
                             scrA[:].rearrange("p (t e) -> p t e", e=NE),
                             ALU.is_equal)
            ve.drain()
            ve.scalar_tensor_tensor(scrB[:],
                                    ind_b[:].rearrange("p (t k e) -> p (t k e)"),
                                    1.0, expv[:], ALU.mult, ALU.mult) \
                if False else None
            ve.scalar_tensor_tensor(
                scrB[:],
                ind_b[:].rearrange("p (t k e) -> p t k e", k=2, e=NE)[:, :, 0, :],
                BIGNEG, expv[:], ALU.mult, ALU.add)
            ve.drain()
            ve.reduce_max(colsB[:, 4 * NT:5 * NT],
                          scrB[:].rearrange("p (t e) -> p t e", e=NE), axis=AX.X)
            ve.drain()
            for i in range(NT):
                ve.tensor_scalar_mul(scrA[:, i * NE:(i + 1) * NE],
                                     ones_b[:, 0:NE],
                                     colsB[:, 4 * NT + i:4 * NT + i + 1])
            ve.drain()
            ve.tensor_tensor(scrC[:], scrB[:], scrA[:], ALU.is_ge)
            ve.drain()
            ve.tensor_tensor(scrC[:], scrC[:], bmi_b[:], ALU.mult)
            ve.drain()
            ve.reduce_max(colsB[:, 5 * NT:6 * NT],
                          scrC[:].rearrange("p (t e) -> p t e", e=NE), axis=AX.X)
            ve.drain()
            ve.tensor_scalar(colsB[:, 5 * NT:6 * NT], colsB[:, 5 * NT:6 * NT],
                             -1.0, 128.0, ALU.mult, ALU.add)
            ve.drain()
            for i in range(NT):
                ve.tensor_scalar_mul(scrA[:, i * NE:(i + 1) * NE],
                                     ones_b[:, 0:NE],
                                     colsB[:, 5 * NT + i:5 * NT + i + 1])
            ve.drain()
            ve.tensor_tensor(ind_b[:].rearrange("p (t k e) -> p t k e", k=2, e=NE)
                             [:, :, 1, :],
                             iotaB[:].rearrange("p (t e) -> p t e", e=NE),
                             scrA[:].rearrange("p (t e) -> p t e", e=NE),
                             ALU.is_equal)
            ve.tensor_tensor(colsB[:, 6 * NT:7 * NT], colsB[:, NT:2 * NT],
                             colsB[:, 2 * NT:3 * NT], ALU.mult)
            ve.tensor_tensor(colsB[:, 7 * NT:8 * NT], colsB[:, 4 * NT:5 * NT],
                             colsB[:, 2 * NT:3 * NT], ALU.mult)
            ve.drain()
            eview = eidx_sb[:].rearrange("p (t k) -> p t k", k=2)
            wview = wco_sb[:].rearrange("p (t k) -> p t k", k=2)
            ve.tensor_copy(eview[:, :, 0], colsB[:, 3 * NT:4 * NT])
            ve.tensor_copy(eview[:, :, 1], colsB[:, 5 * NT:6 * NT])
            ve.tensor_copy(wview[:, :, 0], colsB[:, 6 * NT:7 * NT])
            ve.tensor_copy(wview[:, :, 1], colsB[:, 7 * NT:8 * NT])
            ve.tensor_copy(indp[:], ind_b[:]).then_inc(s_ind, 1)
            for i in range(NT):
                (b, g), (b2, g2) = rec["rank"][i]
                ve.wait_ge(sgd[b], g)
                ve.tensor_sub(rank_sb[:, i * 16:(i + 1) * 16], bank[b][:, 0:16],
                              ind_b[:, i * 16:(i + 1) * 16]).then_inc(sbf[b], 1)
                ve.wait_ge(sgd[b2], g2)
                ve.drain()
                ve.tensor_add(off_sb[0:1, :], off_sb[0:1, :],
                              bank[b2][0:1, 0:16]).then_inc(s_off, 1)
            ve.wait_ge(s_cntb, 16)
            for i in range(NT):
                ve.tensor_copy(cnt_exp[:, i * NE:(i + 1) * NE], cnt_bc[:, 0:NE])
            ve.drain()
            indv = ind_b[:].rearrange("p (t k e) -> p t k e", k=2, e=NE)
            rkv = rank_sb[:].rearrange("p (t k e) -> p t k e", k=2, e=NE)
            cB = colsB
            ve.tensor_tensor(scrA[:].rearrange("p (t e) -> p t e", e=NE),
                             indv[:, :, 0, :], rkv[:, :, 0, :], ALU.mult)
            ve.tensor_tensor(scrB[:].rearrange("p (t e) -> p t e", e=NE),
                             indv[:, :, 1, :], rkv[:, :, 1, :], ALU.mult)
            ve.drain()
            ve.reduce_sum(cB[:, 0:NT], scrA[:].rearrange("p (t e) -> p t e", e=NE),
                          axis=AX.X)
            ve.reduce_sum(cB[:, NT:2 * NT],
                          scrB[:].rearrange("p (t e) -> p t e", e=NE), axis=AX.X)
            ve.drain()
            ve.tensor_tensor(scrA[:].rearrange("p (t e) -> p t e", e=NE),
                             indv[:, :, 1, :],
                             cnt_exp[:].rearrange("p (t e) -> p t e", e=NE), ALU.mult)
            ve.drain()
            ve.reduce_sum(cB[:, 2 * NT:3 * NT],
                          scrA[:].rearrange("p (t e) -> p t e", e=NE), axis=AX.X)
            ve.drain()
            ve.tensor_add(cB[:, 2 * NT:3 * NT], cB[:, 2 * NT:3 * NT],
                          cB[:, NT:2 * NT])
            ve.drain()
            rkst = rk_sb[:].rearrange("p (t k) -> p t k", k=2)
            sctst = sct_sb[:].rearrange("p (t k) -> p t k", k=2)
            rslst = rsl_sb[:].rearrange("p (t k) -> p t k", k=2)
            for kk in range(2):
                slot = cB[:, (0 if kk == 0 else 2) * NT:(1 if kk == 0 else 3) * NT]
                rkl = cB[:, (0 if kk == 0 else 1) * NT:(1 if kk == 0 else 2) * NT]
                ecol = cB[:, (3 if kk == 0 else 5) * NT:(4 if kk == 0 else 6) * NT]
                ovf = cB[:, 8 * NT:9 * NT]
                t1 = cB[:, 9 * NT:10 * NT]
                t2 = cB[:, 10 * NT:11 * NT]
                ve.tensor_scalar(ovf, slot, float(C), None, ALU.is_ge)
                ve.drain()
                ve.scalar_tensor_tensor(t1, ovf, 1.0e9, rkl, ALU.mult, ALU.add)
                ve.drain()
                ve.tensor_copy(rkst[:, :, kk], t1)
                ve.scalar_tensor_tensor(t1, ecol, float(C), slot, ALU.mult, ALU.add)
                ve.tensor_scalar_mul(t2, ones_b[:, 0:NT], dumpc[:])
                ve.drain()
                ve.tensor_sub(t2, t2, t1)
                ve.drain()
                ve.tensor_tensor(t2, t2, ovf, ALU.mult)
                ve.drain()
                ve.tensor_add(t2, t2, t1)
                ve.drain()
                ve.tensor_copy(sctst[:, :, kk], t2)
                ve.tensor_scalar_min(t1, slot, float(C - 1))
                ve.drain()
                ve.scalar_tensor_tensor(t1, ecol, float(C), t1, ALU.mult, ALU.add)
                ve.drain()
                mcp = ve.tensor_copy(rslst[:, :, kk], t1)
            mcp.then_inc(s_slot, 1)
            for j in range(NT + 1):
                if j < NT:
                    if j >= 2:
                        ve.wait_ge(s_exp, j - 1)
                    for ic in range(NTN):
                        b, g = rec["attqk"][j * NTN + ic]
                        ve.wait_ge(sgd[b], g)
                        ve.tensor_scalar(att16[j % 2][:, ic * 512:(ic + 1) * 512],
                                         bank[b][:, :], 0.0, None, ALU.add, ALU.max,
                                         accum_out=cmax[:, ic:ic + 1]) \
                            .then_inc(sbf[b], 1)
                    ve.drain()
                    ve.reduce_max(negmax[:, j % 2:j % 2 + 1], cmax[:, 0:NTN],
                                  axis=AX.X, negate=True).then_inc(s_sm, 1)
                if j >= 1:
                    jj = j - 1
                    ve.wait_ge(s_exp, jj + 1)
                    ve.reciprocal(vrec[:], den[:])
                    ve.drain()
                    if jj >= 1:
                        ve.wait_ge(s_avj, jj)
                    ve.tensor_scalar_mul(vn_sb[:], v_sb[:, jj * HD:(jj + 1) * HD],
                                         vrec[:]).then_inc(s_vn, 1)
            ve.wait_ge(sgd[2], rec["attav_done"])
            AVB = max(1, NTN // 2)
            for ic in range(NTN):
                b = 2 + (ic % AVB)
                hi = ic >= AVB and NTN > 1
                ve.tensor_copy(avacc[0:HD, ic * 512:(ic + 1) * 512],
                               bank[b][64:128, 0:512] if hi
                               else bank[b][0:HD, 0:512]).then_inc(sbf[b], 1)
            ve.wait_ge(sgd[6], rec["base_g"])
            ve.tensor_copy(base_row[:], bank[6][0:1, 0:16]).then_inc(s_basb, 1)
            ve.wait_ge(s_basb, 17)
            indpv = indp[:].rearrange("p (t k e) -> p t k e", k=2, e=NE)
            for kk in range(2):
                gsel = p6cols[:, 0:NT]
                for i in range(NT):
                    ve.tensor_copy(p6A[:, i * NE:(i + 1) * NE],
                                   base_bc[:, kk * NE:(kk + 1) * NE])
                ve.drain()
                ve.tensor_tensor(p6B[:].rearrange("p (t e) -> p t e", e=NE),
                                 indpv[:, :, kk, :],
                                 p6A[:].rearrange("p (t e) -> p t e", e=NE), ALU.mult)
                ve.drain()
                ve.reduce_sum(gsel, p6B[:].rearrange("p (t e) -> p t e", e=NE),
                              axis=AX.X)
                ve.drain()
                ve.tensor_add(gsel, gsel,
                              rk_sb[:].rearrange("p (t k) -> p t k", k=2)[:, :, kk])
                ve.drain()
                ve.tensor_scalar(gsel, gsel, float(CAPACITY), None, ALU.is_lt)
                ve.drain()
                ve.tensor_tensor(gsel, gsel,
                                 wco_sb[:].rearrange("p (t k) -> p t k", k=2)[:, :, kk],
                                 ALU.mult)
                ve.drain()
                ve.tensor_copy(coef_b[:].rearrange("p (t k) -> p t k", k=2)[:, :, kk],
                               gsel)
            ve.drain()
            yn = 0
            for b, npass_, gcount in rec["l2"]:
                yn += 1
                if yn > 4:
                    ve.wait_ge(s_yewb[(yn - 4 - 1) % 4], 16 * ((yn - 4 - 1) // 4 + 1))
                ve.wait_ge(s_l2g, npass_)
                ve.tensor_copy(ye_sb[yn % 4][:, :], bank[b][:, :]).then_inc(sbf[b], 1)
            ve.wait_ge(s_stw, 16 * NT)
            cview = coef_b[:].rearrange("p (t k) -> p t k", k=2)
            for i in range(NT):
                ve.wait_ge(s_ygtb[(2 * i) % 4], 16 * ((2 * i) // 4 + 1))
                ve.wait_ge(s_ygtb[(2 * i + 1) % 4], 16 * ((2 * i + 1) // 4 + 1))
                ve.wait_ge(s_xcb[i % 2], 16 * (i // 2 + 1))
                if i >= 1:
                    ve.wait_ge(s_gsil, i)
                ve.scalar_tensor_tensor(g_sb[:], yg_sb[(2 * i) % 4][:],
                                        coef_b[:, 2 * i:2 * i + 1],
                                        bias_bc[:], ALU.mult, ALU.add)
                ve.drain()
                ve.scalar_tensor_tensor(g_sb[:], yg_sb[(2 * i + 1) % 4][:],
                                        coef_b[:, 2 * i + 1:2 * i + 2],
                                        g_sb[:], ALU.mult, ALU.add).then_inc(s_comb, 1)
                ve.wait_ge(s_gsil, i + 1)
                ob = o_sb[i % 2]
                for c in range(D // HD):
                    ve.tensor_mul(ob[:, c * HD:(c + 1) * HD],
                                  ob[:, c * HD:(c + 1) * HD],
                                  s_sb[:, i * HD:(i + 1) * HD])
                ve.drain()
                ve.tensor_add(ob[:], ob[:], xc_sb[i % 2][:]).then_inc(s_out, 1)

        @block.scalar
        def _(sc):
            sc.wait_ge(s_zt, 1)
            for z in range(NZF):
                r0 = z * 128
                r1 = min(r0 + 128, ZROWS)
                sc.dma_start(out=xe_d[r0:r1, :], in_=zt_sb[0:r1 - r0, :]) \
                    .then_inc(s_zf, 16)
            sc.wait_ge(s_stat, NT + 1)
            sc.activation(statt[:], statv[:], AF.Sqrt,
                          bias=0.0, scale=1.0).then_inc(s_sq, 1)
            sc.wait_ge(s_rst, 1)
            cntA = [len([t for t in range(NT) if t % 3 == p]) for p in range(3)]
            for i in range(NT):
                sc.wait_ge(s_xb[i % 3], 16 * (cntA[i % 3] + i // 3 + 1))
                if i >= 2:
                    sc.wait_ge(s_xnwb[i % 2], 16 * (i // 2))
                sc.activation(xnt_sb[i % 2][:], xt_sb[i % 3][:], AF.Identity,
                              bias=statnb[:, i:i + 1],
                              scale=statr[:, i:i + 1]).then_inc(s_xn, 1)
            sc.wait_ge(s_vs, 1)
            sc.activation(expv[:], logit_sb[:], AF.Exp).then_inc(s_sv, 1)
            for j in range(NT):
                sc.wait_ge(s_sm, j + 1)
                if j >= 1:
                    sc.wait_ge(s_vn, j)
                if j >= 2:
                    sc.wait_ge(s_avj, j - 1)
                sc.activation(e_sb[j % 2][:], att16[j % 2][:], AF.Exp,
                              bias=negmax[:, j % 2:j % 2 + 1], scale=1.0,
                              accum_out=den[:]).then_inc(s_exp, 1)
            for b in sorted(rec["avevac"]):
                sc.wait_ge(sbf[b], rec["avevac"][b])
            sc.dma_start(out=av_d, in_=avacc[:]).then_inc(s_av2d, 16)
            for e in range(NE):
                for m_ in range(FC):
                    off = 0
                    for ni, nsz in enumerate(NXC):
                        b, g, _ = rec["l1"][(e, m_, ni)]
                        sc.wait_ge(sgd[b], g)
                        sc.activation(h_sb[:, m_ * C + off: m_ * C + off + nsz],
                                      bank[b][:, 0:nsz], AF.Gelu_apprx_tanh) \
                            .then_inc(sbf[b], 1)
                        off += nsz
            for i in range(NT):
                sc.wait_ge(s_comb, i + 1)
                if i >= 2:
                    sc.wait_ge(s_owb[i % 2], 16 * ((i - 2) // 2 + 1))
                sc.activation(o_sb[i % 2][:], g_sb[:], AF.Silu).then_inc(s_gsil, 1)

        @block.gpsimd
        def _(gp):
            gp.iota(iota8[:], [[1, NE]], base=0, channel_multiplier=0,
                    allow_small_or_imprecise_dtypes=True)
            gp.iota(iotap[:], [[0, 1]], base=0, channel_multiplier=1,
                    allow_small_or_imprecise_dtypes=True)
            gp.iota(bmi[:], [[-1, NE]], base=128, channel_multiplier=0,
                    allow_small_or_imprecise_dtypes=True)
            gp.dma_start(out=wq_sb[:].rearrange("p (k n) -> p k n", k=KD),
                         in_=wq_d).then_inc(s_wc, 16)
            gp.dma_start(out=wk_sb[:].rearrange("p (k n) -> p k n", k=KD),
                         in_=wk_d).then_inc(s_wc, 16)
            gp.dma_start(out=wvr_sb[:].rearrange("p (k n) -> p k n", k=KD),
                         in_=wvr_d).then_inc(s_wc, 16)
            gp.wait_ge(s_xn, NT)
            gp.iota(iotaB[:], [[0, NT], [1, NE]], base=0, channel_multiplier=0,
                    allow_small_or_imprecise_dtypes=True)
            gp.iota(bmi_b[:], [[0, NT], [-1, NE]], base=128, channel_multiplier=0,
                    allow_small_or_imprecise_dtypes=True) if False else None
            gp.iota(bmi_b[:], [[0, NT], [-1, NE]], base=128,
                    channel_multiplier=0,
                    allow_small_or_imprecise_dtypes=True).then_inc(s_gpio, 1)
            gp.wait_ge(s_off, NT)
            gp.dma_start(out=ccin_d, in_=off_sb[0:1, :]).then_inc(s_ccw, 16)
            gp.wait_ge(s_ccw, 16)
            gp.collective_compute(
                "AllGather", ALU.bypass, replica_groups=[list(range(n_cores))],
                ins=[ccin_d.opt()], outs=[ccout_d.opt()],
            ).then_inc(s_cc, 1)
            gp.wait_ge(s_cc, 1)
            gp.dma_start(out=ccall[:], in_=ccout_d).then_inc(s_ccb, 16)
            gp.dma_start(out=prefm[:], in_=prefm_d).then_inc(s_ccb, 16)
            for i in range(NT):
                gp.wait_ge(s_xwb[i % 4], 16 * (i // 4 + 1))
                gp.wait_ge(s_slot, 1)
                if i == 0:
                    gp.wait_ge(s_zf, 16 * NZF)
                for kk in range(2):
                    gp.indirect_dma_start(
                        xe_d,
                        IndirectOffsetOnAxis(ap=sct_sb[:, 2 * i + kk:2 * i + kk + 1],
                                             axis=0),
                        xw_sb[i % 4][:], None,
                    ).then_inc(s_scatb[i % 4], 16)
            gp.wait_ge(s_basb, 1)
            gp.dma_start(out=ccbase_d, in_=base_row[:]).then_inc(s_ccb, 16)
            w1i = 0
            w2i = 0
            for e in range(NE):
                for m_ in range(FC):
                    w1i += 1
                    if w1i > 2:
                        jb = w1i - 2
                        eb, mb = (jb - 1) // FC, (jb - 1) % FC
                        wb, wg, _ = rec["l1"][(eb, mb, NNX - 1)]
                        gp.wait_ge(sgd[wb], wg)
                    gp.dma_start(out=w1b_sb[(w1i - 1) % NW1BUF][:].rearrange(
                        "p (k f) -> p k f", k=KD),
                        in_=w1_d[e, m_]).then_inc(s_w1b[(w1i - 1) % NW1BUF], 16)
                for half in range(2):
                    for grp in mgroups():
                        for kb in range(NW2B):
                            w2i += 1
                            if w2i > NW2BUF:
                                kind, val = rec["w2done"][w2i - 1 - NW2BUF]
                                gp.wait_ge(s_w2c if kind == "c" else s_l2g, val)
                            gp.dma_start(
                                out=w2b_sb[(w2i - 1) % NW2BUF][:].rearrange(
                                    "p (k j) -> p k j", k=W2KG),
                                in_=w2_d[e, kb * W2KG:(kb + 1) * W2KG, half]
                                .rearrange("k p j -> p k j"),
                            ).then_inc(s_w2b[(w2i - 1) % NW2BUF], 16)
            nyes = NE * 2 * MC
            for p in range(4):
                tot = nyes // 4 + (1 if p < nyes % 4 else 0)
                gp.wait_ge(s_yewb[p], 16 * tot)
            for i in range(NT):
                for kk in range(2):
                    n = 2 * i + kk
                    if n >= 4:
                        gp.wait_ge(s_comb, (n - 4) // 2 + 1)
                    gp.indirect_dma_start(
                        yg_sb[n % 4][:], None,
                        ye_d,
                        IndirectOffsetOnAxis(ap=rsl_sb[:, n:n + 1], axis=0),
                    ).then_inc(s_ygtb[n % 4], 16)

        @block.sync
        def _(sy):
            sy.dma_start(out=bias_bc[:],
                         in_=bias_d.broadcast_to([128, D])).then_inc(s_cl, 16)
            sy.dma_start(out=identf[:], in_=identf_d).then_inc(s_cl, 16)
            sy.dma_start(out=identb[:], in_=identb_d).then_inc(s_cl, 16)
            sy.dma_start(out=lincl[:], in_=lincl_d).then_inc(s_cl, 16)
            sy.dma_start(out=sel127[:], in_=sel127_d).then_inc(s_cl, 16)
            sy.dma_start(out=onescol[:], in_=onescol_d).then_inc(s_cl, 16)
            for i in range(NT):
                if i >= 3:
                    sy.wait_ge(s_stat, i - 2)
                sy.dma_start(out=xt_sb[i % 3][:],
                             in_=x_d[i * 128:(i + 1) * 128, :]).then_inc(s_xb[i % 3], 16)
            for i in range(NT):
                if i < 3:
                    iA = max(t for t in range(NT) if t % 3 == i % 3)
                    sy.wait_ge(s_stat, iA + 1)
                else:
                    sy.wait_ge(s_xn, i - 2)
                sy.dma_start(out=xt_sb[i % 3][:],
                             in_=x_d[i * 128:(i + 1) * 128, :]).then_inc(s_xb[i % 3], 16)
                if i >= 1:
                    sy.wait_ge(s_xn, i)
                    sy.dma_start(out=xn_d[(i - 1) * 128:i * 128, :],
                                 in_=xnt_sb[(i - 1) % 2][:]) \
                        .then_inc(s_xnwb[(i - 1) % 2], 16)
            sy.wait_ge(s_xn, NT)
            sy.dma_start(out=xn_d[(NT - 1) * 128:NT * 128, :],
                         in_=xnt_sb[(NT - 1) % 2][:]) \
                .then_inc(s_xnwb[(NT - 1) % 2], 16)
            sy.wait_ge(s_xnwb[0], 16 * ((NT + 1) // 2))
            sy.wait_ge(s_xnwb[1], 16 * (NT // 2))
            for c in range(KD):
                sy.dma_start(out=xnfm[:, c * S:(c + 1) * S],
                             in_=xn_d[:, c * 128:(c + 1) * 128],
                             transpose=True).then_inc(s_xnfm, 16)
            if debug_taps:
                sy.wait_ge(s_xnfm, 16 * KD)
                sy.dma_start(out=dbg["dbg_xnfm"], in_=xnfm[:]).then_inc(s_cl, 16)
                sy.wait_ge(s_qkd, 1)
                sy.dma_start(out=dbg["dbg_qT"], in_=qT[:]).then_inc(s_cl, 16)
                sy.dma_start(out=dbg["dbg_kT"], in_=kT[:]).then_inc(s_cl, 16)
            sy.wait_ge(s_ccw, 16)
            sy.dma_start(out=cnt_bc[:],
                         in_=ccin_d.broadcast_to([128, 16])).then_inc(s_cntb, 16)
            for i in range(NT):
                if i >= 4:
                    sy.wait_ge(s_scatb[i % 4], 32 * (i // 4))
                sy.dma_start(out=xw_sb[i % 4][:],
                             in_=xn_d[i * 128:(i + 1) * 128, :]) \
                    .then_inc(s_xwb[i % 4], 16)
            sy.wait_ge(s_ccb, 16 * 3)
            sy.dma_start(out=base_bc[:],
                         in_=ccbase_d.broadcast_to([128, 16])).then_inc(s_basb, 16)
            def emit_ye_stores(e, yewn):
                for half in range(2):
                    for grp in mgroups():
                        for m_ in grp:
                            yewn += 1
                            yb, ynp, ygc = rec["l2"][yewn - 1]
                            sy.wait_ge(sbf[yb], ygc)
                            sy.dma_start(
                                out=ye_d[e * C + m_ * 128: e * C + (m_ + 1) * 128,
                                         half * 512:(half + 1) * 512],
                                in_=ye_sb[yewn % 4][:, :]) \
                                .then_inc(s_yewb[(yewn - 1) % 4], 16)
                return yewn

            yewn = 0
            for e in range(NE):
                if e == 0:
                    for p in range(4):
                        tot = 2 * (NT // 4 + (1 if p < NT % 4 else 0))
                        sy.wait_ge(s_scatb[p], 16 * tot)
                if e == 1:
                    qb, qg = rec["attqk"][-1]
                    sy.wait_ge(sgd[qb], qg)
                if e == 2:
                    sy.wait_ge(s_av2d, 16)
                    for i in range(NT):
                        sy.dma_start(out=s_sb[:, i * HD:(i + 1) * HD],
                                     in_=av_d[0:HD, i * 128:(i + 1) * 128],
                                     transpose=True).then_inc(s_stw, 16)
                if e >= 2:
                    wb, wg, _ = rec["l1"][(e - 2, FC - 1, NNX - 1)]
                    sy.wait_ge(sgd[wb], wg)
                for k in range(KD):
                    sy.dma_start(out=xe_sb[e % 2][:, k * C:(k + 1) * C],
                                 in_=xe_d[e * C:(e + 1) * C, k * 128:(k + 1) * 128],
                                 transpose=True).then_inc(s_xeb[e % 2], 16)
                if e >= 1:
                    yewn = emit_ye_stores(e - 1, yewn)
            yewn = emit_ye_stores(NE - 1, yewn)
            for i in range(NT):
                if i >= 2:
                    sy.wait_ge(s_out, i - 1)
                sy.dma_start(out=xc_sb[i % 2][:],
                             in_=x_d[i * 128:(i + 1) * 128, :]) \
                    .then_inc(s_xcb[i % 2], 16)
                if i >= 1:
                    sy.wait_ge(s_out, i)
                    sy.dma_start(out=out_d[(i - 1) * 128:i * 128, :],
                                 in_=o_sb[(i - 1) % 2][:]) \
                        .then_inc(s_owb[(i - 1) % 2], 16)
            sy.wait_ge(s_out, NT)
            sy.dma_start(out=out_d[(NT - 1) * 128:NT * 128, :],
                         in_=o_sb[(NT - 1) % 2][:]).then_inc(s_owb[(NT - 1) % 2], 16)
            sy.wait_ge(s_owb[0], 16 * ((NT + 1) // 2))
            sy.wait_ge(s_owb[1], 16 * (NT // 2))
            if debug_taps:
                ndbg = 0
                sy.dma_start(out=dbg["dbg_xnd"], in_=xn_d).then_inc(s_cl, 16)
                sy.dma_start(out=dbg["dbg_av"], in_=avacc[:]).then_inc(s_cl, 16)
                ndbg += 1
                for name, src_sb in (("dbg_s", s_sb), ("dbg_logit", logit_sb),
                                     ("dbg_wco", wco_sb), ("dbg_eidx", eidx_sb),
                                     ("dbg_rk", rk_sb), ("dbg_rsl", rsl_sb),
                                     ("dbg_sct", sct_sb), ("dbg_base", base_bc),
                                     ("dbg_rank", rank_sb)):
                    sy.dma_start(out=dbg[name], in_=src_sb[:]).then_inc(s_cl, 16)
                    ndbg += 1
                sy.wait_ge(s_cl, 16 * 6 + 16 * (ndbg + 4))

    es.close()
    return nc


def _host_layouts(inputs):
    x = np.ascontiguousarray(inputs["L0_attn_x"])
    wq = np.ascontiguousarray(inputs["L0_attn_Wq"])
    wk = np.ascontiguousarray(inputs["L0_attn_Wk"])
    wv = np.ascontiguousarray(inputs["L0_attn_Wv"])
    wr = np.ascontiguousarray(inputs["L0_ffn_router_w"])
    w1 = np.ascontiguousarray(inputs["L0_ffn_w1"])
    w2 = np.ascontiguousarray(inputs["L0_ffn_w2"])
    bias = np.ascontiguousarray(inputs["L0_ffn_bias"]).reshape(1, D)
    S = x.shape[1]
    F = w1.shape[2]
    FC = F // 128

    def chunk_pk(w, ncols):
        return np.ascontiguousarray(w.reshape(KD, 128, ncols).transpose(1, 0, 2))

    wvr = np.concatenate([wv, wr], axis=1)
    w1h = np.ascontiguousarray(
        w1.reshape(NE, KD, 128, FC, 128).transpose(0, 3, 2, 1, 4))
    w2h = np.ascontiguousarray(
        w2.reshape(NE, FC, 128, 2, 512).transpose(0, 1, 3, 2, 4))
    identf = np.eye(128, dtype=np.float32)
    import ml_dtypes
    identb = np.eye(128).astype(ml_dtypes.bfloat16)
    lincl = np.triu(np.ones((128, 128), np.float32))
    sel127 = np.zeros((128, 128), np.float32)
    sel127[0, :] = 1.0
    onescol = np.ones((128, 1), np.float32)
    common = dict(
        wqh=chunk_pk(wq, HD), wkh=chunk_pk(wk, HD), wvrh=chunk_pk(wvr, 72),
        w1h=w1h, w2h=w2h, bias1=bias,
        identf=identf, identb=identb, lincl=lincl, sel127=sel127, onescol=onescol,
    )
    in_maps = []
    for c in range(x.shape[0]):
        prefm = np.zeros((NE, 1), np.float32)
        prefm[:c, 0] = 1.0
        in_maps.append(dict(common, x=x[c], prefm=prefm))
    return in_maps, S, F


def kernel(**inputs):
    in_maps, S, F = _host_layouts(inputs)
    nc = build_program(S=S, F=F, C=1152, n_cores=8)
    res = run_bass_kernel_spmd(nc, in_maps, core_ids=list(range(8)))
    out = np.stack([r["out"] for r in res.results], axis=0)
    return out.astype(np.float32)


# version 13
# speedup vs baseline: 1.3843x; 1.0097x over previous
import numpy as np
import concourse.bass as bass
import concourse.mybir as mybir
from concourse.bass import IndirectOffsetOnAxis
from concourse.bass_utils import run_bass_kernel_spmd

F32 = mybir.dt.float32
BF16 = mybir.dt.bfloat16
F16 = mybir.dt.float16
U32 = mybir.dt.uint32
AF = mybir.ActivationFunctionType
ALU = mybir.AluOpType
AX = mybir.AxisListType

D = 1024
KD = D // 128
HD = 64
NE = 8
EPS = 1e-5
BIGNEG = -1.0e30


def build_program(S=4096, F=4096, C=1152, n_cores=8, debug_taps=False):
    NT = S // 128
    NTN = S // 512
    FC = F // 128
    MC = C // 128
    NXC = [512] * (C // 512) + ([C % 512] if C % 512 else [])
    NNX = len(NXC)
    W2KG = min(4, FC)
    NW2B = (FC + W2KG - 1) // W2KG
    NW2BUF = 3
    NW1BUF = 3
    CAPACITY = S * n_cores // NE
    assert S % 512 == 0 and C % 128 == 0 and F % (128 * W2KG) == 0

    def mgroups():
        if MC <= 8:
            return [list(range(MC))]
        h1 = (MC + 1) // 2
        return [list(range(h1)), list(range(h1, MC))]

    nc = bass.Bass("TRN2", target_bir_lowering=False, debug=False,
                   detect_race_conditions=False)

    x_d = nc.dram_tensor("x", [S, D], F32, kind="ExternalInput").ap()
    wq_d = nc.dram_tensor("wqh", [128, KD, HD], F32, kind="ExternalInput").ap()
    wk_d = nc.dram_tensor("wkh", [128, KD, HD], F32, kind="ExternalInput").ap()
    wvr_d = nc.dram_tensor("wvrh", [128, KD, 72], F32, kind="ExternalInput").ap()
    w1_d = nc.dram_tensor("w1h", [NE, FC, 128, KD, 128], F32, kind="ExternalInput").ap()
    w2_d = nc.dram_tensor("w2h", [NE, FC, 2, 128, 512], F32, kind="ExternalInput").ap()
    bias_d = nc.dram_tensor("bias1", [1, D], F32, kind="ExternalInput").ap()
    identf_d = nc.dram_tensor("identf", [128, 128], F32, kind="ExternalInput").ap()
    identb_d = nc.dram_tensor("identb", [128, 128], BF16, kind="ExternalInput").ap()
    lincl_d = nc.dram_tensor("lincl", [128, 128], F32, kind="ExternalInput").ap()
    sel127_d = nc.dram_tensor("sel127", [128, 128], F32, kind="ExternalInput").ap()
    onescol_d = nc.dram_tensor("onescol", [128, 1], F32, kind="ExternalInput").ap()
    prefm_d = nc.dram_tensor("prefm", [NE, 1], F32, kind="ExternalInput").ap()
    out_d = nc.dram_tensor("out", [S, D], F32, kind="ExternalOutput").ap()

    xn_d = nc.dram_tensor("xn_dram", [S, D], BF16).ap()
    av_d = nc.dram_tensor("av_dram", [HD, S], BF16).ap()
    ZROWS = NE * C + 128
    xe_d = nc.dram_tensor("xe_dram", [ZROWS, D], BF16).ap()
    ye_d = nc.dram_tensor("ye_dram", [NE * C, D], BF16).ap()
    ccin_d = nc.dram_tensor("cc_in", [1, 16], F32).ap()
    ccout_d = nc.dram_tensor("cc_out", [n_cores, 16], F32).ap()
    ccbase_d = nc.dram_tensor("cc_base", [1, 16], F32).ap()
    if debug_taps:
        NTt = S // 128
        dbg = {
            "dbg_s": nc.dram_tensor("dbg_s", [128, NTt * HD], BF16, kind="ExternalOutput").ap(),
            "dbg_logit": nc.dram_tensor("dbg_logit", [128, NTt * NE], F32, kind="ExternalOutput").ap(),
            "dbg_wco": nc.dram_tensor("dbg_wco", [128, NTt * 2], F32, kind="ExternalOutput").ap(),
            "dbg_eidx": nc.dram_tensor("dbg_eidx", [128, NTt * 2], F32, kind="ExternalOutput").ap(),
            "dbg_rk": nc.dram_tensor("dbg_rk", [128, NTt * 2], F32, kind="ExternalOutput").ap(),
            "dbg_rsl": nc.dram_tensor("dbg_rsl", [128, NTt * 2], U32, kind="ExternalOutput").ap(),
            "dbg_sct": nc.dram_tensor("dbg_sct", [128, NTt * 2], U32, kind="ExternalOutput").ap(),
            "dbg_base": nc.dram_tensor("dbg_base", [128, 16], F32, kind="ExternalOutput").ap(),
            "dbg_rank": nc.dram_tensor("dbg_rank", [128, NTt * 16], F32, kind="ExternalOutput").ap(),
            "dbg_xnd": nc.dram_tensor("dbg_xnd", [S, D], BF16, kind="ExternalOutput").ap(),
            "dbg_av": nc.dram_tensor("dbg_av", [HD, S], BF16, kind="ExternalOutput").ap(),
            "dbg_xnfm": nc.dram_tensor("dbg_xnfm", [128, KD * S], BF16, kind="ExternalOutput").ap(),
            "dbg_qT": nc.dram_tensor("dbg_qT", [64, S], BF16, kind="ExternalOutput").ap(),
            "dbg_kT": nc.dram_tensor("dbg_kT", [64, S], BF16, kind="ExternalOutput").ap(),
        }

    def sbuf(name, shape, dtype):
        return nc.alloc_sbuf_tensor(name, shape, dtype)

    def nbytes(shape, dtype):
        return shape[1] * mybir.dt.size(dtype)

    h_sb = sbuf("h_sb", [128, FC * C], BF16)
    h_addr = nc.lookup_mloc(h_sb).addr
    if KD * S <= FC * C:
        xnfm = nc.alloc_sbuf_tensor_at("xn_fm", [128, KD * S], BF16, offset=h_addr)
    else:
        xnfm = sbuf("xn_fm", [128, KD * S], BF16)
    if KD * S * 2 + S * 2 <= FC * C * 2:
        qT = nc.alloc_sbuf_tensor_at("qT", [64, S], BF16,
                                     offset=h_addr + KD * S * 2)
    else:
        qT = sbuf("qT", [64, S], BF16)
    xe_sb = [sbuf(f"xe{i}", [128, KD * C], BF16) for i in range(2)]
    if S * 2 <= KD * C * 2:
        kT = nc.alloc_sbuf_tensor_at("kT", [64, S], BF16,
                                     offset=nc.lookup_mloc(xe_sb[1]).addr)
    else:
        kT = sbuf("kT", [64, S], BF16)

    R1_P1 = max(3 * 4096 + 2 * 2048 + 4096,
                10 * (S // 128) * NE * 4 + (S // 128) * 14 * 4)
    R1_P4 = 2 * S + 2 * 2 * S + 2 * 2 * S
    R1_P6 = 4 * 2048 + 5 * 4096 + 2 * (S // 128) * NE * 4 + (S // 128) * 16
    _g2off = -(-R1_P6 // 32) * 32
    _sw = max(32, -((S // 128) * -4) // 32 * 32)
    _stat0 = max(R1_P1, R1_P4, _g2off + 4096)
    R1 = _stat0 + 5 * _sw
    r1_sb = sbuf("r1_arena", [128, R1], mybir.dt.uint8)
    r1_addr = nc.lookup_mloc(r1_sb).addr

    def at(name, shape, dtype, off):
        assert off + nbytes(shape, dtype) <= R1, (name, off)
        return nc.alloc_sbuf_tensor_at(name, shape, dtype, offset=r1_addr + off)

    xt_sb = [at(f"xt{i}", [128, D], F32, i * 4096) for i in range(3)]
    NT_ = S // 128
    expv = at("expv", [128, NT_ * NE], F32, 0)
    scrA = at("scrA", [128, NT_ * NE], F32, NT_ * NE * 4)
    scrB = at("scrB", [128, NT_ * NE], F32, 2 * NT_ * NE * 4)
    scrC = at("scrC", [128, NT_ * NE], F32, 3 * NT_ * NE * 4)
    ind_b = at("ind_b", [128, NT_ * 16], F32, 4 * NT_ * NE * 4)
    iotaB = at("iotaB", [128, NT_ * NE], F32, 6 * NT_ * NE * 4)
    bmi_b = at("bmi_b", [128, NT_ * NE], F32, 7 * NT_ * NE * 4)
    cnt_exp = at("cnt_exp", [128, NT_ * NE], F32, 8 * NT_ * NE * 4)
    ones_b = at("ones_b", [128, NT_ * NE], F32, 9 * NT_ * NE * 4)
    colsB = at("colsB", [128, NT_ * 14], F32, 10 * NT_ * NE * 4)
    p6A = at("p6A", [128, NT_ * NE], F32, 4 * 2048 + 5 * 4096)
    p6B = at("p6B", [128, NT_ * NE], F32, 4 * 2048 + 5 * 4096 + NT_ * NE * 4)
    p6cols = at("p6cols", [128, NT_ * 4], F32, 4 * 2048 + 5 * 4096 + 2 * NT_ * NE * 4)
    coef_b = sbuf("coef_b", [128, NT_ * 2], F32)
    indp = sbuf("indp", [128, NT_ * 16], BF16)
    xnt_sb = [at(f"xnt{i}", [128, D], BF16, 3 * 4096 + i * 2048) for i in range(2)]
    scr_sb = at("scr_sb", [128, D], F32, 3 * 4096 + 2 * 2048)
    avacc = at("avacc", [64, S], BF16, 0)
    att16 = [at(f"att16_{i}", [128, S], F16, 2 * S + i * 2 * S) for i in range(2)]
    e_sb = [at(f"esb{i}", [128, S], BF16, 2 * S + 4 * S + i * 2 * S)
            for i in range(2)]
    yg_sb = [at(f"yg{i}", [128, D], BF16, i * 2048) for i in range(4)]
    xc_sb = [at(f"xc{i}", [128, D], F32, 4 * 2048 + i * 4096) for i in range(2)]
    o_sb = [at(f"osb{i}", [128, D], F32, 4 * 2048 + 2 * 4096 + i * 4096)
            for i in range(2)]
    g_sb = at("g_sb", [128, D], F32, 4 * 2048 + 4 * 4096)
    g2_sb = at("g2_sb", [128, D], F32, _g2off)
    gbuf = [g_sb, g2_sb]

    v_sb = sbuf("v_sb", [128, NT * HD], BF16)
    s_sb = sbuf("s_sb", [128, NT * HD], BF16)
    logit_sb = sbuf("logit_sb", [128, NT * NE], F32)
    rank_sb = sbuf("rank_sb", [128, NT * 16], F32)
    rk_sb = sbuf("rk_sb", [128, NT * 2], F32)
    eidx_sb = sbuf("eidx_sb", [128, NT * 2], F32)
    wco_sb = sbuf("wco_sb", [128, NT * 2], F32)
    sct_sb = sbuf("sct_sb", [128, NT * 2], U32)
    rsl_sb = sbuf("rsl_sb", [128, NT * 2], U32)
    ind_sb = [sbuf(f"ind{i}", [128, 16], F32) for i in range(2)]
    off_sb = sbuf("off_sb", [128, 16], F32)
    cnt_bc = sbuf("cnt_bc", [128, 16], F32)
    base_bc = sbuf("base_bc", [128, 16], F32)
    ccall = sbuf("ccall", [NE, 16], F32)
    prefm = sbuf("prefm_sb", [NE, 1], F32)
    xw_sb = [sbuf(f"xw{i}", [128, D], BF16) for i in range(4)]
    w1b_sb = [sbuf(f"w1b{i}", [128, KD * 128], BF16) for i in range(NW1BUF)]
    w2b_sb = [sbuf(f"w2b{i}", [128, W2KG * 512], BF16) for i in range(NW2BUF)]
    wq_sb = sbuf("wq_sb", [128, KD * HD], BF16)
    wk_sb = sbuf("wk_sb", [128, KD * HD], BF16)
    wvr_sb = sbuf("wvr_sb", [128, KD * 72], BF16)
    ye_sb = [sbuf(f"yesb{i}", [128, 512], BF16) for i in range(4)]
    bias_bc = sbuf("bias_bc", [128, D], F32)
    zt_sb = sbuf("zt_sb", [128, D], BF16)
    identf = sbuf("identf_sb", [128, 128], F32)
    identb = sbuf("identb_sb", [128, 128], BF16)
    lincl = sbuf("lincl_sb", [128, 128], F32)
    sel127 = sbuf("sel127_sb", [128, 128], F32)
    onescol = sbuf("onescol_sb", [128, 1], F32)
    iota8 = sbuf("iota8", [128, NE], F32)
    iotap = sbuf("iotap", [128, 1], F32)
    bmi = sbuf("bmi", [128, NE], F32)
    st1 = sbuf("st1", [128, NE], F32)
    st2 = sbuf("st2", [128, NE], F32)
    st3 = sbuf("st3", [128, NE], F32)
    st4 = sbuf("st4", [128, NE], F32)
    cmax = sbuf("cmax", [128, 16], F32)
    statm = at("statm", [128, S // 128], F32, _stat0)
    statv = at("statv", [128, S // 128], F32, _stat0 + _sw)
    statr = at("statr", [128, S // 128], F32, _stat0 + 2 * _sw)
    statnb = at("statnb", [128, S // 128], F32, _stat0 + 3 * _sw)
    statt = at("statt", [128, S // 128], F32, _stat0 + 4 * _sw)
    colA = sbuf("colA", [128, 1], F32)
    colB = sbuf("colB", [128, 1], F32)
    colC = sbuf("colC", [128, 1], F32)
    colD = sbuf("colD", [128, 1], F32)
    colE = sbuf("colE", [128, 1], F32)
    colF = sbuf("colF", [128, 1], F32)
    colG = sbuf("colG", [128, 1], F32)
    colH = sbuf("colH", [128, 1], F32)
    c0col = sbuf("c0col", [128, 1], F32)
    dumpc = sbuf("dumpc", [128, 1], F32)
    c1col = sbuf("c1col", [128, 1], F32)
    negmax = sbuf("negmax", [128, 2], F32)
    den = sbuf("den", [128, 1], F32)
    vrec = sbuf("vrec", [128, 1], F32)
    vn_sb = sbuf("vn_sb", [128, HD], BF16)
    base_row = sbuf("base_row", [1, 16], F32)

    from contextlib import ExitStack
    es = ExitStack()
    bank = [es.enter_context(nc.psum_tensor(f"bank{i}", [128, 512], F32))
            for i in range(8)]

    def sem(name):
        return es.enter_context(nc.semaphore(name))

    s_xb = [sem(f"s_xb{i}") for i in range(3)]
    s_xnwb = [sem(f"s_xnwb{i}") for i in range(2)]
    s_xwb = [sem(f"s_xwb{i}") for i in range(4)]
    s_scatb = [sem(f"s_scatb{i}") for i in range(4)]
    s_xeb = [sem(f"s_xeb{i}") for i in range(2)]
    s_w1b = [sem(f"s_w1b{i}") for i in range(8)]
    s_w2b = [sem(f"s_w2b{i}") for i in range(8)]
    s_yewb = [sem(f"s_yewb{i}") for i in range(4)]
    s_ygtb = [sem(f"s_ygtb{i}") for i in range(4)]
    s_xcb = [sem(f"s_xcb{i}") for i in range(2)]
    s_owb = [sem(f"s_owb{i}") for i in range(2)]
    s_stat = sem("s_stat"); s_xn = sem("s_xn")
    s_xnfm = sem("s_xnfm"); s_wc = sem("s_wc")
    s_zf = sem("s_zf"); s_zt = sem("s_zt"); s_cl = sem("s_cl")
    s_ind = sem("s_ind"); s_off = sem("s_off"); s_ccw = sem("s_ccw")
    s_cc = sem("s_cc"); s_cntb = sem("s_cntb"); s_slot = sem("s_slot")
    s_sm = sem("s_sm")
    s_exp = sem("s_exp"); s_vn = sem("s_vn"); s_l2g = sem("s_l2g")
    s_w2c = sem("s_w2c")
    s_gsil = sem("s_gsil"); s_out = sem("s_out")
    s_ccb = sem("s_ccb"); s_basb = sem("s_basb"); s_sv = sem("s_sv")
    s_vs = sem("s_vs")
    s_comb = sem("s_comb"); s_sq = sem("s_sq"); s_rst = sem("s_rst")
    s_av2d = sem("s_av2d"); s_stw = sem("s_stw")
    s_qkd = sem("s_qkd"); s_gpio = sem("s_gpio"); s_avj = sem("s_avj")
    sgd = [sem(f"sgd{i}") for i in range(8)]
    sbf = [sem(f"sbf{i}") for i in range(8)]

    NZF = (ZROWS + 127) // 128

    rec = {"qk": [], "vr": [], "rank": [], "attqk": [], "attav": [],
           "stp": [], "l1": {}, "l2": [], "base": []}
    ngrp = [0] * 8
    sgdcnt = [0] * 8

    def grp_begin(te, b):
        if ngrp[b] > 0:
            te.wait_ge(sbf[b], ngrp[b])

    def grp_end(stream, b):
        ngrp[b] += 1
        sgdcnt[b] += 1
        rec[stream].append((b, sgdcnt[b]))

    with nc.Block() as block:
        @block.tensor
        def _(te):
            te.wait_ge(s_wc, 16 * 3)
            for wsb in (wq_sb, wk_sb):
                for n in range(NTN):
                    te.wait_ge(s_xnfm, 16 * KD * (n + 1))
                    b = n % 2
                    grp_begin(te, b)
                    for k in range(KD):
                        m = te.matmul(bank[b][0:HD, 0:512],
                                      wsb[:, k * HD:(k + 1) * HD],
                                      xnfm[:, k * S + n * 512: k * S + (n + 1) * 512],
                                      start=(k == 0), stop=(k == KD - 1))
                    m.then_inc(sgd[b], 1)
                    grp_end("qk", b)
            for i in range(NT):
                te.wait_ge(s_xnfm, 16 * KD * (i // 4 + 1))
                b = 2 + (i % 2)
                grp_begin(te, b)
                for k in range(KD):
                    m = te.matmul(bank[b][0:128, 0:72],
                                  xnfm[:, k * S + i * 128: k * S + (i + 1) * 128],
                                  wvr_sb[:, k * 72:(k + 1) * 72],
                                  start=(k == 0), stop=(k == KD - 1))
                m.then_inc(sgd[b], 1)
                grp_end("vr", b)
            te.wait_ge(s_cl, 16 * 6)
            for i in range(NT):
                b = 4 + (i % 2)
                b2 = 6 + (i % 2)
                grp_begin(te, b)
                te.wait_ge(s_ind, 1)
                te.wait_ge(s_off, i)
                te.matmul(bank[b][:, 0:16], lincl[:], ind_b[:, i * 16:(i + 1) * 16],
                          start=True, stop=False)
                m = te.matmul(bank[b][:, 0:16], sel127[:], off_sb[:],
                              start=False, stop=True)
                m.then_inc(sgd[b], 1)
                m2 = te.matmul(bank[b2][0:1, 0:16], onescol[:],
                               ind_b[:, i * 16:(i + 1) * 16],
                               start=True, stop=True)
                m2.then_inc(sgd[b2], 1)
                sgdcnt[b2] += 1
                ngrp[b] += 1
                sgdcnt[b] += 1
                rec["rank"].append(((b, sgdcnt[b]), (b2, sgdcnt[b2])))
            AVB = max(1, NTN // 2)
            avbanks = sorted({2 + (ic % AVB) for ic in range(NTN)})

            def emit_qk(jq):
                for ic in range(NTN):
                    b = ic % 2
                    grp_begin(te, b)
                    m = te.matmul(bank[b][:, :],
                                  kT[0:HD, jq * 128:(jq + 1) * 128],
                                  qT[0:HD, ic * 512:(ic + 1) * 512],
                                  start=True, stop=True)
                    m.then_inc(sgd[b], 1)
                    grp_end("attqk", b)

            QKLA = 2
            for jq in range(min(QKLA, NT)):
                emit_qk(jq)
            for j in range(NT):
                if j + QKLA < NT:
                    emit_qk(j + QKLA)
                if j == 0:
                    te.wait_ge(s_cc, 1)
                    te.wait_ge(s_ccb, 16 * 2)
                    te.wait_ge(s_off, NT)
                    m = te.matmul(bank[6][0:1, 0:16], prefm[:], ccall[:],
                                  start=True, stop=True)
                    m.then_inc(sgd[6], 1)
                    sgdcnt[6] += 1
                    rec["base_g"] = sgdcnt[6]
                    for b in avbanks:
                        grp_begin(te, b)
                te.wait_ge(s_vn, j + 1)
                for ic in range(NTN):
                    b = 2 + (ic % AVB)
                    hi = ic >= AVB and NTN > 1
                    m = te.matmul(bank[b][64:128, :] if hi else bank[b][0:HD, :],
                                  vn_sb[:],
                                  e_sb[j % 2][:, ic * 512:(ic + 1) * 512],
                                  start=(j == 0), stop=(j == NT - 1),
                                  tile_position=(0, 64) if hi else None,
                                  skip_group_check=True)
                    if ic == NTN - 1 and j < NT - 1:
                        m.then_inc(s_avj, 1)
                    if j == NT - 1 and (ic == NTN - 1 or
                                        (NTN > 1 and ic == NTN - 1 - AVB and False)):
                        pass
                if j == NT - 1:
                    m.then_inc(sgd[2], 1)
                    for b in avbanks:
                        ngrp[b] += 2 if (NTN > 1) else 1
                    sgdcnt[2] += 1
                    rec["attav_done"] = sgdcnt[2]
                    rec["avevac"] = {b: ngrp[b] for b in avbanks}
            w1_used = 0
            w2_used = 0
            gel_total = 0
            npass = 0
            nw2c = 0
            rec["w2done"] = []
            for e in range(NE):
                te.wait_ge(s_xeb[e % 2], 16 * KD * (e // 2 + 1))
                for m_ in range(FC):
                    w1_used += 1
                    te.wait_ge(s_w1b[(w1_used - 1) % NW1BUF],
                               16 * ((w1_used - 1) // NW1BUF + 1))
                    bs = (m_ % 2) * NNX
                    for ni in range(NNX):
                        grp_begin(te, bs + ni)
                    for k in range(KD):
                        off = 0
                        for ni, nsz in enumerate(NXC):
                            b = bs + ni
                            mm = te.matmul(
                                bank[b][:, 0:nsz],
                                w1b_sb[(w1_used - 1) % NW1BUF][:, k * 128:(k + 1) * 128],
                                xe_sb[e % 2][:, k * C + off: k * C + off + nsz],
                                start=(k == 0), stop=(k == KD - 1))
                            if k == KD - 1:
                                mm.then_inc(sgd[b], 1)
                            off += nsz
                    for ni in range(NNX):
                        sgdcnt[bs + ni] += 1
                        rec["l1"][(e, m_, ni)] = (bs + ni, sgdcnt[bs + ni],
                                                  ngrp[bs + ni] + 1)
                        ngrp[bs + ni] += 1
                first_pass = True
                for half in range(2):
                    for grp in mgroups():
                        for mi in range(len(grp)):
                            grp_begin(te, mi % 8)
                        for k2 in range(FC):
                            if first_pass:
                                for ni in range(NNX):
                                    hb, _, hn = rec["l1"][(e, k2, ni)]
                                    te.wait_ge(sbf[hb], hn)
                            if k2 % W2KG == 0:
                                w2_used += 1
                                te.wait_ge(s_w2b[(w2_used - 1) % NW2BUF],
                                           16 * ((w2_used - 1) // NW2BUF + 1))
                            ws = w2b_sb[(w2_used - 1) % NW2BUF][:, (k2 % W2KG) * 512:
                                                                (k2 % W2KG + 1) * 512]
                            blk_end = (k2 % W2KG == W2KG - 1 or k2 == FC - 1)
                            for mi, m_ in enumerate(grp):
                                b = mi % 8
                                mm = te.matmul(
                                    bank[b][:, :],
                                    h_sb[:, k2 * C + m_ * 128: k2 * C + (m_ + 1) * 128],
                                    ws,
                                    start=(k2 == 0), stop=(k2 == FC - 1))
                                last_mi = (mi == len(grp) - 1)
                                if k2 == FC - 1 and last_mi:
                                    mm.then_inc(s_l2g, 1)
                                elif blk_end and last_mi:
                                    mm.then_inc(s_w2c, 1)
                            if blk_end:
                                if k2 == FC - 1:
                                    rec["w2done"].append(("g", npass + 1))
                                else:
                                    nw2c += 1
                                    rec["w2done"].append(("c", nw2c))
                        npass += 1
                        for mi in range(len(grp)):
                            b = mi % 8
                            ngrp[b] += 1
                            rec["l2"].append((b, npass, ngrp[b]))
                        first_pass = False

        @block.vector
        def _(ve):
            ve.memset(off_sb[:], 0.0)
            ve.wait_ge(s_wc, 16 * 3)
            ve.drain()
            ve.tensor_scalar_add(dumpc[:], iotap[:], float(NE * C))
            ve.memset(zt_sb[:], 0).then_inc(s_zt, 1)
            for i in range(NT):
                ve.wait_ge(s_xb[i % 3], 16 * (i // 3 + 1))
                xt = xt_sb[i % 3]
                ve.reduce_sum(statm[:, i:i + 1], xt[:], axis=AX.X)
                ve.scalar_tensor_tensor(scr_sb[:], xt[:], 1.0, xt[:],
                                        ALU.mult, ALU.mult,
                                        accum_out=statv[:, i:i + 1]) \
                    .then_inc(s_stat, 1)
            ve.drain()
            ve.tensor_scalar_mul(statm[:], statm[:], 1.0 / D)
            ve.tensor_scalar_mul(statv[:], statv[:], 1.0 / D)
            ve.drain()
            ve.tensor_tensor(statt[:], statm[:], statm[:], ALU.mult)
            ve.drain()
            ve.tensor_sub(statv[:], statv[:], statt[:])
            ve.drain()
            ve.tensor_scalar_add(statv[:], statv[:], EPS).then_inc(s_stat, 1)
            ve.wait_ge(s_sq, 1)
            ve.reciprocal(statr[:], statt[:])
            ve.drain()
            ve.tensor_tensor(statnb[:], statm[:], statr[:], ALU.mult)
            ve.drain()
            ve.tensor_scalar_mul(statnb[:], statnb[:], -1.0).then_inc(s_rst, 1)
            qki = 0
            for dst in (qT, kT):
                for n in range(NTN):
                    b, g = rec["qk"][qki]; qki += 1
                    ve.wait_ge(sgd[b], g)
                    ve.tensor_copy(dst[0:HD, n * 512:(n + 1) * 512],
                                   bank[b][0:HD, 0:512]).then_inc(sbf[b], 1)
            ve.sem_inc(s_qkd, 1)
            for i in range(NT):
                b, g = rec["vr"][i]
                ve.wait_ge(sgd[b], g)
                ve.tensor_copy(v_sb[:, i * HD:(i + 1) * HD], bank[b][0:128, 0:HD])
                mlast = ve.tensor_copy(logit_sb[:, i * NE:(i + 1) * NE],
                                       bank[b][0:128, HD:72])
                mlast.then_inc(sbf[b], 1)
                if i == NT - 1:
                    ve.memset(colA[:], 0.0).then_inc(s_vs, 1)
            ve.memset(ones_b[:], 1.0)
            ve.wait_ge(s_gpio, 1)
            ve.wait_ge(s_sv, 1)
            ve.reduce_sum(colsB[:, 0:NT], expv[:].rearrange("p (t e) -> p t e", e=NE),
                          axis=AX.X)
            ve.reduce_max(colsB[:, NT:2 * NT],
                          expv[:].rearrange("p (t e) -> p t e", e=NE), axis=AX.X)
            ve.drain()
            ve.reciprocal(colsB[:, 2 * NT:3 * NT], colsB[:, 0:NT])
            for i in range(NT):
                ve.tensor_scalar_mul(scrA[:, i * NE:(i + 1) * NE],
                                     ones_b[:, 0:NE], colsB[:, NT + i:NT + i + 1])
            ve.drain()
            ve.tensor_tensor(scrB[:], expv[:], scrA[:], ALU.is_ge)
            ve.drain()
            ve.tensor_tensor(scrC[:], scrB[:], bmi_b[:], ALU.mult)
            ve.drain()
            ve.reduce_max(colsB[:, 3 * NT:4 * NT],
                          scrC[:].rearrange("p (t e) -> p t e", e=NE), axis=AX.X)
            ve.drain()
            ve.tensor_scalar(colsB[:, 3 * NT:4 * NT], colsB[:, 3 * NT:4 * NT],
                             -1.0, 128.0, ALU.mult, ALU.add)
            ve.drain()
            for i in range(NT):
                ve.tensor_scalar_mul(scrA[:, i * NE:(i + 1) * NE],
                                     ones_b[:, 0:NE],
                                     colsB[:, 3 * NT + i:3 * NT + i + 1])
            ve.drain()
            ve.tensor_tensor(ind_b[:].rearrange("p (t k e) -> p t k e", k=2, e=NE)
                             [:, :, 0, :],
                             iotaB[:].rearrange("p (t e) -> p t e", e=NE),
                             scrA[:].rearrange("p (t e) -> p t e", e=NE),
                             ALU.is_equal)
            ve.drain()
            ve.scalar_tensor_tensor(scrB[:],
                                    ind_b[:].rearrange("p (t k e) -> p (t k e)"),
                                    1.0, expv[:], ALU.mult, ALU.mult) \
                if False else None
            ve.scalar_tensor_tensor(
                scrB[:],
                ind_b[:].rearrange("p (t k e) -> p t k e", k=2, e=NE)[:, :, 0, :],
                BIGNEG, expv[:], ALU.mult, ALU.add)
            ve.drain()
            ve.reduce_max(colsB[:, 4 * NT:5 * NT],
                          scrB[:].rearrange("p (t e) -> p t e", e=NE), axis=AX.X)
            ve.drain()
            for i in range(NT):
                ve.tensor_scalar_mul(scrA[:, i * NE:(i + 1) * NE],
                                     ones_b[:, 0:NE],
                                     colsB[:, 4 * NT + i:4 * NT + i + 1])
            ve.drain()
            ve.tensor_tensor(scrC[:], scrB[:], scrA[:], ALU.is_ge)
            ve.drain()
            ve.tensor_tensor(scrC[:], scrC[:], bmi_b[:], ALU.mult)
            ve.drain()
            ve.reduce_max(colsB[:, 5 * NT:6 * NT],
                          scrC[:].rearrange("p (t e) -> p t e", e=NE), axis=AX.X)
            ve.drain()
            ve.tensor_scalar(colsB[:, 5 * NT:6 * NT], colsB[:, 5 * NT:6 * NT],
                             -1.0, 128.0, ALU.mult, ALU.add)
            ve.drain()
            for i in range(NT):
                ve.tensor_scalar_mul(scrA[:, i * NE:(i + 1) * NE],
                                     ones_b[:, 0:NE],
                                     colsB[:, 5 * NT + i:5 * NT + i + 1])
            ve.drain()
            ve.tensor_tensor(ind_b[:].rearrange("p (t k e) -> p t k e", k=2, e=NE)
                             [:, :, 1, :],
                             iotaB[:].rearrange("p (t e) -> p t e", e=NE),
                             scrA[:].rearrange("p (t e) -> p t e", e=NE),
                             ALU.is_equal)
            ve.tensor_tensor(colsB[:, 6 * NT:7 * NT], colsB[:, NT:2 * NT],
                             colsB[:, 2 * NT:3 * NT], ALU.mult)
            ve.tensor_tensor(colsB[:, 7 * NT:8 * NT], colsB[:, 4 * NT:5 * NT],
                             colsB[:, 2 * NT:3 * NT], ALU.mult)
            ve.drain()
            eview = eidx_sb[:].rearrange("p (t k) -> p t k", k=2)
            wview = wco_sb[:].rearrange("p (t k) -> p t k", k=2)
            ve.tensor_copy(eview[:, :, 0], colsB[:, 3 * NT:4 * NT])
            ve.tensor_copy(eview[:, :, 1], colsB[:, 5 * NT:6 * NT])
            ve.tensor_copy(wview[:, :, 0], colsB[:, 6 * NT:7 * NT])
            ve.tensor_copy(wview[:, :, 1], colsB[:, 7 * NT:8 * NT])
            ve.tensor_copy(indp[:], ind_b[:]).then_inc(s_ind, 1)
            for i in range(NT):
                (b, g), (b2, g2) = rec["rank"][i]
                ve.wait_ge(sgd[b], g)
                ve.tensor_sub(rank_sb[:, i * 16:(i + 1) * 16], bank[b][:, 0:16],
                              ind_b[:, i * 16:(i + 1) * 16]).then_inc(sbf[b], 1)
                ve.wait_ge(sgd[b2], g2)
                ve.drain()
                ve.tensor_add(off_sb[0:1, :], off_sb[0:1, :],
                              bank[b2][0:1, 0:16]).then_inc(s_off, 1)
            ve.wait_ge(s_cntb, 16)
            for i in range(NT):
                ve.tensor_copy(cnt_exp[:, i * NE:(i + 1) * NE], cnt_bc[:, 0:NE])
            ve.drain()
            indv = ind_b[:].rearrange("p (t k e) -> p t k e", k=2, e=NE)
            rkv = rank_sb[:].rearrange("p (t k e) -> p t k e", k=2, e=NE)
            cB = colsB
            ve.tensor_tensor(scrA[:].rearrange("p (t e) -> p t e", e=NE),
                             indv[:, :, 0, :], rkv[:, :, 0, :], ALU.mult)
            ve.tensor_tensor(scrB[:].rearrange("p (t e) -> p t e", e=NE),
                             indv[:, :, 1, :], rkv[:, :, 1, :], ALU.mult)
            ve.drain()
            ve.reduce_sum(cB[:, 0:NT], scrA[:].rearrange("p (t e) -> p t e", e=NE),
                          axis=AX.X)
            ve.reduce_sum(cB[:, NT:2 * NT],
                          scrB[:].rearrange("p (t e) -> p t e", e=NE), axis=AX.X)
            ve.drain()
            ve.tensor_tensor(scrA[:].rearrange("p (t e) -> p t e", e=NE),
                             indv[:, :, 1, :],
                             cnt_exp[:].rearrange("p (t e) -> p t e", e=NE), ALU.mult)
            ve.drain()
            ve.reduce_sum(cB[:, 2 * NT:3 * NT],
                          scrA[:].rearrange("p (t e) -> p t e", e=NE), axis=AX.X)
            ve.drain()
            ve.tensor_add(cB[:, 2 * NT:3 * NT], cB[:, 2 * NT:3 * NT],
                          cB[:, NT:2 * NT])
            ve.drain()
            rkst = rk_sb[:].rearrange("p (t k) -> p t k", k=2)
            sctst = sct_sb[:].rearrange("p (t k) -> p t k", k=2)
            rslst = rsl_sb[:].rearrange("p (t k) -> p t k", k=2)
            for kk in range(2):
                slot = cB[:, (0 if kk == 0 else 2) * NT:(1 if kk == 0 else 3) * NT]
                rkl = cB[:, (0 if kk == 0 else 1) * NT:(1 if kk == 0 else 2) * NT]
                ecol = cB[:, (3 if kk == 0 else 5) * NT:(4 if kk == 0 else 6) * NT]
                ovf = cB[:, 8 * NT:9 * NT]
                t1 = cB[:, 9 * NT:10 * NT]
                t2 = cB[:, 10 * NT:11 * NT]
                ve.tensor_scalar(ovf, slot, float(C), None, ALU.is_ge)
                ve.drain()
                ve.scalar_tensor_tensor(t1, ovf, 1.0e9, rkl, ALU.mult, ALU.add)
                ve.drain()
                ve.tensor_copy(rkst[:, :, kk], t1)
                ve.scalar_tensor_tensor(t1, ecol, float(C), slot, ALU.mult, ALU.add)
                ve.tensor_scalar_mul(t2, ones_b[:, 0:NT], dumpc[:])
                ve.drain()
                ve.tensor_sub(t2, t2, t1)
                ve.drain()
                ve.tensor_tensor(t2, t2, ovf, ALU.mult)
                ve.drain()
                ve.tensor_add(t2, t2, t1)
                ve.drain()
                ve.tensor_copy(sctst[:, :, kk], t2)
                ve.tensor_scalar_min(t1, slot, float(C - 1))
                ve.drain()
                ve.scalar_tensor_tensor(t1, ecol, float(C), t1, ALU.mult, ALU.add)
                ve.drain()
                mcp = ve.tensor_copy(rslst[:, :, kk], t1)
            mcp.then_inc(s_slot, 1)
            for j in range(NT + 1):
                if j < NT:
                    if j >= 2:
                        ve.wait_ge(s_exp, j - 1)
                    for ic in range(NTN):
                        b, g = rec["attqk"][j * NTN + ic]
                        ve.wait_ge(sgd[b], g)
                        ve.tensor_scalar(att16[j % 2][:, ic * 512:(ic + 1) * 512],
                                         bank[b][:, :], 0.0, None, ALU.add, ALU.max,
                                         accum_out=cmax[:, ic:ic + 1]) \
                            .then_inc(sbf[b], 1)
                    ve.drain()
                    ve.reduce_max(negmax[:, j % 2:j % 2 + 1], cmax[:, 0:NTN],
                                  axis=AX.X, negate=True).then_inc(s_sm, 1)
                    if j == min(4, NT - 1):
                        ve.wait_ge(sgd[6], rec["base_g"])
                        ve.tensor_copy(base_row[:], bank[6][0:1, 0:16]) \
                            .then_inc(s_basb, 1)
                if j >= 1:
                    jj = j - 1
                    ve.wait_ge(s_exp, jj + 1)
                    ve.reciprocal(vrec[:], den[:])
                    ve.drain()
                    if jj >= 1:
                        ve.wait_ge(s_avj, jj)
                    ve.tensor_scalar_mul(vn_sb[:], v_sb[:, jj * HD:(jj + 1) * HD],
                                         vrec[:]).then_inc(s_vn, 1)
            ve.wait_ge(sgd[2], rec["attav_done"])
            AVB = max(1, NTN // 2)
            for ic in range(NTN):
                b = 2 + (ic % AVB)
                hi = ic >= AVB and NTN > 1
                ve.tensor_copy(avacc[0:HD, ic * 512:(ic + 1) * 512],
                               bank[b][64:128, 0:512] if hi
                               else bank[b][0:HD, 0:512]).then_inc(sbf[b], 1)
            ve.wait_ge(s_basb, 17)
            indpv = indp[:].rearrange("p (t k e) -> p t k e", k=2, e=NE)
            for kk in range(2):
                gsel = p6cols[:, 0:NT]
                for i in range(NT):
                    ve.tensor_copy(p6A[:, i * NE:(i + 1) * NE],
                                   base_bc[:, kk * NE:(kk + 1) * NE])
                ve.drain()
                ve.tensor_tensor(p6B[:].rearrange("p (t e) -> p t e", e=NE),
                                 indpv[:, :, kk, :],
                                 p6A[:].rearrange("p (t e) -> p t e", e=NE), ALU.mult)
                ve.drain()
                ve.reduce_sum(gsel, p6B[:].rearrange("p (t e) -> p t e", e=NE),
                              axis=AX.X)
                ve.drain()
                ve.tensor_add(gsel, gsel,
                              rk_sb[:].rearrange("p (t k) -> p t k", k=2)[:, :, kk])
                ve.drain()
                ve.tensor_scalar(gsel, gsel, float(CAPACITY), None, ALU.is_lt)
                ve.drain()
                ve.tensor_tensor(gsel, gsel,
                                 wco_sb[:].rearrange("p (t k) -> p t k", k=2)[:, :, kk],
                                 ALU.mult)
                ve.drain()
                ve.tensor_copy(coef_b[:].rearrange("p (t k) -> p t k", k=2)[:, :, kk],
                               gsel)
            ve.drain()
            yn = 0
            for b, npass_, gcount in rec["l2"]:
                yn += 1
                if yn > 4:
                    ve.wait_ge(s_yewb[(yn - 4 - 1) % 4], 16 * ((yn - 4 - 1) // 4 + 1))
                ve.wait_ge(s_l2g, npass_)
                ve.tensor_copy(ye_sb[yn % 4][:, :], bank[b][:, :]).then_inc(sbf[b], 1)
            ve.wait_ge(s_stw, 16 * NT)
            for i in range(NT + 1):
                if i >= 1:
                    ii = i - 1
                    ve.wait_ge(s_gsil, ii + 1)
                    ob = o_sb[ii % 2]
                    for c in range(D // HD):
                        ve.tensor_mul(ob[:, c * HD:(c + 1) * HD],
                                      ob[:, c * HD:(c + 1) * HD],
                                      s_sb[:, ii * HD:(ii + 1) * HD])
                if i < NT:
                    ve.wait_ge(s_ygtb[(2 * i) % 4], 16 * ((2 * i) // 4 + 1))
                    ve.wait_ge(s_ygtb[(2 * i + 1) % 4], 16 * ((2 * i + 1) // 4 + 1))
                    if i >= 2:
                        ve.wait_ge(s_gsil, i - 1)
                    ve.scalar_tensor_tensor(gbuf[i % 2][:], yg_sb[(2 * i) % 4][:],
                                            coef_b[:, 2 * i:2 * i + 1],
                                            bias_bc[:], ALU.mult, ALU.add)
                ve.drain()
                if i < NT:
                    ve.scalar_tensor_tensor(gbuf[i % 2][:],
                                            yg_sb[(2 * i + 1) % 4][:],
                                            coef_b[:, 2 * i + 1:2 * i + 2],
                                            gbuf[i % 2][:], ALU.mult, ALU.add) \
                        .then_inc(s_comb, 1)
                if i >= 1:
                    ii = i - 1
                    ve.wait_ge(s_xcb[ii % 2], 16 * (ii // 2 + 1))
                    ve.tensor_add(o_sb[ii % 2][:], o_sb[ii % 2][:],
                                  xc_sb[ii % 2][:]).then_inc(s_out, 1)

        @block.scalar
        def _(sc):
            sc.wait_ge(s_zt, 1)
            for z in range(NZF):
                r0 = z * 128
                r1 = min(r0 + 128, ZROWS)
                sc.dma_start(out=xe_d[r0:r1, :], in_=zt_sb[0:r1 - r0, :]) \
                    .then_inc(s_zf, 16)
            sc.wait_ge(s_stat, NT + 1)
            sc.activation(statt[:], statv[:], AF.Sqrt,
                          bias=0.0, scale=1.0).then_inc(s_sq, 1)
            sc.wait_ge(s_rst, 1)
            cntA = [len([t for t in range(NT) if t % 3 == p]) for p in range(3)]
            for i in range(NT):
                sc.wait_ge(s_xb[i % 3], 16 * (cntA[i % 3] + i // 3 + 1))
                if i >= 2:
                    sc.wait_ge(s_xnwb[i % 2], 16 * (i // 2))
                sc.activation(xnt_sb[i % 2][:], xt_sb[i % 3][:], AF.Identity,
                              bias=statnb[:, i:i + 1],
                              scale=statr[:, i:i + 1]).then_inc(s_xn, 1)
            sc.wait_ge(s_vs, 1)
            sc.activation(expv[:], logit_sb[:], AF.Exp).then_inc(s_sv, 1)
            for j in range(NT):
                sc.wait_ge(s_sm, j + 1)
                if j >= 1:
                    sc.wait_ge(s_vn, j)
                if j >= 2:
                    sc.wait_ge(s_avj, j - 1)
                sc.activation(e_sb[j % 2][:], att16[j % 2][:], AF.Exp,
                              bias=negmax[:, j % 2:j % 2 + 1], scale=1.0,
                              accum_out=den[:]).then_inc(s_exp, 1)
            for b in sorted(rec["avevac"]):
                sc.wait_ge(sbf[b], rec["avevac"][b])
            sc.dma_start(out=av_d, in_=avacc[:]).then_inc(s_av2d, 16)
            for e in range(NE):
                for m_ in range(FC):
                    off = 0
                    for ni, nsz in enumerate(NXC):
                        b, g, _ = rec["l1"][(e, m_, ni)]
                        sc.wait_ge(sgd[b], g)
                        sc.activation(h_sb[:, m_ * C + off: m_ * C + off + nsz],
                                      bank[b][:, 0:nsz], AF.Gelu_apprx_tanh) \
                            .then_inc(sbf[b], 1)
                        off += nsz
            for i in range(NT):
                sc.wait_ge(s_comb, i + 1)
                if i >= 2:
                    sc.wait_ge(s_owb[i % 2], 16 * ((i - 2) // 2 + 1))
                sc.activation(o_sb[i % 2][:], gbuf[i % 2][:], AF.Silu) \
                    .then_inc(s_gsil, 1)

        @block.gpsimd
        def _(gp):
            gp.iota(iota8[:], [[1, NE]], base=0, channel_multiplier=0,
                    allow_small_or_imprecise_dtypes=True)
            gp.iota(iotap[:], [[0, 1]], base=0, channel_multiplier=1,
                    allow_small_or_imprecise_dtypes=True)
            gp.iota(bmi[:], [[-1, NE]], base=128, channel_multiplier=0,
                    allow_small_or_imprecise_dtypes=True)
            gp.dma_start(out=wq_sb[:].rearrange("p (k n) -> p k n", k=KD),
                         in_=wq_d).then_inc(s_wc, 16)
            gp.dma_start(out=wk_sb[:].rearrange("p (k n) -> p k n", k=KD),
                         in_=wk_d).then_inc(s_wc, 16)
            gp.dma_start(out=wvr_sb[:].rearrange("p (k n) -> p k n", k=KD),
                         in_=wvr_d).then_inc(s_wc, 16)
            gp.wait_ge(s_xn, NT)
            gp.iota(iotaB[:], [[0, NT], [1, NE]], base=0, channel_multiplier=0,
                    allow_small_or_imprecise_dtypes=True)
            gp.iota(bmi_b[:], [[0, NT], [-1, NE]], base=128, channel_multiplier=0,
                    allow_small_or_imprecise_dtypes=True) if False else None
            gp.iota(bmi_b[:], [[0, NT], [-1, NE]], base=128,
                    channel_multiplier=0,
                    allow_small_or_imprecise_dtypes=True).then_inc(s_gpio, 1)
            gp.wait_ge(s_off, NT)
            gp.dma_start(out=ccin_d, in_=off_sb[0:1, :]).then_inc(s_ccw, 16)
            gp.wait_ge(s_ccw, 16)
            gp.collective_compute(
                "AllGather", ALU.bypass, replica_groups=[list(range(n_cores))],
                ins=[ccin_d.opt()], outs=[ccout_d.opt()],
            ).then_inc(s_cc, 1)
            gp.wait_ge(s_cc, 1)
            gp.dma_start(out=ccall[:], in_=ccout_d).then_inc(s_ccb, 16)
            gp.dma_start(out=prefm[:], in_=prefm_d).then_inc(s_ccb, 16)
            for i in range(NT):
                gp.wait_ge(s_xwb[i % 4], 16 * (i // 4 + 1))
                gp.wait_ge(s_slot, 1)
                if i == 0:
                    gp.wait_ge(s_zf, 16 * NZF)
                for kk in range(2):
                    gp.indirect_dma_start(
                        xe_d,
                        IndirectOffsetOnAxis(ap=sct_sb[:, 2 * i + kk:2 * i + kk + 1],
                                             axis=0),
                        xw_sb[i % 4][:], None,
                    ).then_inc(s_scatb[i % 4], 16)
            gp.wait_ge(s_basb, 1)
            gp.dma_start(out=ccbase_d, in_=base_row[:]).then_inc(s_ccb, 16)
            w1i = 0
            w2i = 0
            for e in range(NE):
                for m_ in range(FC):
                    w1i += 1
                    if w1i > 2:
                        jb = w1i - 2
                        eb, mb = (jb - 1) // FC, (jb - 1) % FC
                        wb, wg, _ = rec["l1"][(eb, mb, NNX - 1)]
                        gp.wait_ge(sgd[wb], wg)
                    gp.dma_start(out=w1b_sb[(w1i - 1) % NW1BUF][:].rearrange(
                        "p (k f) -> p k f", k=KD),
                        in_=w1_d[e, m_]).then_inc(s_w1b[(w1i - 1) % NW1BUF], 16)
                for half in range(2):
                    for grp in mgroups():
                        for kb in range(NW2B):
                            w2i += 1
                            if w2i > NW2BUF:
                                kind, val = rec["w2done"][w2i - 1 - NW2BUF]
                                gp.wait_ge(s_w2c if kind == "c" else s_l2g, val)
                            gp.dma_start(
                                out=w2b_sb[(w2i - 1) % NW2BUF][:].rearrange(
                                    "p (k j) -> p k j", k=W2KG),
                                in_=w2_d[e, kb * W2KG:(kb + 1) * W2KG, half]
                                .rearrange("k p j -> p k j"),
                            ).then_inc(s_w2b[(w2i - 1) % NW2BUF], 16)
            nyes = NE * 2 * MC
            for p in range(4):
                tot = nyes // 4 + (1 if p < nyes % 4 else 0)
                gp.wait_ge(s_yewb[p], 16 * tot)
            for i in range(NT):
                for kk in range(2):
                    n = 2 * i + kk
                    if n >= 4:
                        gp.wait_ge(s_comb, (n - 4) // 2 + 1)
                    gp.indirect_dma_start(
                        yg_sb[n % 4][:], None,
                        ye_d,
                        IndirectOffsetOnAxis(ap=rsl_sb[:, n:n + 1], axis=0),
                    ).then_inc(s_ygtb[n % 4], 16)

        @block.sync
        def _(sy):
            sy.dma_start(out=bias_bc[:],
                         in_=bias_d.broadcast_to([128, D])).then_inc(s_cl, 16)
            sy.dma_start(out=identf[:], in_=identf_d).then_inc(s_cl, 16)
            sy.dma_start(out=identb[:], in_=identb_d).then_inc(s_cl, 16)
            sy.dma_start(out=lincl[:], in_=lincl_d).then_inc(s_cl, 16)
            sy.dma_start(out=sel127[:], in_=sel127_d).then_inc(s_cl, 16)
            sy.dma_start(out=onescol[:], in_=onescol_d).then_inc(s_cl, 16)
            for i in range(NT):
                if i >= 3:
                    sy.wait_ge(s_stat, i - 2)
                sy.dma_start(out=xt_sb[i % 3][:],
                             in_=x_d[i * 128:(i + 1) * 128, :]).then_inc(s_xb[i % 3], 16)
            def emit_xnfm(n):
                sy.wait_ge(s_xnwb[0], 16 * (2 * n + 2))
                sy.wait_ge(s_xnwb[1], 16 * (2 * n + 2))
                for c in range(KD):
                    sy.dma_start(out=xnfm[:, c * S + n * 512: c * S + (n + 1) * 512],
                                 in_=xn_d[n * 512:(n + 1) * 512,
                                          c * 128:(c + 1) * 128],
                                 transpose=True).then_inc(s_xnfm, 16)

            for i in range(NT):
                if i < 3:
                    iA = max(t for t in range(NT) if t % 3 == i % 3)
                    sy.wait_ge(s_stat, iA + 1)
                else:
                    sy.wait_ge(s_xn, i - 2)
                sy.dma_start(out=xt_sb[i % 3][:],
                             in_=x_d[i * 128:(i + 1) * 128, :]).then_inc(s_xb[i % 3], 16)
                if i >= 1:
                    sy.wait_ge(s_xn, i)
                    sy.dma_start(out=xn_d[(i - 1) * 128:i * 128, :],
                                 in_=xnt_sb[(i - 1) % 2][:]) \
                        .then_inc(s_xnwb[(i - 1) % 2], 16)
                    if i % 4 == 0:
                        emit_xnfm(i // 4 - 1)
            sy.wait_ge(s_xn, NT)
            sy.dma_start(out=xn_d[(NT - 1) * 128:NT * 128, :],
                         in_=xnt_sb[(NT - 1) % 2][:]) \
                .then_inc(s_xnwb[(NT - 1) % 2], 16)
            for n in range((NT - 1) // 4, NT // 4):
                emit_xnfm(n)
            if debug_taps:
                sy.wait_ge(s_xnfm, 16 * KD * NTN)
                sy.dma_start(out=dbg["dbg_xnfm"], in_=xnfm[:]).then_inc(s_cl, 16)
                sy.wait_ge(s_qkd, 1)
                sy.dma_start(out=dbg["dbg_qT"], in_=qT[:]).then_inc(s_cl, 16)
                sy.dma_start(out=dbg["dbg_kT"], in_=kT[:]).then_inc(s_cl, 16)
            sy.wait_ge(s_ccw, 16)
            sy.dma_start(out=cnt_bc[:],
                         in_=ccin_d.broadcast_to([128, 16])).then_inc(s_cntb, 16)
            for i in range(NT):
                if i >= 4:
                    sy.wait_ge(s_scatb[i % 4], 32 * (i // 4))
                sy.dma_start(out=xw_sb[i % 4][:],
                             in_=xn_d[i * 128:(i + 1) * 128, :]) \
                    .then_inc(s_xwb[i % 4], 16)
            sy.wait_ge(s_ccb, 16 * 3)
            sy.dma_start(out=base_bc[:],
                         in_=ccbase_d.broadcast_to([128, 16])).then_inc(s_basb, 16)
            def emit_ye_stores(e, yewn):
                for half in range(2):
                    for grp in mgroups():
                        for m_ in grp:
                            yewn += 1
                            yb, ynp, ygc = rec["l2"][yewn - 1]
                            sy.wait_ge(sbf[yb], ygc)
                            sy.dma_start(
                                out=ye_d[e * C + m_ * 128: e * C + (m_ + 1) * 128,
                                         half * 512:(half + 1) * 512],
                                in_=ye_sb[yewn % 4][:, :]) \
                                .then_inc(s_yewb[(yewn - 1) % 4], 16)
                return yewn

            yewn = 0
            for e in range(NE):
                if e == 0:
                    for p in range(4):
                        tot = 2 * (NT // 4 + (1 if p < NT % 4 else 0))
                        sy.wait_ge(s_scatb[p], 16 * tot)
                if e == 1:
                    qb, qg = rec["attqk"][-1]
                    sy.wait_ge(sgd[qb], qg)
                if e == 2:
                    sy.wait_ge(s_av2d, 16)
                    for i in range(NT):
                        sy.dma_start(out=s_sb[:, i * HD:(i + 1) * HD],
                                     in_=av_d[0:HD, i * 128:(i + 1) * 128],
                                     transpose=True).then_inc(s_stw, 16)
                if e >= 2:
                    wb, wg, _ = rec["l1"][(e - 2, FC - 1, NNX - 1)]
                    sy.wait_ge(sgd[wb], wg)
                for k in range(KD):
                    sy.dma_start(out=xe_sb[e % 2][:, k * C:(k + 1) * C],
                                 in_=xe_d[e * C:(e + 1) * C, k * 128:(k + 1) * 128],
                                 transpose=True).then_inc(s_xeb[e % 2], 16)
                if e >= 1:
                    yewn = emit_ye_stores(e - 1, yewn)
            yewn = emit_ye_stores(NE - 1, yewn)
            for i in range(NT):
                if i >= 2:
                    sy.wait_ge(s_out, i - 1)
                sy.dma_start(out=xc_sb[i % 2][:],
                             in_=x_d[i * 128:(i + 1) * 128, :]) \
                    .then_inc(s_xcb[i % 2], 16)
                if i >= 1:
                    sy.wait_ge(s_out, i)
                    sy.dma_start(out=out_d[(i - 1) * 128:i * 128, :],
                                 in_=o_sb[(i - 1) % 2][:]) \
                        .then_inc(s_owb[(i - 1) % 2], 16)
            sy.wait_ge(s_out, NT)
            sy.dma_start(out=out_d[(NT - 1) * 128:NT * 128, :],
                         in_=o_sb[(NT - 1) % 2][:]).then_inc(s_owb[(NT - 1) % 2], 16)
            sy.wait_ge(s_owb[0], 16 * ((NT + 1) // 2))
            sy.wait_ge(s_owb[1], 16 * (NT // 2))
            if debug_taps:
                ndbg = 0
                sy.dma_start(out=dbg["dbg_xnd"], in_=xn_d).then_inc(s_cl, 16)
                sy.dma_start(out=dbg["dbg_av"], in_=avacc[:]).then_inc(s_cl, 16)
                ndbg += 1
                for name, src_sb in (("dbg_s", s_sb), ("dbg_logit", logit_sb),
                                     ("dbg_wco", wco_sb), ("dbg_eidx", eidx_sb),
                                     ("dbg_rk", rk_sb), ("dbg_rsl", rsl_sb),
                                     ("dbg_sct", sct_sb), ("dbg_base", base_bc),
                                     ("dbg_rank", rank_sb)):
                    sy.dma_start(out=dbg[name], in_=src_sb[:]).then_inc(s_cl, 16)
                    ndbg += 1
                sy.wait_ge(s_cl, 16 * 6 + 16 * (ndbg + 4))

    es.close()
    return nc


def _host_layouts(inputs):
    x = np.ascontiguousarray(inputs["L0_attn_x"])
    wq = np.ascontiguousarray(inputs["L0_attn_Wq"])
    wk = np.ascontiguousarray(inputs["L0_attn_Wk"])
    wv = np.ascontiguousarray(inputs["L0_attn_Wv"])
    wr = np.ascontiguousarray(inputs["L0_ffn_router_w"])
    w1 = np.ascontiguousarray(inputs["L0_ffn_w1"])
    w2 = np.ascontiguousarray(inputs["L0_ffn_w2"])
    bias = np.ascontiguousarray(inputs["L0_ffn_bias"]).reshape(1, D)
    S = x.shape[1]
    F = w1.shape[2]
    FC = F // 128

    def chunk_pk(w, ncols):
        return np.ascontiguousarray(w.reshape(KD, 128, ncols).transpose(1, 0, 2))

    wvr = np.concatenate([wv, wr], axis=1)
    w1h = np.ascontiguousarray(
        w1.reshape(NE, KD, 128, FC, 128).transpose(0, 3, 2, 1, 4))
    w2h = np.ascontiguousarray(
        w2.reshape(NE, FC, 128, 2, 512).transpose(0, 1, 3, 2, 4))
    identf = np.eye(128, dtype=np.float32)
    import ml_dtypes
    identb = np.eye(128).astype(ml_dtypes.bfloat16)
    lincl = np.triu(np.ones((128, 128), np.float32))
    sel127 = np.zeros((128, 128), np.float32)
    sel127[0, :] = 1.0
    onescol = np.ones((128, 1), np.float32)
    common = dict(
        wqh=chunk_pk(wq, HD), wkh=chunk_pk(wk, HD), wvrh=chunk_pk(wvr, 72),
        w1h=w1h, w2h=w2h, bias1=bias,
        identf=identf, identb=identb, lincl=lincl, sel127=sel127, onescol=onescol,
    )
    in_maps = []
    for c in range(x.shape[0]):
        prefm = np.zeros((NE, 1), np.float32)
        prefm[:c, 0] = 1.0
        in_maps.append(dict(common, x=x[c], prefm=prefm))
    return in_maps, S, F


def kernel(**inputs):
    in_maps, S, F = _host_layouts(inputs)
    nc = build_program(S=S, F=F, C=1152, n_cores=8)
    res = run_bass_kernel_spmd(nc, in_maps, core_ids=list(range(8)))
    out = np.stack([r["out"] for r in res.results], axis=0)
    return out.astype(np.float32)


# version 14 (final)
# speedup vs baseline: 1.4060x; 1.0157x over previous
import numpy as np
import concourse.bass as bass
import concourse.mybir as mybir
from concourse.bass import IndirectOffsetOnAxis
from concourse.bass_utils import run_bass_kernel_spmd

F32 = mybir.dt.float32
BF16 = mybir.dt.bfloat16
F16 = mybir.dt.float16
U32 = mybir.dt.uint32
AF = mybir.ActivationFunctionType
ALU = mybir.AluOpType
AX = mybir.AxisListType

D = 1024
KD = D // 128
HD = 64
NE = 8
EPS = 1e-5
BIGNEG = -1.0e30


def build_program(S=4096, F=4096, C=1152, n_cores=8, debug_taps=False):
    NT = S // 128
    NTN = S // 512
    FC = F // 128
    MC = C // 128
    NXC = [512] * (C // 512) + ([C % 512] if C % 512 else [])
    NNX = len(NXC)
    W2KG = min(4, FC)
    NW2B = (FC + W2KG - 1) // W2KG
    NW2BUF = 3
    NW1BUF = 4
    CAPACITY = S * n_cores // NE
    assert S % 512 == 0 and C % 128 == 0 and F % (128 * W2KG) == 0

    def mgroups():
        if MC <= 8:
            return [list(range(MC))]
        h1 = (MC + 1) // 2
        return [list(range(h1)), list(range(h1, MC))]

    nc = bass.Bass("TRN2", target_bir_lowering=False, debug=False,
                   detect_race_conditions=False)

    x_d = nc.dram_tensor("x", [S, D], F32, kind="ExternalInput").ap()
    wq_d = nc.dram_tensor("wqh", [128, KD, HD], F32, kind="ExternalInput").ap()
    wk_d = nc.dram_tensor("wkh", [128, KD, HD], F32, kind="ExternalInput").ap()
    wvr_d = nc.dram_tensor("wvrh", [128, KD, 72], F32, kind="ExternalInput").ap()
    w1_d = nc.dram_tensor("w1h", [NE, FC, 128, KD, 128], F32, kind="ExternalInput").ap()
    w2_d = nc.dram_tensor("w2h", [NE, FC, 2, 128, 512], F32, kind="ExternalInput").ap()
    bias_d = nc.dram_tensor("bias1", [1, D], F32, kind="ExternalInput").ap()
    identf_d = nc.dram_tensor("identf", [128, 128], F32, kind="ExternalInput").ap()
    identb_d = nc.dram_tensor("identb", [128, 128], BF16, kind="ExternalInput").ap()
    lincl_d = nc.dram_tensor("lincl", [128, 128], F32, kind="ExternalInput").ap()
    sel127_d = nc.dram_tensor("sel127", [128, 128], F32, kind="ExternalInput").ap()
    onescol_d = nc.dram_tensor("onescol", [128, 1], F32, kind="ExternalInput").ap()
    prefm_d = nc.dram_tensor("prefm", [NE, 1], F32, kind="ExternalInput").ap()
    out_d = nc.dram_tensor("out", [S, D], F32, kind="ExternalOutput").ap()

    xn_d = nc.dram_tensor("xn_dram", [S, D], BF16).ap()
    av_d = nc.dram_tensor("av_dram", [HD, S], BF16).ap()
    ZROWS = NE * C + 128
    xe_d = nc.dram_tensor("xe_dram", [ZROWS, D], BF16).ap()
    ye_d = nc.dram_tensor("ye_dram", [NE * C, D], BF16).ap()
    ccin_d = nc.dram_tensor("cc_in", [1, 16], F32).ap()
    ccout_d = nc.dram_tensor("cc_out", [n_cores, 16], F32).ap()
    ccbase_d = nc.dram_tensor("cc_base", [1, 16], F32).ap()
    if debug_taps:
        NTt = S // 128
        dbg = {
            "dbg_s": nc.dram_tensor("dbg_s", [128, NTt * HD], BF16, kind="ExternalOutput").ap(),
            "dbg_logit": nc.dram_tensor("dbg_logit", [128, NTt * NE], F32, kind="ExternalOutput").ap(),
            "dbg_wco": nc.dram_tensor("dbg_wco", [128, NTt * 2], F32, kind="ExternalOutput").ap(),
            "dbg_eidx": nc.dram_tensor("dbg_eidx", [128, NTt * 2], F32, kind="ExternalOutput").ap(),
            "dbg_rk": nc.dram_tensor("dbg_rk", [128, NTt * 2], F32, kind="ExternalOutput").ap(),
            "dbg_rsl": nc.dram_tensor("dbg_rsl", [128, NTt * 2], U32, kind="ExternalOutput").ap(),
            "dbg_sct": nc.dram_tensor("dbg_sct", [128, NTt * 2], U32, kind="ExternalOutput").ap(),
            "dbg_base": nc.dram_tensor("dbg_base", [128, 16], F32, kind="ExternalOutput").ap(),
            "dbg_rank": nc.dram_tensor("dbg_rank", [128, NTt * 16], F32, kind="ExternalOutput").ap(),
            "dbg_xnd": nc.dram_tensor("dbg_xnd", [S, D], BF16, kind="ExternalOutput").ap(),
            "dbg_av": nc.dram_tensor("dbg_av", [HD, S], BF16, kind="ExternalOutput").ap(),
            "dbg_xnfm": nc.dram_tensor("dbg_xnfm", [128, KD * S], BF16, kind="ExternalOutput").ap(),
            "dbg_qT": nc.dram_tensor("dbg_qT", [64, S], BF16, kind="ExternalOutput").ap(),
            "dbg_kT": nc.dram_tensor("dbg_kT", [64, S], BF16, kind="ExternalOutput").ap(),
        }

    def sbuf(name, shape, dtype):
        return nc.alloc_sbuf_tensor(name, shape, dtype)

    def nbytes(shape, dtype):
        return shape[1] * mybir.dt.size(dtype)

    h_sb = sbuf("h_sb", [128, FC * C], BF16)
    h_addr = nc.lookup_mloc(h_sb).addr
    if KD * S <= FC * C:
        xnfm = nc.alloc_sbuf_tensor_at("xn_fm", [128, KD * S], BF16, offset=h_addr)
    else:
        xnfm = sbuf("xn_fm", [128, KD * S], BF16)
    if KD * S * 2 + S * 2 <= FC * C * 2:
        qT = nc.alloc_sbuf_tensor_at("qT", [64, S], BF16,
                                     offset=h_addr + KD * S * 2)
    else:
        qT = sbuf("qT", [64, S], BF16)
    xe_sb = [sbuf(f"xe{i}", [128, KD * C], BF16) for i in range(2)]
    if S * 2 <= KD * C * 2:
        kT = nc.alloc_sbuf_tensor_at("kT", [64, S], BF16,
                                     offset=nc.lookup_mloc(xe_sb[1]).addr)
    else:
        kT = sbuf("kT", [64, S], BF16)

    R1_P1 = max(3 * 4096 + 2 * 2048 + 4096,
                10 * (S // 128) * NE * 4 + (S // 128) * 14 * 4)
    R1_P4 = 2 * S + 2 * 2 * S + 2 * 2 * S
    R1_P6 = 4 * 2048 + 5 * 4096 + 2 * (S // 128) * NE * 4 + (S // 128) * 16
    _g2off = -(-R1_P6 // 32) * 32
    _sw = max(32, -((S // 128) * -4) // 32 * 32)
    _stat0 = max(R1_P1, R1_P4, _g2off + 4096)
    R1 = _stat0 + 5 * _sw
    r1_sb = sbuf("r1_arena", [128, R1], mybir.dt.uint8)
    r1_addr = nc.lookup_mloc(r1_sb).addr

    def at(name, shape, dtype, off):
        assert off + nbytes(shape, dtype) <= R1, (name, off)
        return nc.alloc_sbuf_tensor_at(name, shape, dtype, offset=r1_addr + off)

    xt_sb = [at(f"xt{i}", [128, D], F32, i * 4096) for i in range(3)]
    NT_ = S // 128
    expv = at("expv", [128, NT_ * NE], F32, 0)
    scrA = at("scrA", [128, NT_ * NE], F32, NT_ * NE * 4)
    scrB = at("scrB", [128, NT_ * NE], F32, 2 * NT_ * NE * 4)
    scrC = at("scrC", [128, NT_ * NE], F32, 3 * NT_ * NE * 4)
    ind_b = at("ind_b", [128, NT_ * 16], F32, 4 * NT_ * NE * 4)
    iotaB = at("iotaB", [128, NT_ * NE], F32, 6 * NT_ * NE * 4)
    bmi_b = at("bmi_b", [128, NT_ * NE], F32, 7 * NT_ * NE * 4)
    cnt_exp = at("cnt_exp", [128, NT_ * NE], F32, 8 * NT_ * NE * 4)
    ones_b = at("ones_b", [128, NT_ * NE], F32, 9 * NT_ * NE * 4)
    colsB = at("colsB", [128, NT_ * 14], F32, 10 * NT_ * NE * 4)
    p6A = at("p6A", [128, NT_ * NE], F32, 4 * 2048 + 5 * 4096)
    p6B = at("p6B", [128, NT_ * NE], F32, 4 * 2048 + 5 * 4096 + NT_ * NE * 4)
    p6cols = at("p6cols", [128, NT_ * 4], F32, 4 * 2048 + 5 * 4096 + 2 * NT_ * NE * 4)
    coef_b = sbuf("coef_b", [128, NT_ * 2], F32)
    indp = sbuf("indp", [128, NT_ * 16], BF16)
    xnt_sb = [at(f"xnt{i}", [128, D], BF16, 3 * 4096 + i * 2048) for i in range(2)]
    scr_sb = at("scr_sb", [128, D], F32, 3 * 4096 + 2 * 2048)
    avacc = at("avacc", [64, S], BF16, 0)
    att16 = [at(f"att16_{i}", [128, S], F16, 2 * S + i * 2 * S) for i in range(2)]
    e_sb = [at(f"esb{i}", [128, S], BF16, 2 * S + 4 * S + i * 2 * S)
            for i in range(2)]
    yg_sb = [at(f"yg{i}", [128, D], BF16, i * 2048) for i in range(4)]
    xc_sb = [at(f"xc{i}", [128, D], F32, 4 * 2048 + i * 4096) for i in range(2)]
    o_sb = [at(f"osb{i}", [128, D], F32, 4 * 2048 + 2 * 4096 + i * 4096)
            for i in range(2)]
    g_sb = at("g_sb", [128, D], F32, 4 * 2048 + 4 * 4096)
    g2_sb = at("g2_sb", [128, D], F32, _g2off)
    gbuf = [g_sb, g2_sb]

    v_sb = sbuf("v_sb", [128, NT * HD], BF16)
    s_sb = sbuf("s_sb", [128, NT * HD], BF16)
    logit_sb = sbuf("logit_sb", [128, NT * NE], F32)
    rank_sb = sbuf("rank_sb", [128, NT * 16], F32)
    rk_sb = sbuf("rk_sb", [128, NT * 2], F32)
    eidx_sb = sbuf("eidx_sb", [128, NT * 2], F32)
    wco_sb = sbuf("wco_sb", [128, NT * 2], F32)
    sct_sb = sbuf("sct_sb", [128, NT * 2], U32)
    rsl_sb = sbuf("rsl_sb", [128, NT * 2], U32)
    ind_sb = [sbuf(f"ind{i}", [128, 16], F32) for i in range(2)]
    off_sb = sbuf("off_sb", [128, 16], F32)
    cnt_bc = sbuf("cnt_bc", [128, 16], F32)
    base_bc = sbuf("base_bc", [128, 16], F32)
    ccall = sbuf("ccall", [NE, 16], F32)
    prefm = sbuf("prefm_sb", [NE, 1], F32)
    xw_sb = [sbuf(f"xw{i}", [128, D], BF16) for i in range(4)]
    w1b_sb = [sbuf(f"w1b{i}", [128, KD * 128], BF16) for i in range(NW1BUF)]
    w2b_sb = [sbuf(f"w2b{i}", [128, W2KG * 512], BF16) for i in range(NW2BUF)]
    wq_sb = sbuf("wq_sb", [128, KD * HD], BF16)
    wk_sb = sbuf("wk_sb", [128, KD * HD], BF16)
    wvr_sb = sbuf("wvr_sb", [128, KD * 72], BF16)
    ye_sb = [sbuf(f"yesb{i}", [128, 512], BF16) for i in range(4)]
    bias_bc = sbuf("bias_bc", [128, D], F32)
    zt_sb = sbuf("zt_sb", [128, D], BF16)
    identf = sbuf("identf_sb", [128, 128], F32)
    identb = sbuf("identb_sb", [128, 128], BF16)
    lincl = sbuf("lincl_sb", [128, 128], F32)
    sel127 = sbuf("sel127_sb", [128, 128], F32)
    onescol = sbuf("onescol_sb", [128, 1], F32)
    iota8 = sbuf("iota8", [128, NE], F32)
    iotap = sbuf("iotap", [128, 1], F32)
    bmi = sbuf("bmi", [128, NE], F32)
    st1 = sbuf("st1", [128, NE], F32)
    st2 = sbuf("st2", [128, NE], F32)
    st3 = sbuf("st3", [128, NE], F32)
    st4 = sbuf("st4", [128, NE], F32)
    cmax = sbuf("cmax", [128, 16], F32)
    statm = at("statm", [128, S // 128], F32, _stat0)
    statv = at("statv", [128, S // 128], F32, _stat0 + _sw)
    statr = at("statr", [128, S // 128], F32, _stat0 + 2 * _sw)
    statnb = at("statnb", [128, S // 128], F32, _stat0 + 3 * _sw)
    statt = at("statt", [128, S // 128], F32, _stat0 + 4 * _sw)
    colA = sbuf("colA", [128, 1], F32)
    colB = sbuf("colB", [128, 1], F32)
    colC = sbuf("colC", [128, 1], F32)
    colD = sbuf("colD", [128, 1], F32)
    colE = sbuf("colE", [128, 1], F32)
    colF = sbuf("colF", [128, 1], F32)
    colG = sbuf("colG", [128, 1], F32)
    colH = sbuf("colH", [128, 1], F32)
    c0col = sbuf("c0col", [128, 1], F32)
    dumpc = sbuf("dumpc", [128, 1], F32)
    c1col = sbuf("c1col", [128, 1], F32)
    negmax = sbuf("negmax", [128, 2], F32)
    den = sbuf("den", [128, 1], F32)
    vrec = sbuf("vrec", [128, 1], F32)
    vn_sb = sbuf("vn_sb", [128, HD], BF16)
    base_row = sbuf("base_row", [1, 16], F32)

    from contextlib import ExitStack
    es = ExitStack()
    bank = [es.enter_context(nc.psum_tensor(f"bank{i}", [128, 512], F32))
            for i in range(8)]

    def sem(name):
        return es.enter_context(nc.semaphore(name))

    s_xb = [sem(f"s_xb{i}") for i in range(3)]
    s_xnwb = [sem(f"s_xnwb{i}") for i in range(2)]
    s_xwb = [sem(f"s_xwb{i}") for i in range(4)]
    s_scatb = [sem(f"s_scatb{i}") for i in range(4)]
    s_xeb = [sem(f"s_xeb{i}") for i in range(2)]
    s_w1b = [sem(f"s_w1b{i}") for i in range(8)]
    s_w2b = [sem(f"s_w2b{i}") for i in range(8)]
    s_yewb = [sem(f"s_yewb{i}") for i in range(4)]
    s_ygtb = [sem(f"s_ygtb{i}") for i in range(4)]
    s_xcb = [sem(f"s_xcb{i}") for i in range(2)]
    s_owb = [sem(f"s_owb{i}") for i in range(2)]
    s_stat = sem("s_stat"); s_xn = sem("s_xn")
    s_xnfm = sem("s_xnfm"); s_wc = sem("s_wc")
    s_zf = sem("s_zf"); s_zt = sem("s_zt"); s_cl = sem("s_cl")
    s_ind = sem("s_ind"); s_off = sem("s_off"); s_ccw = sem("s_ccw")
    s_cc = sem("s_cc"); s_cntb = sem("s_cntb"); s_slot = sem("s_slot")
    s_sm = sem("s_sm")
    s_exp = sem("s_exp"); s_vn = sem("s_vn"); s_l2g = sem("s_l2g")
    s_w2c = sem("s_w2c")
    s_gsil = sem("s_gsil"); s_out = sem("s_out")
    s_ccb = sem("s_ccb"); s_basb = sem("s_basb"); s_sv = sem("s_sv")
    s_vs = sem("s_vs")
    s_comb = sem("s_comb"); s_sq = sem("s_sq"); s_rst = sem("s_rst")
    s_av2d = sem("s_av2d"); s_stw = sem("s_stw")
    s_qkd = sem("s_qkd"); s_gpio = sem("s_gpio"); s_avj = sem("s_avj")
    sgd = [sem(f"sgd{i}") for i in range(8)]
    sbf = [sem(f"sbf{i}") for i in range(8)]

    NZF = (ZROWS + 127) // 128

    rec = {"qk": [], "vr": [], "rank": [], "attqk": [], "attav": [],
           "stp": [], "l1": {}, "l2": [], "base": []}
    ngrp = [0] * 8
    sgdcnt = [0] * 8

    def grp_begin(te, b):
        if ngrp[b] > 0:
            te.wait_ge(sbf[b], ngrp[b])

    def grp_end(stream, b):
        ngrp[b] += 1
        sgdcnt[b] += 1
        rec[stream].append((b, sgdcnt[b]))

    with nc.Block() as block:
        @block.tensor
        def _(te):
            te.wait_ge(s_wc, 16 * 3)
            for n in range(NTN):
                te.wait_ge(s_xnfm, 16 * KD * (n + 1))
                for wi, wsb in enumerate((wq_sb, wk_sb)):
                    b = wi
                    grp_begin(te, b)
                    for k in range(KD):
                        m = te.matmul(bank[b][0:HD, 0:512],
                                      wsb[:, k * HD:(k + 1) * HD],
                                      xnfm[:, k * S + n * 512: k * S + (n + 1) * 512],
                                      start=(k == 0), stop=(k == KD - 1))
                    m.then_inc(sgd[b], 1)
                    grp_end("qk", b)
                for i in range(4 * n, min(4 * n + 4, NT)):
                    b = 2 + (i % 2)
                    grp_begin(te, b)
                    for k in range(KD):
                        m = te.matmul(bank[b][0:128, 0:72],
                                      xnfm[:, k * S + i * 128: k * S + (i + 1) * 128],
                                      wvr_sb[:, k * 72:(k + 1) * 72],
                                      start=(k == 0), stop=(k == KD - 1))
                    m.then_inc(sgd[b], 1)
                    grp_end("vr", b)
            te.wait_ge(s_cl, 16 * 6)
            for i in range(NT):
                b = 4 + (i % 2)
                b2 = 6 + (i % 2)
                grp_begin(te, b)
                te.wait_ge(s_ind, 1)
                te.wait_ge(s_off, i)
                te.matmul(bank[b][:, 0:16], lincl[:], ind_b[:, i * 16:(i + 1) * 16],
                          start=True, stop=False)
                m = te.matmul(bank[b][:, 0:16], sel127[:], off_sb[:],
                              start=False, stop=True)
                m.then_inc(sgd[b], 1)
                m2 = te.matmul(bank[b2][0:1, 0:16], onescol[:],
                               ind_b[:, i * 16:(i + 1) * 16],
                               start=True, stop=True)
                m2.then_inc(sgd[b2], 1)
                sgdcnt[b2] += 1
                ngrp[b] += 1
                sgdcnt[b] += 1
                rec["rank"].append(((b, sgdcnt[b]), (b2, sgdcnt[b2])))
            AVB = max(1, NTN // 2)
            avbanks = sorted({2 + (ic % AVB) for ic in range(NTN)})

            def emit_qk(jq):
                for ic in range(NTN):
                    b = ic % 2
                    grp_begin(te, b)
                    m = te.matmul(bank[b][:, :],
                                  kT[0:HD, jq * 128:(jq + 1) * 128],
                                  qT[0:HD, ic * 512:(ic + 1) * 512],
                                  start=True, stop=True)
                    m.then_inc(sgd[b], 1)
                    grp_end("attqk", b)

            QKLA = 2
            for jq in range(min(QKLA, NT)):
                emit_qk(jq)
            for j in range(NT):
                if j + QKLA < NT:
                    emit_qk(j + QKLA)
                if j == 0:
                    te.wait_ge(s_cc, 1)
                    te.wait_ge(s_ccb, 16 * 2)
                    te.wait_ge(s_off, NT)
                    m = te.matmul(bank[6][0:1, 0:16], prefm[:], ccall[:],
                                  start=True, stop=True)
                    m.then_inc(sgd[6], 1)
                    sgdcnt[6] += 1
                    rec["base_g"] = sgdcnt[6]
                    for b in avbanks:
                        grp_begin(te, b)
                te.wait_ge(s_vn, j + 1)
                for ic in range(NTN):
                    b = 2 + (ic % AVB)
                    hi = ic >= AVB and NTN > 1
                    m = te.matmul(bank[b][64:128, :] if hi else bank[b][0:HD, :],
                                  vn_sb[:],
                                  e_sb[j % 2][:, ic * 512:(ic + 1) * 512],
                                  start=(j == 0), stop=(j == NT - 1),
                                  tile_position=(0, 64) if hi else None,
                                  skip_group_check=True)
                    if ic == NTN - 1 and j < NT - 1:
                        m.then_inc(s_avj, 1)
                    if j == NT - 1 and (ic == NTN - 1 or
                                        (NTN > 1 and ic == NTN - 1 - AVB and False)):
                        pass
                if j == NT - 1:
                    m.then_inc(sgd[2], 1)
                    for b in avbanks:
                        ngrp[b] += 2 if (NTN > 1) else 1
                    sgdcnt[2] += 1
                    rec["attav_done"] = sgdcnt[2]
                    rec["avevac"] = {b: ngrp[b] for b in avbanks}
            w1_used = 0
            w2_used = 0
            gel_total = 0
            npass = 0
            nw2c = 0
            rec["w2done"] = []
            for e in range(NE):
                te.wait_ge(s_xeb[e % 2], 16 * KD * (e // 2 + 1))
                for m_ in range(FC):
                    w1_used += 1
                    te.wait_ge(s_w1b[(w1_used - 1) % NW1BUF],
                               16 * ((w1_used - 1) // NW1BUF + 1))
                    bs = (m_ % 2) * NNX
                    for ni in range(NNX):
                        grp_begin(te, bs + ni)
                    for k in range(KD):
                        off = 0
                        for ni, nsz in enumerate(NXC):
                            b = bs + ni
                            mm = te.matmul(
                                bank[b][:, 0:nsz],
                                w1b_sb[(w1_used - 1) % NW1BUF][:, k * 128:(k + 1) * 128],
                                xe_sb[e % 2][:, k * C + off: k * C + off + nsz],
                                start=(k == 0), stop=(k == KD - 1))
                            if k == KD - 1:
                                mm.then_inc(sgd[b], 1)
                            off += nsz
                    for ni in range(NNX):
                        sgdcnt[bs + ni] += 1
                        rec["l1"][(e, m_, ni)] = (bs + ni, sgdcnt[bs + ni],
                                                  ngrp[bs + ni] + 1)
                        ngrp[bs + ni] += 1
                first_pass = True
                for half in range(2):
                    for grp in mgroups():
                        for mi in range(len(grp)):
                            grp_begin(te, mi % 8)
                        for k2 in range(FC):
                            if first_pass:
                                for ni in range(NNX):
                                    hb, _, hn = rec["l1"][(e, k2, ni)]
                                    te.wait_ge(sbf[hb], hn)
                            if k2 % W2KG == 0:
                                w2_used += 1
                                te.wait_ge(s_w2b[(w2_used - 1) % NW2BUF],
                                           16 * ((w2_used - 1) // NW2BUF + 1))
                            ws = w2b_sb[(w2_used - 1) % NW2BUF][:, (k2 % W2KG) * 512:
                                                                (k2 % W2KG + 1) * 512]
                            blk_end = (k2 % W2KG == W2KG - 1 or k2 == FC - 1)
                            for mi, m_ in enumerate(grp):
                                b = mi % 8
                                mm = te.matmul(
                                    bank[b][:, :],
                                    h_sb[:, k2 * C + m_ * 128: k2 * C + (m_ + 1) * 128],
                                    ws,
                                    start=(k2 == 0), stop=(k2 == FC - 1))
                                last_mi = (mi == len(grp) - 1)
                                if k2 == FC - 1 and last_mi:
                                    mm.then_inc(s_l2g, 1)
                                elif blk_end and last_mi:
                                    mm.then_inc(s_w2c, 1)
                            if blk_end:
                                if k2 == FC - 1:
                                    rec["w2done"].append(("g", npass + 1))
                                else:
                                    nw2c += 1
                                    rec["w2done"].append(("c", nw2c))
                        npass += 1
                        for mi in range(len(grp)):
                            b = mi % 8
                            ngrp[b] += 1
                            rec["l2"].append((b, npass, ngrp[b]))
                        first_pass = False

        @block.vector
        def _(ve):
            ve.memset(off_sb[:], 0.0)
            ve.wait_ge(s_wc, 16 * 3)
            ve.drain()
            ve.tensor_scalar_add(dumpc[:], iotap[:], float(NE * C))
            ve.memset(zt_sb[:], 0).then_inc(s_zt, 1)
            for i in range(NT):
                ve.wait_ge(s_xb[i % 3], 16 * (i // 3 + 1))
                xt = xt_sb[i % 3]
                ve.reduce_sum(statm[:, i:i + 1], xt[:], axis=AX.X)
                ve.scalar_tensor_tensor(scr_sb[:], xt[:], 1.0, xt[:],
                                        ALU.mult, ALU.mult,
                                        accum_out=statv[:, i:i + 1]) \
                    .then_inc(s_stat, 1)
            ve.drain()
            ve.tensor_scalar_mul(statm[:], statm[:], 1.0 / D)
            ve.tensor_scalar_mul(statv[:], statv[:], 1.0 / D)
            ve.drain()
            ve.tensor_tensor(statt[:], statm[:], statm[:], ALU.mult)
            ve.drain()
            ve.tensor_sub(statv[:], statv[:], statt[:])
            ve.drain()
            ve.tensor_scalar_add(statv[:], statv[:], EPS).then_inc(s_stat, 1)
            ve.wait_ge(s_sq, 1)
            ve.reciprocal(statr[:], statt[:])
            ve.drain()
            ve.tensor_tensor(statnb[:], statm[:], statr[:], ALU.mult)
            ve.drain()
            ve.tensor_scalar_mul(statnb[:], statnb[:], -1.0).then_inc(s_rst, 1)
            qki = 0
            for n in range(NTN):
                for dst in (qT, kT):
                    b, g = rec["qk"][qki]; qki += 1
                    ve.wait_ge(sgd[b], g)
                    ve.tensor_copy(dst[0:HD, n * 512:(n + 1) * 512],
                                   bank[b][0:HD, 0:512]).then_inc(sbf[b], 1)
                for i in range(4 * n, min(4 * n + 4, NT)):
                    b, g = rec["vr"][i]
                    ve.wait_ge(sgd[b], g)
                    ve.tensor_copy(v_sb[:, i * HD:(i + 1) * HD], bank[b][0:128, 0:HD])
                    mlast = ve.tensor_copy(logit_sb[:, i * NE:(i + 1) * NE],
                                           bank[b][0:128, HD:72])
                    mlast.then_inc(sbf[b], 1)
                    if i == NT - 1:
                        ve.memset(colA[:], 0.0).then_inc(s_vs, 1)
            ve.sem_inc(s_qkd, 1)
            ve.memset(ones_b[:], 1.0)
            ve.wait_ge(s_gpio, 1)
            ve.wait_ge(s_sv, 1)
            ve.reduce_sum(colsB[:, 0:NT], expv[:].rearrange("p (t e) -> p t e", e=NE),
                          axis=AX.X)
            ve.reduce_max(colsB[:, NT:2 * NT],
                          expv[:].rearrange("p (t e) -> p t e", e=NE), axis=AX.X)
            ve.drain()
            ve.reciprocal(colsB[:, 2 * NT:3 * NT], colsB[:, 0:NT])
            for i in range(NT):
                ve.tensor_scalar_mul(scrA[:, i * NE:(i + 1) * NE],
                                     ones_b[:, 0:NE], colsB[:, NT + i:NT + i + 1])
            ve.drain()
            ve.tensor_tensor(scrB[:], expv[:], scrA[:], ALU.is_ge)
            ve.drain()
            ve.tensor_tensor(scrC[:], scrB[:], bmi_b[:], ALU.mult)
            ve.drain()
            ve.reduce_max(colsB[:, 3 * NT:4 * NT],
                          scrC[:].rearrange("p (t e) -> p t e", e=NE), axis=AX.X)
            ve.drain()
            ve.tensor_scalar(colsB[:, 3 * NT:4 * NT], colsB[:, 3 * NT:4 * NT],
                             -1.0, 128.0, ALU.mult, ALU.add)
            ve.drain()
            for i in range(NT):
                ve.tensor_scalar_mul(scrA[:, i * NE:(i + 1) * NE],
                                     ones_b[:, 0:NE],
                                     colsB[:, 3 * NT + i:3 * NT + i + 1])
            ve.drain()
            ve.tensor_tensor(ind_b[:].rearrange("p (t k e) -> p t k e", k=2, e=NE)
                             [:, :, 0, :],
                             iotaB[:].rearrange("p (t e) -> p t e", e=NE),
                             scrA[:].rearrange("p (t e) -> p t e", e=NE),
                             ALU.is_equal)
            ve.drain()
            ve.scalar_tensor_tensor(scrB[:],
                                    ind_b[:].rearrange("p (t k e) -> p (t k e)"),
                                    1.0, expv[:], ALU.mult, ALU.mult) \
                if False else None
            ve.scalar_tensor_tensor(
                scrB[:],
                ind_b[:].rearrange("p (t k e) -> p t k e", k=2, e=NE)[:, :, 0, :],
                BIGNEG, expv[:], ALU.mult, ALU.add)
            ve.drain()
            ve.reduce_max(colsB[:, 4 * NT:5 * NT],
                          scrB[:].rearrange("p (t e) -> p t e", e=NE), axis=AX.X)
            ve.drain()
            for i in range(NT):
                ve.tensor_scalar_mul(scrA[:, i * NE:(i + 1) * NE],
                                     ones_b[:, 0:NE],
                                     colsB[:, 4 * NT + i:4 * NT + i + 1])
            ve.drain()
            ve.tensor_tensor(scrC[:], scrB[:], scrA[:], ALU.is_ge)
            ve.drain()
            ve.tensor_tensor(scrC[:], scrC[:], bmi_b[:], ALU.mult)
            ve.drain()
            ve.reduce_max(colsB[:, 5 * NT:6 * NT],
                          scrC[:].rearrange("p (t e) -> p t e", e=NE), axis=AX.X)
            ve.drain()
            ve.tensor_scalar(colsB[:, 5 * NT:6 * NT], colsB[:, 5 * NT:6 * NT],
                             -1.0, 128.0, ALU.mult, ALU.add)
            ve.drain()
            for i in range(NT):
                ve.tensor_scalar_mul(scrA[:, i * NE:(i + 1) * NE],
                                     ones_b[:, 0:NE],
                                     colsB[:, 5 * NT + i:5 * NT + i + 1])
            ve.drain()
            ve.tensor_tensor(ind_b[:].rearrange("p (t k e) -> p t k e", k=2, e=NE)
                             [:, :, 1, :],
                             iotaB[:].rearrange("p (t e) -> p t e", e=NE),
                             scrA[:].rearrange("p (t e) -> p t e", e=NE),
                             ALU.is_equal)
            ve.tensor_tensor(colsB[:, 6 * NT:7 * NT], colsB[:, NT:2 * NT],
                             colsB[:, 2 * NT:3 * NT], ALU.mult)
            ve.tensor_tensor(colsB[:, 7 * NT:8 * NT], colsB[:, 4 * NT:5 * NT],
                             colsB[:, 2 * NT:3 * NT], ALU.mult)
            ve.drain()
            eview = eidx_sb[:].rearrange("p (t k) -> p t k", k=2)
            wview = wco_sb[:].rearrange("p (t k) -> p t k", k=2)
            ve.tensor_copy(eview[:, :, 0], colsB[:, 3 * NT:4 * NT])
            ve.tensor_copy(eview[:, :, 1], colsB[:, 5 * NT:6 * NT])
            ve.tensor_copy(wview[:, :, 0], colsB[:, 6 * NT:7 * NT])
            ve.tensor_copy(wview[:, :, 1], colsB[:, 7 * NT:8 * NT])
            ve.tensor_copy(indp[:], ind_b[:]).then_inc(s_ind, 1)
            for i in range(NT):
                (b, g), (b2, g2) = rec["rank"][i]
                ve.wait_ge(sgd[b], g)
                ve.tensor_sub(rank_sb[:, i * 16:(i + 1) * 16], bank[b][:, 0:16],
                              ind_b[:, i * 16:(i + 1) * 16]).then_inc(sbf[b], 1)
                ve.wait_ge(sgd[b2], g2)
                ve.drain()
                ve.tensor_add(off_sb[0:1, :], off_sb[0:1, :],
                              bank[b2][0:1, 0:16]).then_inc(s_off, 1)
            ve.wait_ge(s_cntb, 16)
            for i in range(NT):
                ve.tensor_copy(cnt_exp[:, i * NE:(i + 1) * NE], cnt_bc[:, 0:NE])
            ve.drain()
            indv = ind_b[:].rearrange("p (t k e) -> p t k e", k=2, e=NE)
            rkv = rank_sb[:].rearrange("p (t k e) -> p t k e", k=2, e=NE)
            cB = colsB
            ve.tensor_tensor(scrA[:].rearrange("p (t e) -> p t e", e=NE),
                             indv[:, :, 0, :], rkv[:, :, 0, :], ALU.mult)
            ve.tensor_tensor(scrB[:].rearrange("p (t e) -> p t e", e=NE),
                             indv[:, :, 1, :], rkv[:, :, 1, :], ALU.mult)
            ve.drain()
            ve.reduce_sum(cB[:, 0:NT], scrA[:].rearrange("p (t e) -> p t e", e=NE),
                          axis=AX.X)
            ve.reduce_sum(cB[:, NT:2 * NT],
                          scrB[:].rearrange("p (t e) -> p t e", e=NE), axis=AX.X)
            ve.drain()
            ve.tensor_tensor(scrA[:].rearrange("p (t e) -> p t e", e=NE),
                             indv[:, :, 1, :],
                             cnt_exp[:].rearrange("p (t e) -> p t e", e=NE), ALU.mult)
            ve.drain()
            ve.reduce_sum(cB[:, 2 * NT:3 * NT],
                          scrA[:].rearrange("p (t e) -> p t e", e=NE), axis=AX.X)
            ve.drain()
            ve.tensor_add(cB[:, 2 * NT:3 * NT], cB[:, 2 * NT:3 * NT],
                          cB[:, NT:2 * NT])
            ve.drain()
            rkst = rk_sb[:].rearrange("p (t k) -> p t k", k=2)
            sctst = sct_sb[:].rearrange("p (t k) -> p t k", k=2)
            rslst = rsl_sb[:].rearrange("p (t k) -> p t k", k=2)
            for kk in range(2):
                slot = cB[:, (0 if kk == 0 else 2) * NT:(1 if kk == 0 else 3) * NT]
                rkl = cB[:, (0 if kk == 0 else 1) * NT:(1 if kk == 0 else 2) * NT]
                ecol = cB[:, (3 if kk == 0 else 5) * NT:(4 if kk == 0 else 6) * NT]
                ovf = cB[:, 8 * NT:9 * NT]
                t1 = cB[:, 9 * NT:10 * NT]
                t2 = cB[:, 10 * NT:11 * NT]
                ve.tensor_scalar(ovf, slot, float(C), None, ALU.is_ge)
                ve.drain()
                ve.scalar_tensor_tensor(t1, ovf, 1.0e9, rkl, ALU.mult, ALU.add)
                ve.drain()
                ve.tensor_copy(rkst[:, :, kk], t1)
                ve.scalar_tensor_tensor(t1, ecol, float(C), slot, ALU.mult, ALU.add)
                ve.tensor_scalar_mul(t2, ones_b[:, 0:NT], dumpc[:])
                ve.drain()
                ve.tensor_sub(t2, t2, t1)
                ve.drain()
                ve.tensor_tensor(t2, t2, ovf, ALU.mult)
                ve.drain()
                ve.tensor_add(t2, t2, t1)
                ve.drain()
                ve.tensor_copy(sctst[:, :, kk], t2)
                ve.tensor_scalar_min(t1, slot, float(C - 1))
                ve.drain()
                ve.scalar_tensor_tensor(t1, ecol, float(C), t1, ALU.mult, ALU.add)
                ve.drain()
                mcp = ve.tensor_copy(rslst[:, :, kk], t1)
            mcp.then_inc(s_slot, 1)
            for j in range(NT + 1):
                if j < NT:
                    if j >= 2:
                        ve.wait_ge(s_exp, j - 1)
                    for ic in range(NTN):
                        b, g = rec["attqk"][j * NTN + ic]
                        ve.wait_ge(sgd[b], g)
                        ve.tensor_scalar(att16[j % 2][:, ic * 512:(ic + 1) * 512],
                                         bank[b][:, :], 0.0, None, ALU.add, ALU.max,
                                         accum_out=cmax[:, ic:ic + 1]) \
                            .then_inc(sbf[b], 1)
                    ve.drain()
                    ve.reduce_max(negmax[:, j % 2:j % 2 + 1], cmax[:, 0:NTN],
                                  axis=AX.X, negate=True).then_inc(s_sm, 1)
                    if j == min(4, NT - 1):
                        ve.wait_ge(sgd[6], rec["base_g"])
                        ve.tensor_copy(base_row[:], bank[6][0:1, 0:16]) \
                            .then_inc(s_basb, 1)
                if j >= 1:
                    jj = j - 1
                    ve.wait_ge(s_exp, jj + 1)
                    ve.reciprocal(vrec[:], den[:])
                    ve.drain()
                    if jj >= 1:
                        ve.wait_ge(s_avj, jj)
                    ve.tensor_scalar_mul(vn_sb[:], v_sb[:, jj * HD:(jj + 1) * HD],
                                         vrec[:]).then_inc(s_vn, 1)
            ve.wait_ge(sgd[2], rec["attav_done"])
            AVB = max(1, NTN // 2)
            for ic in range(NTN):
                b = 2 + (ic % AVB)
                hi = ic >= AVB and NTN > 1
                ve.tensor_copy(avacc[0:HD, ic * 512:(ic + 1) * 512],
                               bank[b][64:128, 0:512] if hi
                               else bank[b][0:HD, 0:512]).then_inc(sbf[b], 1)
            ve.wait_ge(s_basb, 17)
            indpv = indp[:].rearrange("p (t k e) -> p t k e", k=2, e=NE)
            for kk in range(2):
                gsel = p6cols[:, 0:NT]
                for i in range(NT):
                    ve.tensor_copy(p6A[:, i * NE:(i + 1) * NE],
                                   base_bc[:, kk * NE:(kk + 1) * NE])
                ve.drain()
                ve.tensor_tensor(p6B[:].rearrange("p (t e) -> p t e", e=NE),
                                 indpv[:, :, kk, :],
                                 p6A[:].rearrange("p (t e) -> p t e", e=NE), ALU.mult)
                ve.drain()
                ve.reduce_sum(gsel, p6B[:].rearrange("p (t e) -> p t e", e=NE),
                              axis=AX.X)
                ve.drain()
                ve.tensor_add(gsel, gsel,
                              rk_sb[:].rearrange("p (t k) -> p t k", k=2)[:, :, kk])
                ve.drain()
                ve.tensor_scalar(gsel, gsel, float(CAPACITY), None, ALU.is_lt)
                ve.drain()
                ve.tensor_tensor(gsel, gsel,
                                 wco_sb[:].rearrange("p (t k) -> p t k", k=2)[:, :, kk],
                                 ALU.mult)
                ve.drain()
                ve.tensor_copy(coef_b[:].rearrange("p (t k) -> p t k", k=2)[:, :, kk],
                               gsel)
            ve.drain()
            yn = 0
            for b, npass_, gcount in rec["l2"]:
                yn += 1
                if yn > 4:
                    ve.wait_ge(s_yewb[(yn - 4 - 1) % 4], 16 * ((yn - 4 - 1) // 4 + 1))
                ve.wait_ge(s_l2g, npass_)
                ve.tensor_copy(ye_sb[yn % 4][:, :], bank[b][:, :]).then_inc(sbf[b], 1)
            ve.wait_ge(s_stw, 16 * NT)
            for i in range(NT + 1):
                if i >= 1:
                    ii = i - 1
                    ve.wait_ge(s_gsil, ii + 1)
                    ob = o_sb[ii % 2]
                    for c in range(D // HD):
                        ve.tensor_mul(ob[:, c * HD:(c + 1) * HD],
                                      ob[:, c * HD:(c + 1) * HD],
                                      s_sb[:, ii * HD:(ii + 1) * HD])
                if i < NT:
                    ve.wait_ge(s_ygtb[(2 * i) % 4], 16 * ((2 * i) // 4 + 1))
                    ve.wait_ge(s_ygtb[(2 * i + 1) % 4], 16 * ((2 * i + 1) // 4 + 1))
                    if i >= 2:
                        ve.wait_ge(s_gsil, i - 1)
                    ve.scalar_tensor_tensor(gbuf[i % 2][:], yg_sb[(2 * i) % 4][:],
                                            coef_b[:, 2 * i:2 * i + 1],
                                            bias_bc[:], ALU.mult, ALU.add)
                ve.drain()
                if i < NT:
                    ve.scalar_tensor_tensor(gbuf[i % 2][:],
                                            yg_sb[(2 * i + 1) % 4][:],
                                            coef_b[:, 2 * i + 1:2 * i + 2],
                                            gbuf[i % 2][:], ALU.mult, ALU.add) \
                        .then_inc(s_comb, 1)
                if i >= 1:
                    ii = i - 1
                    ve.wait_ge(s_xcb[ii % 2], 16 * (ii // 2 + 1))
                    ve.tensor_add(o_sb[ii % 2][:], o_sb[ii % 2][:],
                                  xc_sb[ii % 2][:]).then_inc(s_out, 1)

        @block.scalar
        def _(sc):
            sc.wait_ge(s_zt, 1)
            for z in range(NZF):
                r0 = z * 128
                r1 = min(r0 + 128, ZROWS)
                sc.dma_start(out=xe_d[r0:r1, :], in_=zt_sb[0:r1 - r0, :]) \
                    .then_inc(s_zf, 16)
            sc.wait_ge(s_stat, NT + 1)
            sc.activation(statt[:], statv[:], AF.Sqrt,
                          bias=0.0, scale=1.0).then_inc(s_sq, 1)
            sc.wait_ge(s_rst, 1)
            cntA = [len([t for t in range(NT) if t % 3 == p]) for p in range(3)]
            for i in range(NT):
                sc.wait_ge(s_xb[i % 3], 16 * (cntA[i % 3] + i // 3 + 1))
                if i >= 2:
                    sc.wait_ge(s_xnwb[i % 2], 16 * (i // 2))
                sc.activation(xnt_sb[i % 2][:], xt_sb[i % 3][:], AF.Identity,
                              bias=statnb[:, i:i + 1],
                              scale=statr[:, i:i + 1]).then_inc(s_xn, 1)
            sc.wait_ge(s_vs, 1)
            sc.activation(expv[:], logit_sb[:], AF.Exp).then_inc(s_sv, 1)
            for j in range(NT):
                sc.wait_ge(s_sm, j + 1)
                if j >= 1:
                    sc.wait_ge(s_vn, j)
                if j >= 2:
                    sc.wait_ge(s_avj, j - 1)
                sc.activation(e_sb[j % 2][:], att16[j % 2][:], AF.Exp,
                              bias=negmax[:, j % 2:j % 2 + 1], scale=1.0,
                              accum_out=den[:]).then_inc(s_exp, 1)
            for b in sorted(rec["avevac"]):
                sc.wait_ge(sbf[b], rec["avevac"][b])
            sc.dma_start(out=av_d, in_=avacc[:]).then_inc(s_av2d, 16)
            for e in range(NE):
                for m_ in range(FC):
                    off = 0
                    for ni, nsz in enumerate(NXC):
                        b, g, _ = rec["l1"][(e, m_, ni)]
                        sc.wait_ge(sgd[b], g)
                        sc.activation(h_sb[:, m_ * C + off: m_ * C + off + nsz],
                                      bank[b][:, 0:nsz], AF.Gelu_apprx_tanh) \
                            .then_inc(sbf[b], 1)
                        off += nsz
            for i in range(NT):
                sc.wait_ge(s_comb, i + 1)
                if i >= 2:
                    sc.wait_ge(s_owb[i % 2], 16 * ((i - 2) // 2 + 1))
                sc.activation(o_sb[i % 2][:], gbuf[i % 2][:], AF.Silu) \
                    .then_inc(s_gsil, 1)

        @block.gpsimd
        def _(gp):
            gp.iota(iota8[:], [[1, NE]], base=0, channel_multiplier=0,
                    allow_small_or_imprecise_dtypes=True)
            gp.iota(iotap[:], [[0, 1]], base=0, channel_multiplier=1,
                    allow_small_or_imprecise_dtypes=True)
            gp.iota(bmi[:], [[-1, NE]], base=128, channel_multiplier=0,
                    allow_small_or_imprecise_dtypes=True)
            gp.dma_start(out=wq_sb[:].rearrange("p (k n) -> p k n", k=KD),
                         in_=wq_d).then_inc(s_wc, 16)
            gp.dma_start(out=wk_sb[:].rearrange("p (k n) -> p k n", k=KD),
                         in_=wk_d).then_inc(s_wc, 16)
            gp.dma_start(out=wvr_sb[:].rearrange("p (k n) -> p k n", k=KD),
                         in_=wvr_d).then_inc(s_wc, 16)
            gp.wait_ge(s_xn, NT)
            gp.iota(iotaB[:], [[0, NT], [1, NE]], base=0, channel_multiplier=0,
                    allow_small_or_imprecise_dtypes=True)
            gp.iota(bmi_b[:], [[0, NT], [-1, NE]], base=128, channel_multiplier=0,
                    allow_small_or_imprecise_dtypes=True) if False else None
            gp.iota(bmi_b[:], [[0, NT], [-1, NE]], base=128,
                    channel_multiplier=0,
                    allow_small_or_imprecise_dtypes=True).then_inc(s_gpio, 1)
            gp.wait_ge(s_off, NT)
            gp.dma_start(out=ccin_d, in_=off_sb[0:1, :]).then_inc(s_ccw, 16)
            gp.wait_ge(s_ccw, 16)
            gp.collective_compute(
                "AllGather", ALU.bypass, replica_groups=[list(range(n_cores))],
                ins=[ccin_d.opt()], outs=[ccout_d.opt()],
            ).then_inc(s_cc, 1)
            gp.wait_ge(s_cc, 1)
            gp.dma_start(out=ccall[:], in_=ccout_d).then_inc(s_ccb, 16)
            gp.dma_start(out=prefm[:], in_=prefm_d).then_inc(s_ccb, 16)
            for i in range(NT):
                gp.wait_ge(s_xwb[i % 4], 16 * (i // 4 + 1))
                gp.wait_ge(s_slot, 1)
                if i == 0:
                    gp.wait_ge(s_zf, 16 * NZF)
                for kk in range(2):
                    gp.indirect_dma_start(
                        xe_d,
                        IndirectOffsetOnAxis(ap=sct_sb[:, 2 * i + kk:2 * i + kk + 1],
                                             axis=0),
                        xw_sb[i % 4][:], None,
                    ).then_inc(s_scatb[i % 4], 16)
            gp.wait_ge(s_basb, 1)
            gp.dma_start(out=ccbase_d, in_=base_row[:]).then_inc(s_ccb, 16)
            w1i = 0
            w2i = 0
            for e in range(NE):
                for m_ in range(FC):
                    w1i += 1
                    if w1i > 2:
                        jb = w1i - 2
                        eb, mb = (jb - 1) // FC, (jb - 1) % FC
                        wb, wg, _ = rec["l1"][(eb, mb, NNX - 1)]
                        gp.wait_ge(sgd[wb], wg)
                    gp.dma_start(out=w1b_sb[(w1i - 1) % NW1BUF][:].rearrange(
                        "p (k f) -> p k f", k=KD),
                        in_=w1_d[e, m_]).then_inc(s_w1b[(w1i - 1) % NW1BUF], 16)
                for half in range(2):
                    for grp in mgroups():
                        for kb in range(NW2B):
                            w2i += 1
                            if w2i > NW2BUF:
                                kind, val = rec["w2done"][w2i - 1 - NW2BUF]
                                gp.wait_ge(s_w2c if kind == "c" else s_l2g, val)
                            gp.dma_start(
                                out=w2b_sb[(w2i - 1) % NW2BUF][:].rearrange(
                                    "p (k j) -> p k j", k=W2KG),
                                in_=w2_d[e, kb * W2KG:(kb + 1) * W2KG, half]
                                .rearrange("k p j -> p k j"),
                            ).then_inc(s_w2b[(w2i - 1) % NW2BUF], 16)
            nyes = NE * 2 * MC
            for p in range(4):
                tot = nyes // 4 + (1 if p < nyes % 4 else 0)
                gp.wait_ge(s_yewb[p], 16 * tot)
            for i in range(NT):
                for kk in range(2):
                    n = 2 * i + kk
                    if n >= 4:
                        gp.wait_ge(s_comb, (n - 4) // 2 + 1)
                    gp.indirect_dma_start(
                        yg_sb[n % 4][:], None,
                        ye_d,
                        IndirectOffsetOnAxis(ap=rsl_sb[:, n:n + 1], axis=0),
                    ).then_inc(s_ygtb[n % 4], 16)

        @block.sync
        def _(sy):
            sy.dma_start(out=bias_bc[:],
                         in_=bias_d.broadcast_to([128, D])).then_inc(s_cl, 16)
            sy.dma_start(out=identf[:], in_=identf_d).then_inc(s_cl, 16)
            sy.dma_start(out=identb[:], in_=identb_d).then_inc(s_cl, 16)
            sy.dma_start(out=lincl[:], in_=lincl_d).then_inc(s_cl, 16)
            sy.dma_start(out=sel127[:], in_=sel127_d).then_inc(s_cl, 16)
            sy.dma_start(out=onescol[:], in_=onescol_d).then_inc(s_cl, 16)
            for i in range(NT):
                if i >= 3:
                    sy.wait_ge(s_stat, i - 2)
                sy.dma_start(out=xt_sb[i % 3][:],
                             in_=x_d[i * 128:(i + 1) * 128, :]).then_inc(s_xb[i % 3], 16)
            def emit_xnfm(n):
                sy.wait_ge(s_xnwb[0], 16 * (2 * n + 2))
                sy.wait_ge(s_xnwb[1], 16 * (2 * n + 2))
                for c in range(KD):
                    sy.dma_start(out=xnfm[:, c * S + n * 512: c * S + (n + 1) * 512],
                                 in_=xn_d[n * 512:(n + 1) * 512,
                                          c * 128:(c + 1) * 128],
                                 transpose=True).then_inc(s_xnfm, 16)

            for i in range(NT):
                if i < 3:
                    iA = max(t for t in range(NT) if t % 3 == i % 3)
                    sy.wait_ge(s_stat, iA + 1)
                else:
                    sy.wait_ge(s_xn, i - 2)
                sy.dma_start(out=xt_sb[i % 3][:],
                             in_=x_d[i * 128:(i + 1) * 128, :]).then_inc(s_xb[i % 3], 16)
                if i >= 1:
                    sy.wait_ge(s_xn, i)
                    sy.dma_start(out=xn_d[(i - 1) * 128:i * 128, :],
                                 in_=xnt_sb[(i - 1) % 2][:]) \
                        .then_inc(s_xnwb[(i - 1) % 2], 16)
                    if i % 4 == 0:
                        emit_xnfm(i // 4 - 1)
            sy.wait_ge(s_xn, NT)
            sy.dma_start(out=xn_d[(NT - 1) * 128:NT * 128, :],
                         in_=xnt_sb[(NT - 1) % 2][:]) \
                .then_inc(s_xnwb[(NT - 1) % 2], 16)
            for n in range((NT - 1) // 4, NT // 4):
                emit_xnfm(n)
            if debug_taps:
                sy.wait_ge(s_xnfm, 16 * KD * NTN)
                sy.dma_start(out=dbg["dbg_xnfm"], in_=xnfm[:]).then_inc(s_cl, 16)
                sy.wait_ge(s_qkd, 1)
                sy.dma_start(out=dbg["dbg_qT"], in_=qT[:]).then_inc(s_cl, 16)
                sy.dma_start(out=dbg["dbg_kT"], in_=kT[:]).then_inc(s_cl, 16)
            sy.wait_ge(s_ccw, 16)
            sy.dma_start(out=cnt_bc[:],
                         in_=ccin_d.broadcast_to([128, 16])).then_inc(s_cntb, 16)
            for i in range(NT):
                if i >= 4:
                    sy.wait_ge(s_scatb[i % 4], 32 * (i // 4))
                sy.dma_start(out=xw_sb[i % 4][:],
                             in_=xn_d[i * 128:(i + 1) * 128, :]) \
                    .then_inc(s_xwb[i % 4], 16)
            sy.wait_ge(s_ccb, 16 * 3)
            sy.dma_start(out=base_bc[:],
                         in_=ccbase_d.broadcast_to([128, 16])).then_inc(s_basb, 16)
            def emit_ye_stores(e, yewn):
                for half in range(2):
                    for grp in mgroups():
                        for m_ in grp:
                            yewn += 1
                            yb, ynp, ygc = rec["l2"][yewn - 1]
                            sy.wait_ge(sbf[yb], ygc)
                            sy.dma_start(
                                out=ye_d[e * C + m_ * 128: e * C + (m_ + 1) * 128,
                                         half * 512:(half + 1) * 512],
                                in_=ye_sb[yewn % 4][:, :]) \
                                .then_inc(s_yewb[(yewn - 1) % 4], 16)
                return yewn

            yewn = 0
            for e in range(NE):
                if e == 0:
                    for p in range(4):
                        tot = 2 * (NT // 4 + (1 if p < NT % 4 else 0))
                        sy.wait_ge(s_scatb[p], 16 * tot)
                if e == 1:
                    qb, qg = rec["attqk"][-1]
                    sy.wait_ge(sgd[qb], qg)
                if e == 2:
                    sy.wait_ge(s_av2d, 16)
                    for i in range(NT):
                        sy.dma_start(out=s_sb[:, i * HD:(i + 1) * HD],
                                     in_=av_d[0:HD, i * 128:(i + 1) * 128],
                                     transpose=True).then_inc(s_stw, 16)
                if e >= 2:
                    wb, wg, _ = rec["l1"][(e - 2, FC - 1, NNX - 1)]
                    sy.wait_ge(sgd[wb], wg)
                for k in range(KD):
                    sy.dma_start(out=xe_sb[e % 2][:, k * C:(k + 1) * C],
                                 in_=xe_d[e * C:(e + 1) * C, k * 128:(k + 1) * 128],
                                 transpose=True).then_inc(s_xeb[e % 2], 16)
                if e >= 1:
                    yewn = emit_ye_stores(e - 1, yewn)
            yewn = emit_ye_stores(NE - 1, yewn)
            for i in range(NT):
                if i >= 2:
                    sy.wait_ge(s_out, i - 1)
                sy.dma_start(out=xc_sb[i % 2][:],
                             in_=x_d[i * 128:(i + 1) * 128, :]) \
                    .then_inc(s_xcb[i % 2], 16)
                if i >= 1:
                    sy.wait_ge(s_out, i)
                    sy.dma_start(out=out_d[(i - 1) * 128:i * 128, :],
                                 in_=o_sb[(i - 1) % 2][:]) \
                        .then_inc(s_owb[(i - 1) % 2], 16)
            sy.wait_ge(s_out, NT)
            sy.dma_start(out=out_d[(NT - 1) * 128:NT * 128, :],
                         in_=o_sb[(NT - 1) % 2][:]).then_inc(s_owb[(NT - 1) % 2], 16)
            sy.wait_ge(s_owb[0], 16 * ((NT + 1) // 2))
            sy.wait_ge(s_owb[1], 16 * (NT // 2))
            if debug_taps:
                ndbg = 0
                sy.dma_start(out=dbg["dbg_xnd"], in_=xn_d).then_inc(s_cl, 16)
                sy.dma_start(out=dbg["dbg_av"], in_=avacc[:]).then_inc(s_cl, 16)
                ndbg += 1
                for name, src_sb in (("dbg_s", s_sb), ("dbg_logit", logit_sb),
                                     ("dbg_wco", wco_sb), ("dbg_eidx", eidx_sb),
                                     ("dbg_rk", rk_sb), ("dbg_rsl", rsl_sb),
                                     ("dbg_sct", sct_sb), ("dbg_base", base_bc),
                                     ("dbg_rank", rank_sb)):
                    sy.dma_start(out=dbg[name], in_=src_sb[:]).then_inc(s_cl, 16)
                    ndbg += 1
                sy.wait_ge(s_cl, 16 * 6 + 16 * (ndbg + 4))

    es.close()
    return nc


def _host_layouts(inputs):
    x = np.ascontiguousarray(inputs["L0_attn_x"])
    wq = np.ascontiguousarray(inputs["L0_attn_Wq"])
    wk = np.ascontiguousarray(inputs["L0_attn_Wk"])
    wv = np.ascontiguousarray(inputs["L0_attn_Wv"])
    wr = np.ascontiguousarray(inputs["L0_ffn_router_w"])
    w1 = np.ascontiguousarray(inputs["L0_ffn_w1"])
    w2 = np.ascontiguousarray(inputs["L0_ffn_w2"])
    bias = np.ascontiguousarray(inputs["L0_ffn_bias"]).reshape(1, D)
    S = x.shape[1]
    F = w1.shape[2]
    FC = F // 128

    def chunk_pk(w, ncols):
        return np.ascontiguousarray(w.reshape(KD, 128, ncols).transpose(1, 0, 2))

    wvr = np.concatenate([wv, wr], axis=1)
    w1h = np.ascontiguousarray(
        w1.reshape(NE, KD, 128, FC, 128).transpose(0, 3, 2, 1, 4))
    w2h = np.ascontiguousarray(
        w2.reshape(NE, FC, 128, 2, 512).transpose(0, 1, 3, 2, 4))
    identf = np.eye(128, dtype=np.float32)
    import ml_dtypes
    identb = np.eye(128).astype(ml_dtypes.bfloat16)
    lincl = np.triu(np.ones((128, 128), np.float32))
    sel127 = np.zeros((128, 128), np.float32)
    sel127[0, :] = 1.0
    onescol = np.ones((128, 1), np.float32)
    common = dict(
        wqh=chunk_pk(wq, HD), wkh=chunk_pk(wk, HD), wvrh=chunk_pk(wvr, 72),
        w1h=w1h, w2h=w2h, bias1=bias,
        identf=identf, identb=identb, lincl=lincl, sel127=sel127, onescol=onescol,
    )
    in_maps = []
    for c in range(x.shape[0]):
        prefm = np.zeros((NE, 1), np.float32)
        prefm[:c, 0] = 1.0
        in_maps.append(dict(common, x=x[c], prefm=prefm))
    return in_maps, S, F


def kernel(**inputs):
    in_maps, S, F = _host_layouts(inputs)
    nc = build_program(S=S, F=F, C=1152, n_cores=8)
    res = run_bass_kernel_spmd(nc, in_maps, core_ids=list(range(8)))
    out = np.stack([r["out"] for r in res.results], axis=0)
    return out.astype(np.float32)
